# version 14
# baseline (speedup 1.0000x reference)
import numpy as np
import ml_dtypes
from contextlib import ExitStack

import concourse.bass as bass
import concourse.mybir as mybir
from concourse.bass_utils import run_bass_kernel_spmd
from concourse.alu_op_type import AluOpType as OP

F32 = mybir.dt.float32
BF16 = mybir.dt.bfloat16
AF = mybir.ActivationFunctionType
AX = mybir.AxisListType

T = 2048
D = 1024
C = 8
NT = 16
LTF = 256
NLTF = T // LTF

BF = ml_dtypes.bfloat16
NCORES = 8



def _prep_weights(inp):
    w = {}
    cc = np.ascontiguousarray
    w["gate_w"] = cc(inp["L0_ffn_gate_w"].reshape(8, 128, 8).transpose(1, 0, 2))
    w["gate_b"] = cc(inp["L0_ffn_gate_b"].reshape(8, 1))
    w["sw1"] = cc(inp["L0_ffn_sw1"][0])
    w["sb1"] = cc(inp["L0_ffn_sb1"][0].reshape(16, 128).T)
    w["sw2"] = cc(inp["L0_ffn_sw2"][0])
    w["sb2"] = cc(inp["L0_ffn_sb2"][0].reshape(8, 128).T)
    w["rw1"] = cc(inp["L0_ffn_rw1"])
    w["rb1"] = cc(inp["L0_ffn_rb1"].reshape(8, 16, 128).transpose(2, 0, 1))
    w["rw2"] = cc(inp["L0_ffn_rw2"])
    w["rb2"] = cc(inp["L0_ffn_rb2"])
    for nm in ("Wq", "Wk", "Wv", "Wo"):
        w["a0" + nm] = cc(inp["L0_attn_" + nm])
    for nm in ("bq", "bk", "bv", "bo"):
        w["a0" + nm] = cc(inp["L0_attn_" + nm].reshape(8, 128).T)
    for nm in ("Wq", "Wk", "Wv"):
        w["a1" + nm + "T"] = cc(inp["L1_attn_" + nm].T)
    w["a1WoT"] = cc(inp["L1_attn_Wo"].T)
    w["centT"] = cc(inp["L1_ffn_centroids"].T.reshape(8, 128, 8).transpose(1, 0, 2))
    w["gamma"] = cc(inp["L1_ffn_gamma"].reshape(8, 8, 128).transpose(2, 0, 1))
    w["beta"] = cc(inp["L1_ffn_beta"].reshape(8, 8, 128).transpose(2, 0, 1))
    w["W1T"] = cc(inp["L1_ffn_W1"].transpose(0, 2, 1))
    w["b1"] = cc(inp["L1_ffn_b1"].reshape(8, 32, 128).transpose(2, 0, 1))
    w["W2T"] = cc(inp["L1_ffn_W2"].transpose(0, 2, 1))
    w["b2"] = cc(inp["L1_ffn_b2"])
    w["ident_f"] = np.eye(128, dtype=np.float32)
    w["ident_b"] = np.eye(128, dtype=BF)
    w["ones_f"] = np.ones((128, 128), dtype=np.float32)
    w["ones_b"] = np.ones((128, 128), dtype=BF)
    sel8 = np.zeros((8, 8 * 128), dtype=BF)
    for e in range(8):
        sel8[e, e * 128:(e + 1) * 128] = 1
    w["sel8"] = sel8
    w["causal"] = np.where(np.tril(np.ones((128, 128), dtype=bool)),
                           0.0, -1e30).astype(np.float32)
    w["maskb"] = (np.arange(896)[None, :]
                  >= (np.arange(128)[:, None] + 384)).astype(BF)
    sel16c = np.zeros((16, 256), dtype=BF)
    sel16r = np.zeros((16, 256), dtype=BF)
    for b in range(16):
        sel16c[:, b * 16 + b] = 1
        sel16r[b, b * 16:(b + 1) * 16] = 1
    w["sel16c"] = sel16c
    w["sel16r"] = sel16r
    w["epsb"] = np.full((1, 1), 1e-5, dtype=np.float32)
    return w


_WSPECS = {
    "gate_w": ([128, 8, 8], np.float32), "gate_b": ([8, 1], np.float32),
    "sw1": ([1024, 2048], np.float32), "sb1": ([128, 16], np.float32),
    "sw2": ([1024, 1024], np.float32), "sb2": ([128, 8], np.float32),
    "rw1": ([8, 1024, 2048], np.float32), "rb1": ([128, 8, 16], np.float32),
    "rw2": ([8, 1024, 1024], np.float32), "rb2": ([8, 1024], np.float32),
    "a0Wq": ([1024, 1024], np.float32), "a0Wk": ([1024, 1024], np.float32),
    "a0Wv": ([1024, 1024], np.float32), "a0Wo": ([1024, 1024], np.float32),
    "a0bq": ([128, 8], np.float32), "a0bk": ([128, 8], np.float32),
    "a0bv": ([128, 8], np.float32), "a0bo": ([128, 8], np.float32),
    "a1WqT": ([64, 64], np.float32), "a1WkT": ([64, 64], np.float32),
    "a1WvT": ([64, 64], np.float32), "a1WoT": ([1024, 1024], np.float32),
    "centT": ([128, 8, 8], np.float32), "gamma": ([128, 8, 8], np.float32),
    "beta": ([128, 8, 8], np.float32),
    "W1T": ([8, 1024, 4096], np.float32), "b1": ([128, 8, 32], np.float32),
    "W2T": ([8, 4096, 1024], np.float32), "b2": ([8, 1024], np.float32),
    "ident_f": ([128, 128], np.float32), "ident_b": ([128, 128], BF),
    "ones_f": ([128, 128], np.float32), "ones_b": ([128, 128], BF),
    "sel8": ([8, 8 * 128], BF), "causal": ([128, 128], np.float32),
    "maskb": ([128, 896], BF),
    "sel16c": ([16, 256], BF), "sel16r": ([16, 256], BF),
    "epsb": ([1, 1], np.float32),
}

_SB_BF16 = {"gate_w", "rb2", "b2", "centT", "a1WqT", "a1WkT", "a1WvT"}

_SUB_WNAMES = {
    "l0ffn": ["gate_w", "gate_b", "sw1", "sb1", "sw2", "sb2", "rw1", "rb1",
              "rw2", "rb2", "ident_f", "ident_b", "ones_f", "ones_b", "sel8",
              "epsb"],
    "l0attn": ["a0Wq", "a0Wk", "a0Wv", "a0Wo", "a0bq", "a0bk", "a0bv", "a0bo",
               "ident_f", "ident_b", "ones_f", "ones_b", "sel16c", "sel16r",
               "epsb"],
    "l1ffn": ["centT", "gamma", "beta", "W1T", "b1", "W2T", "b2",
              "ident_f", "ident_b", "ones_f", "ones_b", "sel8", "epsb"],
    "l1attn": ["a1WqT", "a1WkT", "a1WvT", "a1WoT",
               "ident_f", "ident_b", "ones_f", "ones_b", "maskb", "epsb"],
}

_NP2BIR = {np.float32: F32, BF: BF16}

import os
DEBUG_STOP = os.environ.get("KDEBUG", "")



class _Eng:
    def __init__(self, ctx, obj, name, nsem=4):
        self.ctx = ctx
        self.obj = obj
        self.name = name
        self.sems = [ctx.es.enter_context(ctx.nc.semaphore(f"s_{name}{i}"))
                     for i in range(nsem)]
        self.cur = 0
        self.count = 0
        self.self_waited = 0
        self.last = None
        self.last_inced = True
        self.prev_tok = None
        self.waited = {}

    def rotate(self):
        if self.count > 20000:
            self.prev_tok = self.tok()
            self.cur = (self.cur + 1) % len(self.sems)
            self.count = 0
            self.self_waited = 0
            self.last = None
            self.last_inced = True

    def tok(self):
        if self.last is not None and not self.last_inced:
            self.last.then_inc(self.sems[self.cur], 1)
            self.count += 1
            self.last_inced = True
            if self.count - self.self_waited >= 128:
                self.obj.wait_ge(self.sems[self.cur], self.count - 64)
                self.self_waited = self.count - 64
                k = id(self.sems[self.cur])
                if self.waited.get(k, 0) < self.count - 64:
                    self.waited[k] = self.count - 64
        if self.count == 0:
            return self.prev_tok
        return (self.sems[self.cur], self.count)

    def wait(self, tok):
        if tok is None:
            return
        sem, val = tok
        if val <= 0:
            return
        k = id(sem)
        if self.waited.get(k, 0) >= val:
            return
        self.obj.wait_ge(sem, val)
        self.waited[k] = val

    def issue(self, inst, inc):
        self.last = inst
        self.last_inced = False
        if inc:
            return self.tok()
        return None


class Slot:
    def __init__(self, ctx, name):
        self.sem = ctx.es.enter_context(ctx.nc.semaphore(name))
        self.count = 0
        self.self_waited = 0
        self.free = None
        ctx.all_slots.append(self)


class Ctx:
    def __init__(self, nc, es):
        self.nc = nc
        self.es = es
        self.pe = _Eng(self, nc.tensor, "pe", 6)
        self.dve = _Eng(self, nc.vector, "dve", 4)
        self.act = _Eng(self, nc.scalar, "act", 4)
        self.pool = _Eng(self, nc.gpsimd, "pool", 4)
        self.sp = _Eng(self, nc.sync, "sp", 4)
        self.engs = [self.pe, self.dve, self.act, self.pool, self.sp]
        self.sb = {}
        self.w = {}
        self.all_slots = []
        self.slots = {}
        self._uid = 0

    def finalize(self):
        for e in self.engs:
            t = e.tok()
            self.sp.wait(t)
        for sl in self.all_slots:
            if sl.count > 0:
                self.sp.wait((sl.sem, sl.count))
        for e in self.engs:
            for i, sem in enumerate(e.sems):
                pass

    def nct(self, name, shape, dt):
        self._uid += 1
        return self.nc.sbuf_tensor(f"{name}_{self._uid}", shape, dt)

    def npt(self, name, shape, dt):
        self._uid += 1
        return self.nc.psum_tensor(f"{name}_{self._uid}", shape, dt)

    def slot(self, name):
        if name not in self.slots:
            self.slots[name] = Slot(self, name)
        return self.slots[name]

    def phase(self):
        toks = [e.tok() for e in self.engs]
        for e in self.engs:
            for t in toks:
                if t is not None and (t[0] is not e.sems[e.cur] or True):
                    e.wait(t)
        for e in self.engs:
            e.rotate()

    def _pre(self, eng, waits):
        if waits:
            for t in waits:
                eng.wait(t)

    def dma(self, eng, out, in_, waits=None, slot=None):
        self._pre(eng, waits)
        if slot is not None:
            if slot.free is not None:
                eng.wait(slot.free)
        inst = eng.obj.dma_start(out, in_)
        if slot is not None:
            inst.then_inc(slot.sem, 16)
            slot.count += 16
            eng.last, eng.last_inced = inst, True
            if slot.count - slot.self_waited >= 96:
                eng.obj.wait_ge(slot.sem, slot.count - 32)
                slot.self_waited = slot.count - 32
            return (slot.sem, slot.count)
        inst.then_inc(eng.sems[eng.cur], 16)
        eng.count += 16
        eng.last, eng.last_inced = inst, True
        if eng.count - eng.self_waited >= 128:
            eng.obj.wait_ge(eng.sems[eng.cur], eng.count - 64)
            eng.self_waited = eng.count - 64
            k = id(eng.sems[eng.cur])
            if eng.waited.get(k, 0) < eng.count - 64:
                eng.waited[k] = eng.count - 64
        return (eng.sems[eng.cur], eng.count)

    def mm(self, out, lhsT, rhs, start, stop, waits=None, inc=False):
        self._pre(self.pe, waits)
        return self.pe.issue(
            self.nc.tensor.matmul(out, lhsT, rhs, start=start, stop=stop), inc)

    def xp(self, out, in_, ident, waits=None, inc=False):
        self._pre(self.pe, waits)
        return self.pe.issue(self.nc.tensor.transpose(out, in_, ident), inc)

    def tt(self, out, a, b, op, waits=None, inc=True, eng=None):
        e = eng or self.dve
        self._pre(e, waits)
        return e.issue(e.obj.tensor_tensor(out, a, b, op), inc)

    def ts(self, out, a, s1, s2, op0, op1, waits=None, inc=True, eng=None,
           accum_out=None):
        e = eng or self.dve
        self._pre(e, waits)
        return e.issue(e.obj.tensor_scalar(out, a, s1, s2, op0, op1,
                                           accum_out=accum_out), inc)

    def stt(self, out, in0, scalar, in1, op0, op1, waits=None, inc=True):
        self._pre(self.dve, waits)
        return self.dve.issue(
            self.nc.vector.scalar_tensor_tensor(out, in0, scalar, in1, op0, op1),
            inc)

    def cp(self, out, in_, waits=None, inc=True, eng=None):
        e = eng or self.dve
        self._pre(e, waits)
        if e is self.act:
            return e.issue(self.nc.scalar.copy(out, in_), inc)
        return e.issue(e.obj.tensor_copy(out, in_), inc)

    def red(self, out, in_, op, negate=False, waits=None, inc=True):
        self._pre(self.dve, waits)
        return self.dve.issue(
            self.nc.vector.tensor_reduce(out, in_, AX.X, op, negate=negate), inc)

    def recip(self, out, in_, waits=None, inc=True):
        self._pre(self.dve, waits)
        return self.dve.issue(self.nc.vector.reciprocal(out, in_), inc)

    def a(self, out, in_, func, bias=None, scale=None, accum_out=None,
          waits=None, inc=True):
        self._pre(self.act, waits)
        kw = {}
        if bias is not None:
            kw["bias"] = bias
        if scale is not None:
            kw["scale"] = scale
        if accum_out is not None:
            kw["accum_out"] = accum_out
        return self.act.issue(self.nc.scalar.activation(out, in_, func, **kw), inc)



_STREAMED = {"sw1", "sw2", "rw1", "rw2", "W1T", "W2T",
             "a0Wq", "a0Wk", "a0Wv", "a0Wo", "a1WoT"}


def _load_consts(ctx, names):
    toks = []
    for nm in names:
        if nm in _STREAMED:
            continue
        shape, npdt = _WSPECS[nm]
        sb_dt = BF16 if nm in _SB_BF16 else _NP2BIR[npdt]
        t = ctx.es.enter_context(ctx.nc.sbuf_tensor("c_" + nm, list(shape), sb_dt))
        ctx.sb[nm] = t
        q = ctx.pool if sb_dt != _NP2BIR[npdt] else ctx.sp
        sl = tuple(slice(None) for _ in shape)
        toks.append(ctx.dma(q, t[sl], ctx.w[nm]))
    return toks


def emit_load_x(ctx, x_ap, psx):
    x, idf = ctx.sb["x"], ctx.sb["ident_f"]
    stage_cm = ctx.nct("stage_l", [128, 2, 1024], F32)
    stage = stage_cm.__enter__()
    stage_free = [None, None]
    ev = [None, None]
    for i in range(NT):
        s = i % 2
        ld = ctx.dma(ctx.sp, stage[:, s, :], x_ap[i * 128:(i + 1) * 128, :],
                     waits=[stage_free[s]])
        for c in range(C):
            sl = c % 2
            ctx.xp(psx[:, sl, 0:128], stage[:, s, c * 128:(c + 1) * 128],
                   idf[:, :], waits=[ld, ev[sl]], inc=True)
            pt = ctx.pe.tok()
            ev[sl] = ctx.cp(x[:, c, i * 128:(i + 1) * 128], psx[:, sl, 0:128],
                            waits=[pt])
        stage_free[s] = ctx.pe.tok()
    stage_cm.__exit__(None, None, None)


def emit_store_x(ctx, xo_ap, psx):
    x, idf = ctx.sb["x"], ctx.sb["ident_f"]
    stage_cm = ctx.nct("stage_s", [128, 2, 1024], F32)
    stage = stage_cm.__enter__()
    dma_done = [None, None]
    prev_ev = [None, None]
    toks = []
    for i in range(NT):
        s = i % 2
        ev = None
        for c in range(C):
            sl = c % 2
            ctx.xp(psx[:, sl, 0:128], x[:, c, i * 128:(i + 1) * 128], idf[:, :],
                   waits=[prev_ev[sl], dma_done[s]], inc=True)
            pt = ctx.pe.tok()
            ev = ctx.cp(stage[:, s, c * 128:(c + 1) * 128], psx[:, sl, 0:128],
                        waits=[pt])
            prev_ev[sl] = ev
        dma_done[s] = ctx.dma(ctx.sp, xo_ap[i * 128:(i + 1) * 128, :],
                              stage[:, s, :], waits=[ev])
        toks.append(dma_done[s])
    stage_cm.__exit__(None, None, None)
    return toks


def emit_ln(ctx):
    nc = ctx.nc
    x, xn = ctx.sb["x"], ctx.sb["xn"]
    onf, onb = ctx.sb["ones_f"], ctx.sb["ones_b"]
    with (
        ctx.nct("ln_xq", [128, C, T], BF16) as xq,
        ctx.nct("ln_rows", [1, 3, T], F32) as rows,
        ctx.nct("ln_tmp", [128, 2, 512], F32) as tmp,
        ctx.npt("ln_pr", [128, 2, 512], F32) as pr,
        ctx.npt("ln_pb", [128, 2, 512], F32) as pb,
    ):
        sq_t = [ctx.a(xq[:, cc_, :], x[:, cc_, :], AF.Square) for cc_ in range(C)]
        ev_s = None
        ev_q = None
        for lt in range(4):
            lsl = slice(lt * 512, (lt + 1) * 512)
            for cc_ in range(C):
                ctx.mm(pr[0:1, 0, :], onf[:, 0:1], x[:, cc_, lsl],
                       start=(cc_ == 0), stop=(cc_ == C - 1),
                       waits=[ev_s] if cc_ == 0 else None, inc=(cc_ == C - 1))
            pt_s = ctx.pe.tok()
            for cc_ in range(C):
                ctx.mm(pr[0:1, 1, :], onb[:, 0:1], xq[:, cc_, lsl],
                       start=(cc_ == 0), stop=(cc_ == C - 1),
                       waits=([ev_q] if ev_q else []) + [sq_t[cc_]],
                       inc=(cc_ == C - 1))
            pt_q = ctx.pe.tok()
            ev_s = ctx.cp(rows[0:1, 0, lsl], pr[0:1, 0, :], waits=[pt_s])
            ev_q = ctx.cp(rows[0:1, 1, lsl], pr[0:1, 1, :], waits=[pt_q])
        r0, r1, r2 = rows[0:1, 0, :], rows[0:1, 1, :], rows[0:1, 2, :]
        ctx.ts(r0, r0, 1.0 / D, None, OP.mult, OP.bypass, waits=[ev_s, ev_q])
        ctx.nc.vector.drain()
        ctx.tt(r2, r0, r0, OP.mult)
        ctx.nc.vector.drain()
        t_var = ctx.stt(r1, r1, 1.0 / D, r2, OP.mult, OP.subtract)
        t_ln = ctx.a(r2, r1, AF.Ln, bias=ctx.sb["epsb"][0:1, 0:1],
                     waits=[t_var])
        ctx.nc.scalar.drain()
        t_rs = ctx.a(r1, r2, AF.Exp, scale=-0.5)
        t_mu = ctx.dve.tok()
        tmp_free = [None, None]
        bc_done = None
        last = None
        for lt in range(4):
            lsl = slice(lt * 512, (lt + 1) * 512)
            ctx.mm(pb[:, 0, :], onf[0:1, :], r0[:, lsl], start=True, stop=True,
                   waits=[t_mu, t_rs, bc_done])
            ctx.mm(pb[:, 1, :], onf[0:1, :], r1[:, lsl], start=True, stop=True,
                   inc=True)
            pt = ctx.pe.tok()
            for cc_ in range(C):
                s = cc_ % 2
                ctx.tt(tmp[:, s, :], x[:, cc_, lsl], pb[:, 0, :], OP.subtract,
                       waits=[pt, tmp_free[s]])
                tmp_free[s] = ctx.tt(xn[:, cc_, lsl], tmp[:, s, :], pb[:, 1, :],
                                     OP.mult)
                last = tmp_free[s]
            bc_done = last
        ctx.phase()


def emit_l0ffn(ctx):
    nc = ctx.nc
    xn_cm = ctx.nct("xn0", [128, C, T], BF16)
    ctx.sb["xn"] = xn_cm.__enter__()
    cw_cm = ctx.nct("cw_em", [8, T], BF16)
    ctx.sb["cw_em"] = cw_cm.__enter__()
    emit_ln(ctx)
    x, xn = ctx.sb["x"], ctx.sb["xn"]
    idf, idb = ctx.sb["ident_f"], ctx.sb["ident_b"]
    sel8 = ctx.sb["sel8"]
    gate_w, gate_b = ctx.sb["gate_w"], ctx.sb["gate_b"]
    sb1, sb2 = ctx.sb["sb1"], ctx.sb["sb2"]
    rb1, rb2 = ctx.sb["rb1"], ctx.sb["rb2"]
    cw_em = ctx.sb["cw_em"]
    rw1_ap, rw2_ap = ctx.w["rw1"], ctx.w["rw2"]
    sw1_ap, sw2_ap = ctx.w["sw1"], ctx.w["sw2"]

    with (
        ctx.nct("g_G", [8, T], F32) as G,
        ctx.nct("g_tm", [128, NT, 8], F32) as Gtm,
        ctx.nct("g_E", [128, NT, 8], F32) as Eo,
        ctx.nct("g_pr", [128, NT, 8], F32) as probs,
        ctx.nct("g_m1", [128, NT, 8], F32) as m1,
        ctx.nct("g_pm", [128, NT, 8], F32) as pm,
        ctx.nct("g_mk", [128, NT, 8], F32) as msk,
        ctx.nct("g_cw", [128, NT, 8], BF16) as cwtm,
        ctx.nct("g_r1", [128, NT, 1], F32) as nm_,
        ctx.nct("g_r2", [128, NT, 1], F32) as ss,
        ctx.nct("g_r3", [128, NT, 1], F32) as rr,
        ctx.nct("g_r4", [128, NT, 1], F32) as p1,
        ctx.nct("g_r5", [128, NT, 1], F32) as p2,
        ctx.npt("g_ps", [128, 512], F32) as psg,
        ctx.npt("g_pt", [128, 2, 512], F32) as psgt,
        ctx.npt("g_ct0", [128, 128], BF16) as psct0,
        ctx.npt("g_ct1", [128, 128], BF16) as psct1,
    ):
        ev = None
        for lt in range(4):
            lsl = slice(lt * 512, (lt + 1) * 512)
            for k in range(C):
                ctx.mm(psg[0:8, :], gate_w[:, k, :], xn[:, k, lsl],
                       start=(k == 0), stop=(k == C - 1),
                       waits=[ev] if k == 0 else None, inc=(k == C - 1))
            pt = ctx.pe.tok()
            ev = ctx.ts(G[0:8, lsl], psg[0:8, :], gate_b[:, :], None,
                        OP.add, OP.bypass, waits=[pt])
        ev2 = [None, None]
        for i in range(NT):
            s = i % 2
            ctx.xp(psgt[:, s, 0:8], G[0:8, i * 128:(i + 1) * 128], idf[0:8, 0:8],
                   waits=[ev, ev2[s]], inc=True)
            pt = ctx.pe.tok()
            ev2[s] = ctx.cp(Gtm[:, i, :], psgt[:, s, 0:8], waits=[pt])
        gt = ctx.dve.tok()
        for i in range(NT):
            ctx.red(nm_[:, i, :], Gtm[:, i, :], OP.max, negate=True,
                    waits=[gt] if i == 0 else None)
        dt_ = ctx.dve.tok()
        tE = None
        for i in range(NT):
            tE = ctx.a(Eo[:, i, :], Gtm[:, i, :], AF.Exp, bias=nm_[:, i, :],
                       waits=[dt_] if i == 0 else None)
        for i in range(NT):
            ctx.red(ss[:, i, :], Eo[:, i, :], OP.add,
                    waits=[tE] if i == 0 else None)
        for i in range(NT):
            ctx.recip(rr[:, i, :], ss[:, i, :])
        for i in range(NT):
            ctx.ts(probs[:, i, :], Eo[:, i, :], rr[:, i, :], None, OP.mult,
                   OP.bypass)
        for i in range(NT):
            ctx.red(p1[:, i, :], probs[:, i, :], OP.max)
        for i in range(NT):
            ctx.ts(m1[:, i, :], probs[:, i, :], p1[:, i, :], None, OP.is_ge,
                   OP.bypass)
        for i in range(NT):
            ctx.stt(pm[:, i, :], m1[:, i, :], -1e30, probs[:, i, :], OP.mult,
                    OP.add)
        for i in range(NT):
            ctx.red(p2[:, i, :], pm[:, i, :], OP.max)
        for i in range(NT):
            ctx.ts(msk[:, i, :], probs[:, i, :], p2[:, i, :], None, OP.is_ge,
                   OP.bypass)
        tcw = None
        for i in range(NT):
            tcw = ctx.stt(cwtm[:, i, :], msk[:, i, :], 0.5, probs[:, i, :],
                          OP.mult, OP.mult)
        ev3 = [None, None]
        psct = [psct0, psct1]
        for i in range(NT):
            s = i % 2
            ctx.xp(psct[s][0:8, :], cwtm[:, i, :], idb[:, :],
                   waits=[tcw, ev3[s]], inc=True)
            pt = ctx.pe.tok()
            ev3[s] = ctx.cp(cw_em[0:8, i * 128:(i + 1) * 128], psct[s][0:8, :],
                            waits=[pt])
        ctx.phase()
    if DEBUG_STOP == "gate":
        return

    with (
        ctx.nct("f_w1", [128, 8, 8, 2, 128], BF16) as w1b,
        ctx.nct("f_w2", [128, 8, 1024], BF16) as w2b,
        ctx.nct("f_sa", [128, 3, LTF], BF16) as sa,
        ctx.nct("f_sw", [128, 2, 8, LTF], BF16) as swg,
        ctx.nct("f_cwb", [128, T], BF16) as cwb,
        ctx.nct("f_po", [128, 8, LTF], F32) as posb0,
        ctx.nct("f_s2h", [128, 8], F32) as sb2h,
        ctx.npt("f_p1", [128, 3, 2, 256], F32) as pmm,
        ctx.npt("f_po", [128, 8, 256], F32) as po,
        ctx.npt("f_pb", [128, 512], F32) as psbc,
    ):
        po_flat = po.rearrange("p a b -> p (a b)")
        w1sl = [ctx.slot(f"w1s{j}") for j in range(8)]
        w2sl = [ctx.slot(f"w2s{j}") for j in range(8)]
        w1_tok = [None] * 8
        w2_tok = [None] * 8

        def w1_src(e, j, b):
            src = sw1_ap if e == 0 else rw1_ap[e - 1]
            r = src.rearrange("(k p) (b m) -> p k b m", p=128, b=16)
            return r[:, :, j + 8 * b, :]

        def w2_src(e, j):
            src = sw2_ap if e == 0 else rw2_ap[e - 1]
            return src[j * 128:(j + 1) * 128, :]

        def fetch_w1(e, j):
            if e >= 9:
                return
            ctx.dma(ctx.pool, w1b[:, j, :, 0, :], w1_src(e, j, 0), slot=w1sl[j])
            w1_tok[j] = ctx.dma(ctx.pool, w1b[:, j, :, 1, :], w1_src(e, j, 1),
                                slot=w1sl[j])

        def fetch_w2(e, j):
            if e >= 9:
                return
            w2_tok[j] = ctx.dma(ctx.pool, w2b[:, j, :], w2_src(e, j),
                                slot=w2sl[j])

        for j in range(8):
            fetch_w1(0, j)
        for j in range(8):
            fetch_w2(0, j)

        t_s2h = ctx.ts(sb2h[:, :], sb2[:, :], 0.5, None, OP.mult, OP.bypass)

        mm1_cons = [None, None, None]
        sa_cons = [None, None, None]
        swg_cons = [None, None]
        evac_done = None
        posb_rd = None
        w1_tok = w1_tok
        w2_tok = w2_tok
        cwb_ready = None
        n_exp = 9
        if DEBUG_STOP.startswith("exp"):
            n_exp = int(DEBUG_STOP[3:]) + 1
        for e in range(n_exp):
            if e >= 1:
                ev4 = None
                for lt4 in range(4):
                    ctx.mm(psbc[:, :], sel8[:, (e - 1) * 128:e * 128],
                           cw_em[0:8, lt4 * 512:(lt4 + 1) * 512],
                           start=True, stop=True, waits=[ev4], inc=True)
                    pt = ctx.pe.tok()
                    ev4 = ctx.cp(cwb[:, lt4 * 512:(lt4 + 1) * 512], psbc[:, :],
                                 waits=[pt])
                cwb_ready = ev4
            for lt in range(NLTF):
                lsl = slice(lt * LTF, (lt + 1) * LTF)
                lr = lt % 2
                t_swg = None
                for j in range(8):
                    sj = j % 3
                    for k in range(C):
                        ctx.mm(pmm[:, sj, 0, :], w1b[:, j, k, 0, :], xn[:, k, lsl],
                               start=(k == 0), stop=(k == C - 1),
                               waits=[w1_tok[j], mm1_cons[sj]] if k == 0 else None,
                               inc=(k == C - 1))
                    ta = ctx.pe.tok()
                    for k in range(C):
                        ctx.mm(pmm[:, sj, 1, :], w1b[:, j, k, 1, :], xn[:, k, lsl],
                               start=(k == 0), stop=(k == C - 1),
                               inc=(k == C - 1))
                    tb = ctx.pe.tok()
                    b_a = sb1[:, j:j + 1] if e == 0 else rb1[:, e - 1, j:j + 1]
                    b_b = (sb1[:, j + 8:j + 9] if e == 0
                           else rb1[:, e - 1, j + 8:j + 9])
                    t_sa = ctx.a(sa[:, sj, :], pmm[:, sj, 0, :], AF.Silu,
                                 bias=b_a, waits=[tb, sa_cons[sj]])
                    t_swg = ctx.stt(swg[:, lr, j, :], pmm[:, sj, 1, :], b_b,
                                    sa[:, sj, :], OP.add, OP.mult,
                                    waits=[tb, t_sa, swg_cons[lr]])
                    sa_cons[sj] = t_swg
                    mm1_cons[sj] = t_swg
                    if lt == NLTF - 1:
                        w1sl[j].free = tb
                        fetch_w1(e + 1, j)
                for ph in range(2):
                    for j in range(8):
                        for bi in range(4):
                            dch = 2 * bi + ph
                            wl = []
                            if ph == 0 and j == 0 and bi == 0:
                                wl = [t_swg, evac_done]
                            if ph == 0 and bi == 0:
                                wl.append(w2_tok[j])
                            ctx.mm(po[:, dch, :],
                                   w2b[:, j, dch * 128:(dch + 1) * 128],
                                   swg[:, lr, j, :], start=(j == 0),
                                   stop=(j == 7), waits=wl or None,
                                   inc=(ph == 1 and j == 7 and bi == 3))
                mm2_last = ctx.pe.tok()
                swg_cons[lr] = mm2_last
                if lt == NLTF - 1:
                    for j in range(8):
                        w2sl[j].free = mm2_last
                        fetch_w2(e + 1, j)
                xv = x[:, :, lsl]
                if e == 0:
                    evac_done = ctx.stt(xv, po_flat, 0.5, xv, OP.mult, OP.add,
                                        waits=[mm2_last])
                else:
                    cb3 = cwb[:, lsl].unsqueeze(1).broadcast_to([128, 8, LTF])
                    t_p1 = ctx.tt(posb0[:, :, :], po[:, :, :], cb3, OP.mult,
                                  waits=[mm2_last, cwb_ready, posb_rd])
                    evac_done = t_p1
                    posb_rd = ctx.tt(xv, xv, posb0[:, :, :], OP.add)
        if DEBUG_STOP == "experts" or DEBUG_STOP.startswith("exp"):
            ctx.phase()
            return
        ev5 = None
        for mch in range(8):
            for lt4 in range(4):
                ctx.mm(psbc[:, :], rb2[0:8, mch * 128:(mch + 1) * 128],
                       cw_em[0:8, lt4 * 512:(lt4 + 1) * 512], start=True,
                       stop=True, waits=[evac_done, ev5], inc=True)
                pt = ctx.pe.tok()
                ev5 = ctx.tt(x[:, mch, lt4 * 512:(lt4 + 1) * 512],
                             x[:, mch, lt4 * 512:(lt4 + 1) * 512],
                             psbc[:, :], OP.add, waits=[pt])
        for cc_ in range(C):
            ctx.ts(x[:, cc_, :], x[:, cc_, :], sb2h[:, cc_:cc_ + 1], None,
                   OP.add, OP.bypass, waits=[t_s2h])
        ctx.phase()
    cw_cm.__exit__(None, None, None)
    xn_cm.__exit__(None, None, None)


def emit_l0attn(ctx):
    nc = ctx.nc
    xn_cm = ctx.nct("xn0a", [128, C, T], BF16)
    ctx.sb["xn"] = xn_cm.__enter__()
    emit_ln(ctx)
    x, xn = ctx.sb["x"], ctx.sb["xn"]
    onf, onb = ctx.sb["ones_f"], ctx.sb["ones_b"]
    q_scr, k_scr, v_scr, y_scr = (ctx.w["q_scr"], ctx.w["k_scr"],
                                  ctx.w["v_scr"], ctx.w["y_scr"])

    with (
        ctx.nct("a_w", [128, 8, 1024], BF16) as wbuf,
        ctx.nct("a_st", [128, 2, 2048], BF16) as stg,
        ctx.npt("a_pp", [128, 2, 512], F32) as pp,
    ):
        w_free = None
        for ti, (wnm, bnm, scr, scale) in enumerate((
                ("a0Wq", "a0bq", q_scr, 0.125),
                ("a0Wk", "a0bk", k_scr, None),
                ("a0Wv", "a0bv", v_scr, None))):
            wt = ctx.dma(ctx.pool, wbuf[:, :, :],
                         ctx.w[wnm].rearrange("(k p) m -> p k m", p=128),
                         waits=[w_free])
            bias = ctx.sb[bnm]
            dma_done = [None, None]
            ev = [None, None]
            for m in range(8):
                sm = m % 2
                for lt in range(4):
                    sl_ = slice(lt * 512, (lt + 1) * 512)
                    sb_ = pp[:, lt % 2, :]
                    for k in range(C):
                        ctx.mm(sb_, wbuf[:, k, m * 128:(m + 1) * 128],
                               xn[:, k, sl_], start=(k == 0), stop=(k == C - 1),
                               waits=([wt, ev[lt % 2], dma_done[sm]]
                                      if k == 0 else None),
                               inc=(k == C - 1))
                    pt = ctx.pe.tok()
                    if scale is None:
                        ev[lt % 2] = ctx.ts(stg[:, sm, sl_], sb_,
                                            bias[:, m:m + 1], None,
                                            OP.add, OP.bypass, waits=[pt])
                    else:
                        ev[lt % 2] = ctx.ts(stg[:, sm, sl_], sb_,
                                            bias[:, m:m + 1], scale,
                                            OP.add, OP.mult, waits=[pt])
                dma_done[sm] = ctx.dma(ctx.sp, scr[m * 128:(m + 1) * 128, :],
                                       stg[:, sm, :], waits=[ev[0], ev[1]])
            w_free = ctx.pe.tok()
        ctx.phase()

    sel16c, sel16r = ctx.sb["sel16c"], ctx.sb["sel16r"]
    with (
        ctx.nct("a_qd", [64, 16, 512], BF16) as qdm,
        ctx.nct("a_kd", [64, 16, 512], BF16) as kdm,
        ctx.nct("a_ve", [16, 2, 64, 128], BF16) as vem,
        ctx.nct("a_E", [16, 16, 512], BF16) as Eb,
        ctx.nct("a_P", [16, 2, 32, 16], BF16) as Pb,
        ctx.nct("a_rl", [16, 512], F32) as rrln,
        ctx.nct("a_rr", [16, 512], BF16) as rr16,
        ctx.nct("a_oh", [64, 2, 16, 128], BF16) as ohd,
        ctx.npt("a_ps", [128, 2, 512], F32) as ps_s,
        ctx.npt("a_po", [128, 2, 512], F32) as ps_o2,
        ctx.npt("a_pr", [128, 2, 512], F32) as ps_r2,
    ):
        q_r = q_scr.rearrange("(h d) l -> d h l", h=16)
        k_r = k_scr.rearrange("(h d) l -> d h l", h=16)
        v_r = v_scr.rearrange("(e d) l -> e d l", e=16)
        qk_done = None
        v_done = [None, None]
        exp_t = [None, None]
        ptt = [None, None]
        tt_t = [None, None]
        oc_t = [None, None]
        oh_dma = [None, None]
        e_free = None
        trl_prev = None
        bc_last = None
        for lt in range(4):
            lsl = slice(lt * 512, (lt + 1) * 512)
            tq = ctx.dma(ctx.sp, qdm[:, :, :], q_r[:, :, lsl], waits=[qk_done])
            tk = ctx.dma(ctx.sp, kdm[:, :, :], k_r[:, :, lsl])

            def emit_summ(b):
                ctx.mm(ps_r2[0:16, 0, :], sel16c[:, b * 16:(b + 1) * 16],
                       Eb[:, b, :], start=(b == 0), stop=(b == 15),
                       waits=([exp_t[b % 2]]
                              + ([e_free, trl_prev] if b == 0 else [])),
                       inc=(b == 15))

            for b in range(16):
                sb_ = b % 2
                for t in range(32):
                    l = b * 32 + t
                    ctx.mm(ps_s[0:16, sb_, t * 16:(t + 1) * 16],
                           kdm[:, :, l], qdm[:, :, l], start=True, stop=True,
                           waits=([tq, tk, exp_t[sb_]] if t == 0 else None),
                           inc=(t == 31))
                t_s1 = ctx.pe.tok()
                exp_t[sb_] = ctx.a(Eb[:, b, :], ps_s[0:16, sb_, :], AF.Exp,
                                   waits=[t_s1, e_free])
                if b >= 1:
                    emit_summ(b - 1)
            emit_summ(15)
            qk_done = ctx.pe.tok()
            tl = ctx.a(rrln[:, :], ps_r2[0:16, 0, :], AF.Ln, waits=[qk_done])
            trl_prev = tl
            tr = ctx.a(rr16[:, :], rrln[:, :], AF.Exp, scale=-1.0,
                       waits=[tl, bc_last])

            def emit_bc(b):
                bank = b % 2
                ctx.mm(ps_r2[0:16, 1, bank * 256:(bank + 1) * 256]
                       if False else ps_r2[0:16, bank, :],
                       sel16r[:, b * 16:(b + 1) * 16], rr16[:, :],
                       start=True, stop=True,
                       waits=[tr, tt_t[bank]]
                       + ([trl_prev] if bank == 0 else []), inc=True)
                return ctx.pe.tok()

            def emit_tt(b, t_bc):
                bank = b % 2
                pv = ps_r2[0:16, bank, :].rearrange("p (l h) -> p l h", h=16)
                tt_t[bank] = ctx.tt(
                    Pb[:, b % 2, :, :],
                    Eb[:, b, :].rearrange("p (l h) -> p l h", h=16),
                    pv, OP.mult, waits=[t_bc, ptt[b % 2]])

            t_bc = emit_bc(0)
            emit_tt(0, t_bc)
            for b in range(16):
                sb_ = b % 2
                vt = b // 4
                vs = vt % 2
                if b % 4 == 0:
                    gl = slice(lt * 512 + vt * 128, lt * 512 + (vt + 1) * 128)
                    v_tok = ctx.dma(ctx.sp, vem[:, vs, :, :], v_r[:, :, gl],
                                    waits=[v_done[vs]])
                    v_done[vs] = v_tok
                if b + 1 < 16:
                    t_bc = emit_bc(b + 1)
                    emit_tt(b + 1, t_bc)
                for t in range(32):
                    l = b * 32 + t
                    tl_ = l % 128
                    ctx.mm(ps_o2[0:64, sb_, t * 16:(t + 1) * 16],
                           vem[:, vs, :, tl_], Pb[:, sb_, t, :],
                           start=True, stop=True,
                           waits=([tt_t[sb_], v_tok, oc_t[sb_]]
                                  if t == 0 else None),
                           inc=(t == 31))
                ptt[sb_] = ctx.pe.tok()
                if b % 4 == 3:
                    v_done[vs] = ptt[sb_]
                os_ = vt % 2
                oc_t[sb_] = ctx.cp(
                    ohd[:, os_, :, (b % 4) * 32:(b % 4 + 1) * 32],
                    ps_o2[0:64, sb_, :].rearrange("p (l h) -> p h l", h=16),
                    waits=[ptt[sb_], oh_dma[os_]])
                if b % 4 == 3:
                    oh_dma[os_] = ctx.dma(
                        ctx.sp, y_scr[:, :, lt * 512 + vt * 128:
                                      lt * 512 + (vt + 1) * 128],
                        ohd[:, os_, :, :], waits=[oc_t[sb_]])
            bc_last = t_bc
            e_free = tt_t[1]
        ctx.phase()

    with (
        ctx.nct("a_wo", [128, 8, 1024], BF16) as wo,
        ctx.npt("a_pp2", [128, 2, 512], F32) as pp2,
    ):
        wt = ctx.dma(ctx.pool, wo[:, :, :],
                     ctx.w["a0Wo"].rearrange("(k p) m -> p k m", p=128))
        for h in range(16):
            ctx.dma(ctx.sp, xn[(h % 2) * 64:(h % 2) * 64 + 64, h // 2, :],
                    y_scr[:, h, :])
        ytok = (ctx.sp.sems[ctx.sp.cur], ctx.sp.count)
        bo = ctx.sb["a0bo"]
        ev = [None, None]
        for m in range(8):
            for lt in range(4):
                sl_ = slice(lt * 512, (lt + 1) * 512)
                sb_ = pp2[:, lt % 2, :]
                for k in range(C):
                    ctx.mm(sb_, wo[:, k, m * 128:(m + 1) * 128], xn[:, k, sl_],
                           start=(k == 0), stop=(k == C - 1),
                           waits=[wt, ytok, ev[lt % 2]] if k == 0 else None,
                           inc=(k == C - 1))
                pt = ctx.pe.tok()
                ev[lt % 2] = ctx.stt(x[:, m, sl_], sb_, bo[:, m:m + 1],
                                     x[:, m, sl_], OP.add, OP.add, waits=[pt])
        ctx.phase()
    xn_cm.__exit__(None, None, None)


def emit_l1ffn(ctx):
    nc = ctx.nc
    x = ctx.sb["x"]
    onf, onb = ctx.sb["ones_f"], ctx.sb["ones_b"]
    idf, idb = ctx.sb["ident_f"], ctx.sb["ident_b"]
    sel8 = ctx.sb["sel8"]
    centT = ctx.sb["centT"]
    gamma, beta = ctx.sb["gamma"], ctx.sb["beta"]
    b1c, b2c = ctx.sb["b1"], ctx.sb["b2"]
    x_scr, xn_scr = ctx.w["x_scr"], ctx.w["xn_scr"]
    W1T_ap, W2T_ap = ctx.w["W1T"], ctx.w["W2T"]
    lam_cm = ctx.nct("lam_em", [8, T], BF16)
    lam_em = lam_cm.__enter__()

    G_cm = ctx.nct("G_l1", [8, T], F32)
    G = G_cm.__enter__()
    with (
        ctx.nct("l1_xq", [128, C, T], BF16) as xq,
        ctx.nct("l1_rows", [1, 3, T], F32) as rows,
        ctx.nct("l1_xst", [128, 2, C, 512], BF16) as xst,
        ctx.nct("l1_tmp", [128, 2, 512], F32) as tmp,
        ctx.npt("l1_pr", [128, 2, 512], F32) as pr,
        ctx.npt("l1_pb", [128, 2, 512], F32) as pb,
        ctx.npt("l1_pg", [128, 512], F32) as pg,
    ):
        sq_t = [ctx.a(xq[:, cc_, :], x[:, cc_, :], AF.Square) for cc_ in range(C)]
        ev_s = None
        ev_q = None
        for lt in range(4):
            lsl = slice(lt * 512, (lt + 1) * 512)
            for cc_ in range(C):
                ctx.mm(pr[0:1, 0, :], onf[:, 0:1], x[:, cc_, lsl],
                       start=(cc_ == 0), stop=(cc_ == C - 1),
                       waits=[ev_s] if cc_ == 0 else None, inc=(cc_ == C - 1))
            pt_s = ctx.pe.tok()
            for cc_ in range(C):
                ctx.mm(pr[0:1, 1, :], onb[:, 0:1], xq[:, cc_, lsl],
                       start=(cc_ == 0), stop=(cc_ == C - 1),
                       waits=([ev_q] if ev_q else []) + [sq_t[cc_]],
                       inc=(cc_ == C - 1))
            pt_q = ctx.pe.tok()
            ev_s = ctx.cp(rows[0:1, 0, lsl], pr[0:1, 0, :], waits=[pt_s])
            ev_q = ctx.cp(rows[0:1, 1, lsl], pr[0:1, 1, :], waits=[pt_q])
        r0, r1, r2 = rows[0:1, 0, :], rows[0:1, 1, :], rows[0:1, 2, :]
        ctx.ts(r0, r0, 1.0 / D, None, OP.mult, OP.bypass, waits=[ev_s, ev_q])
        ctx.nc.vector.drain()
        ctx.tt(r2, r0, r0, OP.mult)
        ctx.nc.vector.drain()
        t_var = ctx.stt(r1, r1, 1.0 / D, r2, OP.mult, OP.subtract)
        t_ln = ctx.a(r2, r1, AF.Ln, bias=ctx.sb["epsb"][0:1, 0:1], waits=[t_var])
        ctx.nc.scalar.drain()
        t_rs = ctx.a(r1, r2, AF.Exp, scale=-0.5)
        t_mu = ctx.dve.tok()
        tmp_free = [None, None]
        st_dma = [None, None]
        bc_done = None
        g_ev = None
        for lt in range(4):
            lsl = slice(lt * 512, (lt + 1) * 512)
            sl2 = lt % 2
            ctx.mm(pb[:, 0, :], onf[0:1, :], r0[:, lsl], start=True, stop=True,
                   waits=[t_mu, t_rs, bc_done])
            ctx.mm(pb[:, 1, :], onf[0:1, :], r1[:, lsl], start=True, stop=True,
                   inc=True)
            pt = ctx.pe.tok()
            last = None
            for cc_ in range(C):
                s = cc_ % 2
                ctx.tt(tmp[:, s, :], x[:, cc_, lsl], pb[:, 0, :], OP.subtract,
                       waits=[pt, tmp_free[s], st_dma[sl2]])
                tmp_free[s] = ctx.tt(xst[:, sl2, cc_, :], tmp[:, s, :],
                                     pb[:, 1, :], OP.mult)
                last = tmp_free[s]
            bc_done = last
            st_dma[sl2] = ctx.dma(ctx.sp, xn_scr[:, :, lsl], xst[:, sl2, :, :],
                                  waits=[last])
            for cc_ in range(C):
                ctx.mm(pg[0:8, :], centT[:, cc_, :], xst[:, sl2, cc_, :],
                       start=(cc_ == 0), stop=(cc_ == C - 1),
                       waits=[last, g_ev] if cc_ == 0 else None,
                       inc=(cc_ == C - 1))
            pt2 = ctx.pe.tok()
            g_ev = ctx.cp(G[0:8, lsl], pg[0:8, :], waits=[pt2])
            if lt == 3:
                pass
        sp_toks = []
        for lt in range(4):
            lsl = slice(lt * 512, (lt + 1) * 512)
            sp_toks.append(ctx.dma(ctx.sp, x_scr[:, :, lsl], x[:, :, lsl],
                                   waits=[bc_done]))
        ctx.phase()

    with (
        ctx.nct("g2_tm", [128, NT, 8], F32) as Gtm,
        ctx.nct("g2_sm", [128, NT, 1], F32) as smax,
        ctx.nct("g2_al", [128, NT, 1], F32) as alph,
        ctx.nct("g2_m", [128, NT, 8], F32) as m1,
        ctx.nct("g2_lm", [128, NT, 8], BF16) as lamtm,
        ctx.npt("g2_pt", [128, 2, 512], F32) as psgt,
        ctx.npt("g2_c0", [128, 128], BF16) as psct0,
        ctx.npt("g2_c1", [128, 128], BF16) as psct1,
    ):
        ev2 = [None, None]
        for i in range(NT):
            s = i % 2
            ctx.xp(psgt[:, s, 0:8], G[0:8, i * 128:(i + 1) * 128], idf[0:8, 0:8],
                   waits=[ev2[s]], inc=True)
            pt = ctx.pe.tok()
            ev2[s] = ctx.cp(Gtm[:, i, :], psgt[:, s, 0:8], waits=[pt])
        gt = ctx.dve.tok()
        for i in range(NT):
            ctx.red(smax[:, i, :], Gtm[:, i, :], OP.max,
                    waits=[gt] if i == 0 else None)
        dt_ = ctx.dve.tok()
        tA = None
        for i in range(NT):
            tA = ctx.a(alph[:, i, :], smax[:, i, :], AF.Sigmoid,
                       waits=[dt_] if i == 0 else None)
        for i in range(NT):
            ctx.ts(m1[:, i, :], Gtm[:, i, :], smax[:, i, :], None, OP.is_ge,
                   OP.bypass, waits=[tA] if i == 0 else None)
        tlam = None
        for i in range(NT):
            tlam = ctx.ts(lamtm[:, i, :], m1[:, i, :], alph[:, i, :], 0.5,
                          OP.mult, OP.mult)
        ev3 = [None, None]
        psct = [psct0, psct1]
        for i in range(NT):
            s = i % 2
            ctx.xp(psct[s][0:8, :], lamtm[:, i, :], idb[:, :],
                   waits=[tlam, ev3[s]], inc=True)
            pt = ctx.pe.tok()
            ev3[s] = ctx.cp(lam_em[0:8, i * 128:(i + 1) * 128], psct[s][0:8, :],
                            waits=[pt])
        ctx.phase()
    G_cm.__exit__(None, None, None)

    y_cm = ctx.nct("y_acc", [128, C, T], BF16)
    y = y_cm.__enter__()
    w1b = x[:, :, :].bitcast(BF16).rearrange("p c (u l) -> p c (u l)")         if False else x[:, :, :].bitcast(BF16)
    with (
        ctx.nct("f1_w2", [128, 32, 1024], BF16) as w2b,
        ctx.nct("f1_r", [128, 32, LTF], BF16) as rb,
        ctx.nct("f1_xn", [128, C, LTF], BF16) as xnst,
        ctx.nct("f1_xe", [128, C, LTF], BF16) as xe,
        ctx.nct("f1_ps", [128, 4, LTF], BF16) as posb,
        ctx.nct("f1_lb", [128, T], BF16) as lamb,
        ctx.npt("f1_p1", [128, 3, 512], F32) as pm1,
        ctx.npt("f1_po", [128, 8, 256], F32) as po,
        ctx.npt("f1_pb", [128, 512], F32) as psbc,
    ):
        po_flat = po.rearrange("p a b -> p (a b)")
        w1sl = [ctx.slot(f"v1s{j}") for j in range(8)]
        w2sl = [ctx.slot(f"v2s{j}") for j in range(8)]
        w1_tok = [None] * 8
        w2_tok = [None] * 8

        def fetch_w1(e, j):
            if e >= 8:
                return
            w1_tok[j] = ctx.dma(
                ctx.pool, w1b[:, :, j * 512:(j + 1) * 512],
                W1T_ap[e].rearrange("(k p) m -> p k m", p=128)[:, :, j * 512:(j + 1) * 512],
                slot=w1sl[j])

        def fetch_w2(e, j):
            if e >= 8:
                return
            w2_tok[j] = ctx.dma(
                ctx.pool, w2b[:, j * 4:(j + 1) * 4, :],
                W2T_ap[e].rearrange("(m p) d -> p m d", p=128)[:, j * 4:(j + 1) * 4, :],
                slot=w2sl[j])

        for j in range(8):
            fetch_w1(0, j)
        for j in range(8):
            fetch_w2(0, j)

        xn_r = xn_scr
        mm1_cons = [None, None, None]
        xe_t = None
        xnst_t = None
        mm1_done = None
        mm2_done = None
        evac_done = None
        posb_rd = None
        r_done = None
        for e in range(8):
            ev4 = None
            for lt4 in range(4):
                ctx.mm(psbc[:, :], sel8[:, e * 128:(e + 1) * 128],
                       lam_em[0:8, lt4 * 512:(lt4 + 1) * 512],
                       start=True, stop=True, waits=[ev4], inc=True)
                pt = ctx.pe.tok()
                ev4 = ctx.cp(lamb[:, lt4 * 512:(lt4 + 1) * 512], psbc[:, :],
                             waits=[pt])
            lam_ready = ev4
            for lt in range(NLTF):
                lsl = slice(lt * LTF, (lt + 1) * LTF)
                xnst_t = ctx.dma(ctx.sp, xnst[:, :, :], xn_r[:, :, lsl],
                                 waits=[xe_t])
                for cc_ in range(C):
                    xe_t = ctx.ts(xe[:, cc_, :], xnst[:, cc_, :],
                                  gamma[:, e, cc_:cc_ + 1],
                                  beta[:, e, cc_:cc_ + 1], OP.mult, OP.add,
                                  waits=[xnst_t, mm1_done] if cc_ == 0 else None)
                t_r_d = None
                t_r_a = None
                for m in range(32):
                    sm = m % 3
                    jw = m // 4
                    for k in range(C):
                        ctx.mm(pm1[:, sm, 0:256], w1b[:, k, m * 128:(m + 1) * 128],
                               xe[:, k, :], start=(k == 0), stop=(k == C - 1),
                               waits=([w1_tok[jw], xe_t, mm1_cons[sm]]
                                      if k == 0 else None),
                               inc=(k == C - 1))
                    t_m = ctx.pe.tok()
                    bcol = b1c[:, e, m:m + 1]
                    if m % 2 == 0:
                        t_r_d = ctx.ts(rb[:, m, :], pm1[:, sm, 0:256], bcol, 0.0,
                                       OP.add, OP.max, waits=[t_m])
                        mm1_cons[sm] = t_r_d
                    else:
                        t_r_a = ctx.a(rb[:, m, :], pm1[:, sm, 0:256], AF.Relu,
                                      bias=bcol, waits=[t_m])
                        mm1_cons[sm] = t_r_a
                    if lt == NLTF - 1 and m % 4 == 3:
                        w1sl[m // 4].free = t_m
                        fetch_w1(e + 1, m // 4)
                mm1_done = ctx.pe.tok()
                for ph in range(2):
                    for m in range(32):
                        for bi in range(4):
                            dch = 2 * bi + ph
                            wl = []
                            if ph == 0 and m == 0 and bi == 0:
                                wl = [t_r_d, t_r_a, evac_done]
                            if ph == 0 and bi == 0 and m % 4 == 0:
                                wl.append(w2_tok[m // 4])
                            ctx.mm(po[:, dch, :],
                                   w2b[:, m, dch * 128:(dch + 1) * 128],
                                   rb[:, m, :], start=(m == 0), stop=(m == 31),
                                   waits=wl or None,
                                   inc=(ph == 1 and m == 31 and bi == 3))
                mm2_done = ctx.pe.tok()
                lb4 = lamb[:, lsl].unsqueeze(1).broadcast_to([128, 4, LTF])
                yv = y[:, :, lsl]
                if e == 0:
                    lb8 = lamb[:, lsl].unsqueeze(1).broadcast_to([128, 8, LTF])
                    evac_done = ctx.tt(yv, po[:, :, :], lb8, OP.mult,
                                       waits=[mm2_done, lam_ready])
                else:
                    for hh in range(2):
                        t_p1 = ctx.tt(posb[:, :, :], po[:, hh * 4:hh * 4 + 4, :],
                                      lb4, OP.mult,
                                      waits=[mm2_done, lam_ready, posb_rd])
                        posb_rd = ctx.tt(y[:, hh * 4:hh * 4 + 4, lsl],
                                         y[:, hh * 4:hh * 4 + 4, lsl],
                                         posb[:, :, :], OP.add)
                    evac_done = t_p1
                if lt == NLTF - 1:
                    for j in range(8):
                        w2sl[j].free = mm2_done
                        fetch_w2(e + 1, j)
        ctx.phase()

    with (
        ctx.nct("c_xs", [128, 2, C, 512], F32) as xs,
        ctx.nct("c_xn", [128, 2, C, 512], BF16) as xns,
        ctx.nct("c_t", [128, 2, 512], F32) as ct,
        ctx.npt("c_pb", [128, 2, 512], F32) as cpb,
    ):
        done = [None, None]
        for lt in range(4):
            lsl = slice(lt * 512, (lt + 1) * 512)
            s = lt % 2
            tx = ctx.dma(ctx.sp, xs[:, s, :, :], x_scr[:, :, lsl],
                         waits=[done[s]])
            tn = ctx.dma(ctx.sp, xns[:, s, :, :], xn_scr[:, :, lsl])
            last = None
            for cc_ in range(C):
                sc = cc_ % 2
                ctx.mm(cpb[:, sc, :], b2c[0:8, cc_ * 128:(cc_ + 1) * 128],
                       lam_em[0:8, lsl], start=True, stop=True,
                       waits=[done[s], last] if cc_ < 2 else [last], inc=True)
                ptb = ctx.pe.tok()
                ctx.stt(ct[:, sc, :], xns[:, s, cc_, :], 0.5, xs[:, s, cc_, :],
                        OP.mult, OP.add, waits=[tx, tn])
                ctx.tt(ct[:, sc, :], ct[:, sc, :], y[:, cc_, lsl], OP.add)
                last = ctx.tt(x[:, cc_, lsl], ct[:, sc, :], cpb[:, sc, :],
                              OP.add, waits=[ptb])
            done[s] = last
        ctx.phase()
    y_cm.__exit__(None, None, None)
    lam_cm.__exit__(None, None, None)


def emit_l1attn(ctx):
    nc = ctx.nc
    xn_cm = ctx.nct("xn1a", [128, C, T], BF16)
    ctx.sb["xn"] = xn_cm.__enter__()
    emit_ln(ctx)
    x, xn = ctx.sb["x"], ctx.sb["xn"]
    onf = ctx.sb["ones_f"]
    maskb = ctx.sb["maskb"]

    wqd_cm = ctx.nct("t_wqd", [128, 3, 64], BF16)
    wqd = wqd_cm.__enter__()
    for half in (0, 64):
        for ti, src in enumerate(("a1WqT", "a1WkT", "a1WvT")):
            ctx.dma(ctx.pool, wqd[half:half + 64, ti, :], ctx.w[src])
    wqd_t = (ctx.pool.sems[ctx.pool.cur], ctx.pool.count)
    with (
        ctx.nct("t_q", [64, 2, T], BF16) as q_sb,
        ctx.nct("t_k", [64, 2, T], BF16) as k_sb,
        ctx.nct("t_vT", [128, 2, 16, 65], BF16) as vT,
        ctx.nct("t_E", [128, 2, 1024], BF16) as E2,
        ctx.nct("t_yb", [128, 2, 512], F32) as yb,
        ctx.nct("t_rr", [1, 2, 512], F32) as rrow,
        ctx.nct("t_rl", [1, 2, 512], F32) as rln,
        ctx.npt("t_ps", [128, 4, 512], F32) as ps_s,
        ctx.npt("t_pa", [128, 2, 512], F32) as pav,
        ctx.npt("t_pb", [128, 512], F32) as pbc,
    ):
        t_ones = ctx.cp(
            vT[:, :, :, 64],
            ctx.sb["ones_b"][:, 0:32].rearrange("p (a b) -> p a b", a=2))
        ps_ev = [None, None, None, None]
        exp_w = [None, None]
        av_rd = [None, None]
        yb_cp = [None, None]
        yw_t = [None, None]
        bc_t = [None, None]
        yw_last = [None]
        pend = [None]

        def emit_finish(hp_, hc_, qt_, pa_, tr_):
            qsl_ = slice(qt_ * 512, (qt_ + 1) * 512)
            ctx.mm(pbc[0:64, :], onf[0:1, 0:64], rrow[0:1, pa_, :],
                   start=True, stop=True, waits=[tr_, yw_last[0]], inc=True)
            bc_t[pa_] = ctx.pe.tok()
            yw_t[pa_] = ctx.tt(xn[hp_:hp_ + 64, hc_, qsl_], yb[0:64, pa_, :],
                               pbc[0:64, :], OP.mult, waits=[bc_t[pa_]])
            yw_last[0] = yw_t[pa_]

        for h in range(16):
            hp = (h % 2) * 64
            hc = h // 2
            hs = h % 2
            rr_i = 0
            for ti, scale in ((0, 0.125), (1, None)):
                dst = q_sb if ti == 0 else k_sb
                for lt in range(4):
                    lsl = slice(lt * 512, (lt + 1) * 512)
                    b = rr_i % 4
                    rr_i += 1
                    ctx.mm(ps_s[0:64, b, :], wqd[hp:hp + 64, ti, :],
                           xn[hp:hp + 64, hc, lsl], start=True, stop=True,
                           waits=[ps_ev[b], wqd_t], inc=True)
                    pt = ctx.pe.tok()
                    if scale is None:
                        ps_ev[b] = ctx.cp(dst[:, hs, lsl], ps_s[0:64, b, :],
                                          waits=[pt], eng=ctx.act)
                    else:
                        ps_ev[b] = ctx.ts(dst[:, hs, lsl], ps_s[0:64, b, :],
                                          scale, None, OP.mult, OP.bypass,
                                          waits=[pt])
            for kc in range(16):
                b = rr_i % 4
                rr_i += 1
                ctx.mm(ps_s[0:128, b, 0:64],
                       xn[hp:hp + 64, hc, kc * 128:(kc + 1) * 128],
                       wqd[hp:hp + 64, 2, :], start=True, stop=True,
                       waits=[ps_ev[b]], inc=True)
                pt = ctx.pe.tok()
                ps_ev[b] = ctx.cp(vT[:, hs, kc, 0:64], ps_s[0:128, b, 0:64],
                                  waits=[pt],
                                  eng=(ctx.act if kc % 2 else ctx.dve))
            qk_ready = list(ps_ev) + [t_ones]
            for qt in range(4):
                qsl = slice(qt * 512, (qt + 1) * 512)
                nkc = 4 * qt + 4
                npair = nkc // 2
                pa = qt % 2

                def emit_Spair(p):
                    s = p % 2
                    for half in range(2):
                        kc = 2 * p + half
                        b = 2 * s + half
                        w = [ps_ev[b]] + (qk_ready if kc == 0 else [])
                        ctx.mm(ps_s[:, b, :],
                               k_sb[:, hs, kc * 128:(kc + 1) * 128],
                               q_sb[:, hs, qsl], start=True, stop=True,
                               waits=w, inc=True)
                    t_s = ctx.pe.tok()
                    et = ctx.a(E2[:, s, :],
                               ps_s[:, 2 * s:2 * s + 2, :].rearrange(
                                   "p a b -> p (a b)"),
                               AF.Exp, waits=[t_s, av_rd[s]])
                    ps_ev[2 * s] = et
                    ps_ev[2 * s + 1] = et
                    for half in range(2):
                        kc = 2 * p + half
                        if kc >= 4 * qt:
                            off = (kc - 4 * qt) * 128
                            et = ctx.tt(
                                E2[:, s, half * 512:(half + 1) * 512],
                                E2[:, s, half * 512:(half + 1) * 512],
                                maskb[:, 384 - off:896 - off], OP.mult,
                                waits=[et])
                    exp_w[s] = et

                def emit_AVpair(p):
                    s = p % 2
                    for half in range(2):
                        kc = 2 * p + half
                        ctx.mm(pav[0:65, pa, :], vT[:, hs, kc, 0:65],
                               E2[:, s, half * 512:(half + 1) * 512],
                               start=(kc == 0), stop=(kc == nkc - 1),
                               waits=([exp_w[s], yb_cp[pa]] if kc == 0
                                      else [exp_w[s]]),
                               inc=True)
                        av_rd[s] = ctx.pe.tok()

                emit_Spair(0)
                for p in range(1, npair):
                    emit_Spair(p)
                    emit_AVpair(p - 1)
                    if p == 1 and pend[0] is not None:
                        emit_finish(*pend[0])
                        pend[0] = None
                emit_AVpair(npair - 1)
                last_AV = av_rd[(npair - 1) % 2]
                yb_cp[pa] = ctx.cp(yb[0:65, pa, :], pav[0:65, pa, :],
                                   waits=[last_AV, yw_t[pa]])
                tl = ctx.a(rln[0:1, pa, :], yb[64:65, pa, :], AF.Ln,
                           waits=[yb_cp[pa]])
                tr = ctx.a(rrow[0:1, pa, :], rln[0:1, pa, :], AF.Exp,
                           scale=-1.0, waits=[tl, bc_t[pa]])
                pend[0] = (hp, hc, qt, pa, tr)
        emit_finish(*pend[0])
        pend[0] = None
        ctx.phase()
    wqd_cm.__exit__(None, None, None)

    with (
        ctx.nct("t_wo", [128, 8, 1024], BF16) as wo,
        ctx.npt("t_pp", [128, 2, 512], F32) as pp2,
    ):
        wt = ctx.dma(ctx.pool, wo[:, :, :],
                     ctx.w["a1WoT"].rearrange("(k p) m -> p k m", p=128))
        ev = [None, None]
        for m in range(8):
            for lt in range(4):
                sl_ = slice(lt * 512, (lt + 1) * 512)
                sb_ = pp2[:, lt % 2, :]
                for k in range(C):
                    ctx.mm(sb_, wo[:, k, m * 128:(m + 1) * 128], xn[:, k, sl_],
                           start=(k == 0), stop=(k == C - 1),
                           waits=[wt, ev[lt % 2]] if k == 0 else None,
                           inc=(k == C - 1))
                pt = ctx.pe.tok()
                ev[lt % 2] = ctx.tt(x[:, m, sl_], x[:, m, sl_], sb_, OP.add,
                                    waits=[pt])
        ctx.phase()
    xn_cm.__exit__(None, None, None)



_EMITTERS = {}


def _build_sub(which):
    nc = bass.Bass("TRN2", target_bir_lowering=False)
    with ExitStack() as es:
        ctx = Ctx(nc, es)
        x_in = nc.dram_tensor("x", [T, D], F32, kind="ExternalInput").ap()
        x_out = nc.dram_tensor("xo", [T, D], F32, kind="ExternalOutput").ap()
        for nm in _SUB_WNAMES[which]:
            shape, npdt = _WSPECS[nm]
            ctx.w[nm] = nc.dram_tensor(nm, list(shape), _NP2BIR[npdt],
                                       kind="ExternalInput").ap()
        if which == "l0attn":
            for nm, shp in (("q_scr", [1024, T]), ("k_scr", [1024, T]),
                            ("v_scr", [1024, T]), ("y_scr", [64, 16, T])):
                ctx.w[nm] = nc.dram_tensor(nm, shp, BF16, kind="Internal").ap()
        if which == "l1ffn":
            ctx.w["x_scr"] = nc.dram_tensor("x_scr", [128, C, T], F32,
                                            kind="Internal").ap()
            ctx.w["xn_scr"] = nc.dram_tensor("xn_scr", [128, C, T], BF16,
                                             kind="Internal").ap()
        ctx.sb["x"] = es.enter_context(nc.sbuf_tensor("x_fm", [128, C, T], F32))
        _load_consts(ctx, _SUB_WNAMES[which])
        with nc.psum_tensor("psx", [128, 2, 512], F32) as psx:
            emit_load_x(ctx, x_in, psx)
            ctx.phase()
        if DEBUG_STOP != "loadx":
            _EMITTERS[which](ctx)
        with nc.psum_tensor("psx2", [128, 2, 512], F32) as psx2:
            st = emit_store_x(ctx, x_out, psx2)
            for t in st:
                ctx.sp.wait(t)
        ctx.finalize()
    return nc


_EMITTERS["l0ffn"] = emit_l0ffn
_EMITTERS["l0attn"] = emit_l0attn
_EMITTERS["l1ffn"] = emit_l1ffn
_EMITTERS["l1attn"] = emit_l1attn

def _build_full():
    nc = bass.Bass("TRN2", target_bir_lowering=False)
    names = []
    for sub in ("l0ffn", "l0attn", "l1ffn", "l1attn"):
        for nm in _SUB_WNAMES[sub]:
            if nm not in names:
                names.append(nm)
    with ExitStack() as es:
        ctx = Ctx(nc, es)
        x_in = nc.dram_tensor("x", [T, D], F32, kind="ExternalInput").ap()
        x_out = nc.dram_tensor("xo", [T, D], F32, kind="ExternalOutput").ap()
        for nm in names:
            shape, npdt = _WSPECS[nm]
            ctx.w[nm] = nc.dram_tensor(nm, list(shape), _NP2BIR[npdt],
                                       kind="ExternalInput").ap()
        for nm, shp, dt in (("q_scr", [1024, T], BF16), ("k_scr", [1024, T], BF16),
                            ("v_scr", [1024, T], BF16), ("y_scr", [64, 16, T], BF16),
                            ("x_scr", [128, C, T], F32),
                            ("xn_scr", [128, C, T], BF16)):
            ctx.w[nm] = nc.dram_tensor(nm, shp, dt, kind="Internal").ap()
        ctx.sb["x"] = es.enter_context(nc.sbuf_tensor("x_fm", [128, C, T], F32))
        _load_consts(ctx, names)
        with nc.psum_tensor("psx", [128, 2, 512], F32) as psx:
            emit_load_x(ctx, x_in, psx)
            ctx.phase()
        for sub in _CHAIN:
            _EMITTERS[sub](ctx)
        with nc.psum_tensor("psx2", [128, 2, 512], F32) as psx2:
            st = emit_store_x(ctx, x_out, psx2)
            for t in st:
                ctx.sp.wait(t)
        ctx.finalize()
    return nc


_PROG_CACHE = {}


def _get_prog(which):
    if which not in _PROG_CACHE:
        _PROG_CACHE[which] = _build_sub(which)
    return _PROG_CACHE[which]


def _run_chain(x_full, w, subs):
    x = np.ascontiguousarray(x_full, dtype=np.float32)
    for which in subs:
        nc = _get_prog(which)
        in_maps = []
        for core in range(NCORES):
            m = {"x": np.ascontiguousarray(x[core])}
            for nm in _SUB_WNAMES[which]:
                m[nm] = w[nm]
            in_maps.append(m)
        res = run_bass_kernel_spmd(nc, in_maps, core_ids=list(range(NCORES)))
        x = np.stack([res.results[i]["xo"] for i in range(NCORES)], axis=0)
    return x


_CHAIN = ["l0ffn", "l0attn", "l0ffn", "l1ffn", "l1attn", "l1ffn"]


def _run_fused(x_full, w):
    if "fused" not in _PROG_CACHE:
        _PROG_CACHE["fused"] = _build_full()
    nc = _PROG_CACHE["fused"]
    names = []
    for sub in ("l0ffn", "l0attn", "l1ffn", "l1attn"):
        for nm in _SUB_WNAMES[sub]:
            if nm not in names:
                names.append(nm)
    x = np.ascontiguousarray(x_full, dtype=np.float32)
    in_maps = []
    for core in range(NCORES):
        m = {"x": np.ascontiguousarray(x[core])}
        for nm in names:
            m[nm] = w[nm]
        in_maps.append(m)
    res = run_bass_kernel_spmd(nc, in_maps, core_ids=list(range(NCORES)))
    return np.stack([res.results[i]["xo"] for i in range(NCORES)], axis=0)


def kernel(**inputs):
    w = _prep_weights(inputs)
    x = np.asarray(inputs["L0_attn_x"], dtype=np.float32)
    if os.environ.get("KUNFUSED", ""):
        return _run_chain(x, w, _CHAIN)
    return _run_fused(x, w)


# version 15
# speedup vs baseline: 1.0088x; 1.0088x over previous
import numpy as np
import ml_dtypes
from contextlib import ExitStack

import concourse.bass as bass
import concourse.mybir as mybir
from concourse.bass_utils import run_bass_kernel_spmd
from concourse.alu_op_type import AluOpType as OP

F32 = mybir.dt.float32
BF16 = mybir.dt.bfloat16
AF = mybir.ActivationFunctionType
AX = mybir.AxisListType

T = 2048
D = 1024
C = 8
NT = 16
LTF = 256
NLTF = T // LTF

BF = ml_dtypes.bfloat16
NCORES = 8



def _prep_weights(inp):
    w = {}
    cc = np.ascontiguousarray
    w["gate_w"] = cc(inp["L0_ffn_gate_w"].reshape(8, 128, 8).transpose(1, 0, 2))
    w["gate_b"] = cc(inp["L0_ffn_gate_b"].reshape(8, 1))
    w["sw1"] = cc(inp["L0_ffn_sw1"][0])
    w["sb1"] = cc(inp["L0_ffn_sb1"][0].reshape(16, 128).T)
    w["sw2"] = cc(inp["L0_ffn_sw2"][0])
    w["sb2"] = cc(inp["L0_ffn_sb2"][0].reshape(8, 128).T)
    w["rw1"] = cc(inp["L0_ffn_rw1"])
    w["rb1"] = cc(inp["L0_ffn_rb1"].reshape(8, 16, 128).transpose(2, 0, 1))
    w["rw2"] = cc(inp["L0_ffn_rw2"])
    w["rb2"] = cc(inp["L0_ffn_rb2"])
    for nm in ("Wq", "Wk", "Wv", "Wo"):
        w["a0" + nm] = cc(inp["L0_attn_" + nm])
    for nm in ("bq", "bk", "bv", "bo"):
        w["a0" + nm] = cc(inp["L0_attn_" + nm].reshape(8, 128).T)
    for nm in ("Wq", "Wk", "Wv"):
        w["a1" + nm + "T"] = cc(inp["L1_attn_" + nm].T)
    w["a1WoT"] = cc(inp["L1_attn_Wo"].T)
    w["centT"] = cc(inp["L1_ffn_centroids"].T.reshape(8, 128, 8).transpose(1, 0, 2))
    w["gamma"] = cc(inp["L1_ffn_gamma"].reshape(8, 8, 128).transpose(2, 0, 1))
    w["beta"] = cc(inp["L1_ffn_beta"].reshape(8, 8, 128).transpose(2, 0, 1))
    w["W1T"] = cc(inp["L1_ffn_W1"].transpose(0, 2, 1))
    w["b1"] = cc(inp["L1_ffn_b1"].reshape(8, 32, 128).transpose(2, 0, 1))
    w["W2T"] = cc(inp["L1_ffn_W2"].transpose(0, 2, 1))
    w["b2"] = cc(inp["L1_ffn_b2"])
    w["ident_f"] = np.eye(128, dtype=np.float32)
    w["ident_b"] = np.eye(128, dtype=BF)
    w["ones_f"] = np.ones((128, 128), dtype=np.float32)
    w["ones_b"] = np.ones((128, 128), dtype=BF)
    sel8 = np.zeros((8, 8 * 128), dtype=BF)
    for e in range(8):
        sel8[e, e * 128:(e + 1) * 128] = 1
    w["sel8"] = sel8
    w["causal"] = np.where(np.tril(np.ones((128, 128), dtype=bool)),
                           0.0, -1e30).astype(np.float32)
    w["maskb"] = (np.arange(896)[None, :]
                  >= (np.arange(128)[:, None] + 384)).astype(BF)
    sel16c = np.zeros((16, 256), dtype=BF)
    sel16r = np.zeros((16, 256), dtype=BF)
    for b in range(16):
        sel16c[:, b * 16 + b] = 1
        sel16r[b, b * 16:(b + 1) * 16] = 1
    w["sel16c"] = sel16c
    w["sel16r"] = sel16r
    w["epsb"] = np.full((1, 1), 1e-5, dtype=np.float32)
    return w


_WSPECS = {
    "gate_w": ([128, 8, 8], np.float32), "gate_b": ([8, 1], np.float32),
    "sw1": ([1024, 2048], np.float32), "sb1": ([128, 16], np.float32),
    "sw2": ([1024, 1024], np.float32), "sb2": ([128, 8], np.float32),
    "rw1": ([8, 1024, 2048], np.float32), "rb1": ([128, 8, 16], np.float32),
    "rw2": ([8, 1024, 1024], np.float32), "rb2": ([8, 1024], np.float32),
    "a0Wq": ([1024, 1024], np.float32), "a0Wk": ([1024, 1024], np.float32),
    "a0Wv": ([1024, 1024], np.float32), "a0Wo": ([1024, 1024], np.float32),
    "a0bq": ([128, 8], np.float32), "a0bk": ([128, 8], np.float32),
    "a0bv": ([128, 8], np.float32), "a0bo": ([128, 8], np.float32),
    "a1WqT": ([64, 64], np.float32), "a1WkT": ([64, 64], np.float32),
    "a1WvT": ([64, 64], np.float32), "a1WoT": ([1024, 1024], np.float32),
    "centT": ([128, 8, 8], np.float32), "gamma": ([128, 8, 8], np.float32),
    "beta": ([128, 8, 8], np.float32),
    "W1T": ([8, 1024, 4096], np.float32), "b1": ([128, 8, 32], np.float32),
    "W2T": ([8, 4096, 1024], np.float32), "b2": ([8, 1024], np.float32),
    "ident_f": ([128, 128], np.float32), "ident_b": ([128, 128], BF),
    "ones_f": ([128, 128], np.float32), "ones_b": ([128, 128], BF),
    "sel8": ([8, 8 * 128], BF), "causal": ([128, 128], np.float32),
    "maskb": ([128, 896], BF),
    "sel16c": ([16, 256], BF), "sel16r": ([16, 256], BF),
    "epsb": ([1, 1], np.float32),
}

_SB_BF16 = {"gate_w", "rb2", "b2", "centT", "a1WqT", "a1WkT", "a1WvT"}

_SUB_WNAMES = {
    "l0ffn": ["gate_w", "gate_b", "sw1", "sb1", "sw2", "sb2", "rw1", "rb1",
              "rw2", "rb2", "ident_f", "ident_b", "ones_f", "ones_b", "sel8",
              "epsb"],
    "l0attn": ["a0Wq", "a0Wk", "a0Wv", "a0Wo", "a0bq", "a0bk", "a0bv", "a0bo",
               "ident_f", "ident_b", "ones_f", "ones_b", "sel16c", "sel16r",
               "epsb"],
    "l1ffn": ["centT", "gamma", "beta", "W1T", "b1", "W2T", "b2",
              "ident_f", "ident_b", "ones_f", "ones_b", "sel8", "epsb"],
    "l1attn": ["a1WqT", "a1WkT", "a1WvT", "a1WoT",
               "ident_f", "ident_b", "ones_f", "ones_b", "maskb", "epsb"],
}

_NP2BIR = {np.float32: F32, BF: BF16}

import os
DEBUG_STOP = os.environ.get("KDEBUG", "")



class _Eng:
    def __init__(self, ctx, obj, name, nsem=4):
        self.ctx = ctx
        self.obj = obj
        self.name = name
        self.sems = [ctx.es.enter_context(ctx.nc.semaphore(f"s_{name}{i}"))
                     for i in range(nsem)]
        self.cur = 0
        self.count = 0
        self.self_waited = 0
        self.last = None
        self.last_inced = True
        self.prev_tok = None
        self.waited = {}

    def rotate(self):
        if self.count > 20000:
            self.prev_tok = self.tok()
            self.cur = (self.cur + 1) % len(self.sems)
            self.count = 0
            self.self_waited = 0
            self.last = None
            self.last_inced = True

    def tok(self):
        if self.last is not None and not self.last_inced:
            self.last.then_inc(self.sems[self.cur], 1)
            self.count += 1
            self.last_inced = True
            if self.count - self.self_waited >= 128:
                self.obj.wait_ge(self.sems[self.cur], self.count - 64)
                self.self_waited = self.count - 64
                k = id(self.sems[self.cur])
                if self.waited.get(k, 0) < self.count - 64:
                    self.waited[k] = self.count - 64
        if self.count == 0:
            return self.prev_tok
        return (self.sems[self.cur], self.count)

    def wait(self, tok):
        if tok is None:
            return
        sem, val = tok
        if val <= 0:
            return
        k = id(sem)
        if self.waited.get(k, 0) >= val:
            return
        self.obj.wait_ge(sem, val)
        self.waited[k] = val

    def issue(self, inst, inc):
        self.last = inst
        self.last_inced = False
        if inc:
            return self.tok()
        return None


class Slot:
    def __init__(self, ctx, name):
        self.sem = ctx.es.enter_context(ctx.nc.semaphore(name))
        self.count = 0
        self.self_waited = 0
        self.free = None
        ctx.all_slots.append(self)


class Ctx:
    def __init__(self, nc, es):
        self.nc = nc
        self.es = es
        self.pe = _Eng(self, nc.tensor, "pe", 6)
        self.dve = _Eng(self, nc.vector, "dve", 4)
        self.act = _Eng(self, nc.scalar, "act", 4)
        self.pool = _Eng(self, nc.gpsimd, "pool", 4)
        self.sp = _Eng(self, nc.sync, "sp", 4)
        self.engs = [self.pe, self.dve, self.act, self.pool, self.sp]
        self.sb = {}
        self.w = {}
        self.all_slots = []
        self.slots = {}
        self._uid = 0

    def finalize(self):
        for e in self.engs:
            t = e.tok()
            self.sp.wait(t)
        for sl in self.all_slots:
            if sl.count > 0:
                self.sp.wait((sl.sem, sl.count))
        for e in self.engs:
            for i, sem in enumerate(e.sems):
                pass

    def nct(self, name, shape, dt):
        self._uid += 1
        return self.nc.sbuf_tensor(f"{name}_{self._uid}", shape, dt)

    def npt(self, name, shape, dt):
        self._uid += 1
        return self.nc.psum_tensor(f"{name}_{self._uid}", shape, dt)

    def slot(self, name):
        if name not in self.slots:
            self.slots[name] = Slot(self, name)
        return self.slots[name]

    def phase(self):
        toks = [e.tok() for e in self.engs]
        for e in self.engs:
            for t in toks:
                if t is not None and (t[0] is not e.sems[e.cur] or True):
                    e.wait(t)
        for e in self.engs:
            e.rotate()

    def _pre(self, eng, waits):
        if waits:
            for t in waits:
                eng.wait(t)

    def dma(self, eng, out, in_, waits=None, slot=None):
        self._pre(eng, waits)
        if slot is not None:
            if slot.free is not None:
                eng.wait(slot.free)
        inst = eng.obj.dma_start(out, in_)
        if slot is not None:
            inst.then_inc(slot.sem, 16)
            slot.count += 16
            eng.last, eng.last_inced = inst, True
            if slot.count - slot.self_waited >= 96:
                eng.obj.wait_ge(slot.sem, slot.count - 32)
                slot.self_waited = slot.count - 32
            return (slot.sem, slot.count)
        inst.then_inc(eng.sems[eng.cur], 16)
        eng.count += 16
        eng.last, eng.last_inced = inst, True
        if eng.count - eng.self_waited >= 128:
            eng.obj.wait_ge(eng.sems[eng.cur], eng.count - 64)
            eng.self_waited = eng.count - 64
            k = id(eng.sems[eng.cur])
            if eng.waited.get(k, 0) < eng.count - 64:
                eng.waited[k] = eng.count - 64
        return (eng.sems[eng.cur], eng.count)

    def mm(self, out, lhsT, rhs, start, stop, waits=None, inc=False):
        self._pre(self.pe, waits)
        return self.pe.issue(
            self.nc.tensor.matmul(out, lhsT, rhs, start=start, stop=stop), inc)

    def xp(self, out, in_, ident, waits=None, inc=False):
        self._pre(self.pe, waits)
        return self.pe.issue(self.nc.tensor.transpose(out, in_, ident), inc)

    def tt(self, out, a, b, op, waits=None, inc=True, eng=None):
        e = eng or self.dve
        self._pre(e, waits)
        return e.issue(e.obj.tensor_tensor(out, a, b, op), inc)

    def ts(self, out, a, s1, s2, op0, op1, waits=None, inc=True, eng=None,
           accum_out=None):
        e = eng or self.dve
        self._pre(e, waits)
        return e.issue(e.obj.tensor_scalar(out, a, s1, s2, op0, op1,
                                           accum_out=accum_out), inc)

    def stt(self, out, in0, scalar, in1, op0, op1, waits=None, inc=True):
        self._pre(self.dve, waits)
        return self.dve.issue(
            self.nc.vector.scalar_tensor_tensor(out, in0, scalar, in1, op0, op1),
            inc)

    def cp(self, out, in_, waits=None, inc=True, eng=None):
        e = eng or self.dve
        self._pre(e, waits)
        if e is self.act:
            return e.issue(self.nc.scalar.copy(out, in_), inc)
        return e.issue(e.obj.tensor_copy(out, in_), inc)

    def red(self, out, in_, op, negate=False, waits=None, inc=True):
        self._pre(self.dve, waits)
        return self.dve.issue(
            self.nc.vector.tensor_reduce(out, in_, AX.X, op, negate=negate), inc)

    def recip(self, out, in_, waits=None, inc=True):
        self._pre(self.dve, waits)
        return self.dve.issue(self.nc.vector.reciprocal(out, in_), inc)

    def a(self, out, in_, func, bias=None, scale=None, accum_out=None,
          waits=None, inc=True):
        self._pre(self.act, waits)
        kw = {}
        if bias is not None:
            kw["bias"] = bias
        if scale is not None:
            kw["scale"] = scale
        if accum_out is not None:
            kw["accum_out"] = accum_out
        return self.act.issue(self.nc.scalar.activation(out, in_, func, **kw), inc)



_STREAMED = {"sw1", "sw2", "rw1", "rw2", "W1T", "W2T",
             "a0Wq", "a0Wk", "a0Wv", "a0Wo", "a1WoT"}


def _load_consts(ctx, names):
    toks = []
    for nm in names:
        if nm in _STREAMED:
            continue
        shape, npdt = _WSPECS[nm]
        sb_dt = BF16 if nm in _SB_BF16 else _NP2BIR[npdt]
        t = ctx.es.enter_context(ctx.nc.sbuf_tensor("c_" + nm, list(shape), sb_dt))
        ctx.sb[nm] = t
        q = ctx.pool if sb_dt != _NP2BIR[npdt] else ctx.sp
        sl = tuple(slice(None) for _ in shape)
        toks.append(ctx.dma(q, t[sl], ctx.w[nm]))
    return toks


def emit_load_x(ctx, x_ap, psx):
    x, idf = ctx.sb["x"], ctx.sb["ident_f"]
    stage_cm = ctx.nct("stage_l", [128, 4, 1024], F32)
    stage = stage_cm.__enter__()
    stage_free = [None] * 4
    ev = [None, None]
    for i in range(NT):
        s = i % 4
        ld = ctx.dma(ctx.sp if i % 2 == 0 else ctx.act,
                     stage[:, s, :], x_ap[i * 128:(i + 1) * 128, :],
                     waits=[stage_free[s]])
        for c in range(C):
            sl = c % 2
            ctx.xp(psx[:, sl, 0:128], stage[:, s, c * 128:(c + 1) * 128],
                   idf[:, :], waits=[ld, ev[sl]], inc=True)
            pt = ctx.pe.tok()
            ev[sl] = ctx.cp(x[:, c, i * 128:(i + 1) * 128], psx[:, sl, 0:128],
                            waits=[pt])
        stage_free[s] = ctx.pe.tok()
    stage_cm.__exit__(None, None, None)


def emit_store_x(ctx, xo_ap, psx):
    x, idf = ctx.sb["x"], ctx.sb["ident_f"]
    stage_cm = ctx.nct("stage_s", [128, 4, 1024], F32)
    stage = stage_cm.__enter__()
    dma_done = [None] * 4
    prev_ev = [None, None]
    toks = []
    for i in range(NT):
        s = i % 4
        ev = None
        for c in range(C):
            sl = c % 2
            ctx.xp(psx[:, sl, 0:128], x[:, c, i * 128:(i + 1) * 128], idf[:, :],
                   waits=[prev_ev[sl], dma_done[s]], inc=True)
            pt = ctx.pe.tok()
            ev = ctx.cp(stage[:, s, c * 128:(c + 1) * 128], psx[:, sl, 0:128],
                        waits=[pt])
            prev_ev[sl] = ev
        dma_done[s] = ctx.dma(ctx.sp if i % 2 == 0 else ctx.act,
                              xo_ap[i * 128:(i + 1) * 128, :],
                              stage[:, s, :], waits=[ev])
        toks.append(dma_done[s])
    stage_cm.__exit__(None, None, None)
    return toks


def emit_ln(ctx):
    nc = ctx.nc
    x, xn = ctx.sb["x"], ctx.sb["xn"]
    onf, onb = ctx.sb["ones_f"], ctx.sb["ones_b"]
    with (
        ctx.nct("ln_xq", [128, C, T], BF16) as xq,
        ctx.nct("ln_rows", [1, 3, T], F32) as rows,
        ctx.nct("ln_tmp", [128, 2, 512], F32) as tmp,
        ctx.npt("ln_pr", [128, 2, 512], F32) as pr,
        ctx.npt("ln_pb", [128, 2, 512], F32) as pb,
    ):
        sq_t = [ctx.a(xq[:, cc_, :], x[:, cc_, :], AF.Square) for cc_ in range(C)]
        ev_s = None
        ev_q = None
        for lt in range(4):
            lsl = slice(lt * 512, (lt + 1) * 512)
            for cc_ in range(C):
                ctx.mm(pr[0:1, 0, :], onf[:, 0:1], x[:, cc_, lsl],
                       start=(cc_ == 0), stop=(cc_ == C - 1),
                       waits=[ev_s] if cc_ == 0 else None, inc=(cc_ == C - 1))
            pt_s = ctx.pe.tok()
            for cc_ in range(C):
                ctx.mm(pr[0:1, 1, :], onb[:, 0:1], xq[:, cc_, lsl],
                       start=(cc_ == 0), stop=(cc_ == C - 1),
                       waits=([ev_q] if ev_q else []) + [sq_t[cc_]],
                       inc=(cc_ == C - 1))
            pt_q = ctx.pe.tok()
            ev_s = ctx.cp(rows[0:1, 0, lsl], pr[0:1, 0, :], waits=[pt_s])
            ev_q = ctx.cp(rows[0:1, 1, lsl], pr[0:1, 1, :], waits=[pt_q])
        r0, r1, r2 = rows[0:1, 0, :], rows[0:1, 1, :], rows[0:1, 2, :]
        ctx.ts(r0, r0, 1.0 / D, None, OP.mult, OP.bypass, waits=[ev_s, ev_q])
        ctx.nc.vector.drain()
        ctx.tt(r2, r0, r0, OP.mult)
        ctx.nc.vector.drain()
        t_var = ctx.stt(r1, r1, 1.0 / D, r2, OP.mult, OP.subtract)
        t_ln = ctx.a(r2, r1, AF.Ln, bias=ctx.sb["epsb"][0:1, 0:1],
                     waits=[t_var])
        ctx.nc.scalar.drain()
        t_rs = ctx.a(r1, r2, AF.Exp, scale=-0.5)
        t_mu = ctx.dve.tok()
        tmp_free = [None, None]
        bc_done = None
        last = None
        for lt in range(4):
            lsl = slice(lt * 512, (lt + 1) * 512)
            ctx.mm(pb[:, 0, :], onf[0:1, :], r0[:, lsl], start=True, stop=True,
                   waits=[t_mu, t_rs, bc_done])
            ctx.mm(pb[:, 1, :], onf[0:1, :], r1[:, lsl], start=True, stop=True,
                   inc=True)
            pt = ctx.pe.tok()
            for cc_ in range(C):
                s = cc_ % 2
                ctx.tt(tmp[:, s, :], x[:, cc_, lsl], pb[:, 0, :], OP.subtract,
                       waits=[pt, tmp_free[s]])
                tmp_free[s] = ctx.tt(xn[:, cc_, lsl], tmp[:, s, :], pb[:, 1, :],
                                     OP.mult)
                last = tmp_free[s]
            bc_done = last
        ctx.phase()


def emit_l0ffn(ctx):
    nc = ctx.nc
    xn_cm = ctx.nct("xn0", [128, C, T], BF16)
    ctx.sb["xn"] = xn_cm.__enter__()
    cw_cm = ctx.nct("cw_em", [8, T], BF16)
    ctx.sb["cw_em"] = cw_cm.__enter__()
    emit_ln(ctx)
    x, xn = ctx.sb["x"], ctx.sb["xn"]
    idf, idb = ctx.sb["ident_f"], ctx.sb["ident_b"]
    sel8 = ctx.sb["sel8"]
    gate_w, gate_b = ctx.sb["gate_w"], ctx.sb["gate_b"]
    sb1, sb2 = ctx.sb["sb1"], ctx.sb["sb2"]
    rb1, rb2 = ctx.sb["rb1"], ctx.sb["rb2"]
    cw_em = ctx.sb["cw_em"]
    rw1_ap, rw2_ap = ctx.w["rw1"], ctx.w["rw2"]
    sw1_ap, sw2_ap = ctx.w["sw1"], ctx.w["sw2"]

    with (
        ctx.nct("g_G", [8, T], F32) as G,
        ctx.nct("g_tm", [128, NT, 8], F32) as Gtm,
        ctx.nct("g_E", [128, NT, 8], F32) as Eo,
        ctx.nct("g_pr", [128, NT, 8], F32) as probs,
        ctx.nct("g_m1", [128, NT, 8], F32) as m1,
        ctx.nct("g_pm", [128, NT, 8], F32) as pm,
        ctx.nct("g_mk", [128, NT, 8], F32) as msk,
        ctx.nct("g_cw", [128, NT, 8], BF16) as cwtm,
        ctx.nct("g_r1", [128, NT, 1], F32) as nm_,
        ctx.nct("g_r2", [128, NT, 1], F32) as ss,
        ctx.nct("g_r3", [128, NT, 1], F32) as rr,
        ctx.nct("g_r4", [128, NT, 1], F32) as p1,
        ctx.nct("g_r5", [128, NT, 1], F32) as p2,
        ctx.npt("g_ps", [128, 512], F32) as psg,
        ctx.npt("g_pt", [128, 2, 512], F32) as psgt,
        ctx.npt("g_ct0", [128, 128], BF16) as psct0,
        ctx.npt("g_ct1", [128, 128], BF16) as psct1,
    ):
        ev = None
        for lt in range(4):
            lsl = slice(lt * 512, (lt + 1) * 512)
            for k in range(C):
                ctx.mm(psg[0:8, :], gate_w[:, k, :], xn[:, k, lsl],
                       start=(k == 0), stop=(k == C - 1),
                       waits=[ev] if k == 0 else None, inc=(k == C - 1))
            pt = ctx.pe.tok()
            ev = ctx.ts(G[0:8, lsl], psg[0:8, :], gate_b[:, :], None,
                        OP.add, OP.bypass, waits=[pt])
        ev2 = [None, None]
        for i in range(NT):
            s = i % 2
            ctx.xp(psgt[:, s, 0:8], G[0:8, i * 128:(i + 1) * 128], idf[0:8, 0:8],
                   waits=[ev, ev2[s]], inc=True)
            pt = ctx.pe.tok()
            ev2[s] = ctx.cp(Gtm[:, i, :], psgt[:, s, 0:8], waits=[pt])
        gt = ctx.dve.tok()
        for i in range(NT):
            ctx.red(nm_[:, i, :], Gtm[:, i, :], OP.max, negate=True,
                    waits=[gt] if i == 0 else None)
        dt_ = ctx.dve.tok()
        tE = None
        for i in range(NT):
            tE = ctx.a(Eo[:, i, :], Gtm[:, i, :], AF.Exp, bias=nm_[:, i, :],
                       waits=[dt_] if i == 0 else None)
        for i in range(NT):
            ctx.red(ss[:, i, :], Eo[:, i, :], OP.add,
                    waits=[tE] if i == 0 else None)
        for i in range(NT):
            ctx.recip(rr[:, i, :], ss[:, i, :])
        for i in range(NT):
            ctx.ts(probs[:, i, :], Eo[:, i, :], rr[:, i, :], None, OP.mult,
                   OP.bypass)
        for i in range(NT):
            ctx.red(p1[:, i, :], probs[:, i, :], OP.max)
        for i in range(NT):
            ctx.ts(m1[:, i, :], probs[:, i, :], p1[:, i, :], None, OP.is_ge,
                   OP.bypass)
        for i in range(NT):
            ctx.stt(pm[:, i, :], m1[:, i, :], -1e30, probs[:, i, :], OP.mult,
                    OP.add)
        for i in range(NT):
            ctx.red(p2[:, i, :], pm[:, i, :], OP.max)
        for i in range(NT):
            ctx.ts(msk[:, i, :], probs[:, i, :], p2[:, i, :], None, OP.is_ge,
                   OP.bypass)
        tcw = None
        for i in range(NT):
            tcw = ctx.stt(cwtm[:, i, :], msk[:, i, :], 0.5, probs[:, i, :],
                          OP.mult, OP.mult)
        ev3 = [None, None]
        psct = [psct0, psct1]
        for i in range(NT):
            s = i % 2
            ctx.xp(psct[s][0:8, :], cwtm[:, i, :], idb[:, :],
                   waits=[tcw, ev3[s]], inc=True)
            pt = ctx.pe.tok()
            ev3[s] = ctx.cp(cw_em[0:8, i * 128:(i + 1) * 128], psct[s][0:8, :],
                            waits=[pt])
        ctx.phase()
    if DEBUG_STOP == "gate":
        return

    with (
        ctx.nct("f_w1", [128, 8, 8, 2, 128], BF16) as w1b,
        ctx.nct("f_w2", [128, 8, 1024], BF16) as w2b,
        ctx.nct("f_sa", [128, 3, LTF], BF16) as sa,
        ctx.nct("f_sw", [128, 2, 8, LTF], BF16) as swg,
        ctx.nct("f_cwb", [128, T], BF16) as cwb,
        ctx.nct("f_po", [128, 8, LTF], F32) as posb0,
        ctx.nct("f_s2h", [128, 8], F32) as sb2h,
        ctx.npt("f_p1", [128, 3, 2, 256], F32) as pmm,
        ctx.npt("f_po", [128, 8, 256], F32) as po,
        ctx.npt("f_pb", [128, 512], F32) as psbc,
    ):
        po_flat = po.rearrange("p a b -> p (a b)")
        w1sl = [ctx.slot(f"w1s{j}") for j in range(8)]
        w2sl = [ctx.slot(f"w2s{j}") for j in range(8)]
        w1_tok = [None] * 8
        w2_tok = [None] * 8

        def w1_src(e, j, b):
            src = sw1_ap if e == 0 else rw1_ap[e - 1]
            r = src.rearrange("(k p) (b m) -> p k b m", p=128, b=16)
            return r[:, :, j + 8 * b, :]

        def w2_src(e, j):
            src = sw2_ap if e == 0 else rw2_ap[e - 1]
            return src[j * 128:(j + 1) * 128, :]

        def fetch_w1(e, j):
            if e >= 9:
                return
            ctx.dma(ctx.pool, w1b[:, j, :, 0, :], w1_src(e, j, 0), slot=w1sl[j])
            w1_tok[j] = ctx.dma(ctx.pool, w1b[:, j, :, 1, :], w1_src(e, j, 1),
                                slot=w1sl[j])

        def fetch_w2(e, j):
            if e >= 9:
                return
            w2_tok[j] = ctx.dma(ctx.pool, w2b[:, j, :], w2_src(e, j),
                                slot=w2sl[j])

        for j in range(8):
            fetch_w1(0, j)
        for j in range(8):
            fetch_w2(0, j)

        t_s2h = ctx.ts(sb2h[:, :], sb2[:, :], 0.5, None, OP.mult, OP.bypass)

        mm1_cons = [None, None, None]
        sa_cons = [None, None, None]
        swg_cons = [None, None]
        evac_done = None
        posb_rd = None
        w1_tok = w1_tok
        w2_tok = w2_tok
        cwb_ready = None
        n_exp = 9
        if DEBUG_STOP.startswith("exp"):
            n_exp = int(DEBUG_STOP[3:]) + 1
        for e in range(n_exp):
            if e >= 1:
                ev4 = None
                for lt4 in range(4):
                    ctx.mm(psbc[:, :], sel8[:, (e - 1) * 128:e * 128],
                           cw_em[0:8, lt4 * 512:(lt4 + 1) * 512],
                           start=True, stop=True, waits=[ev4], inc=True)
                    pt = ctx.pe.tok()
                    ev4 = ctx.cp(cwb[:, lt4 * 512:(lt4 + 1) * 512], psbc[:, :],
                                 waits=[pt])
                cwb_ready = ev4
            for lt in range(NLTF):
                lsl = slice(lt * LTF, (lt + 1) * LTF)
                lr = lt % 2
                t_swg = None
                for j in range(8):
                    sj = j % 3
                    for k in range(C):
                        ctx.mm(pmm[:, sj, 0, :], w1b[:, j, k, 0, :], xn[:, k, lsl],
                               start=(k == 0), stop=(k == C - 1),
                               waits=[w1_tok[j], mm1_cons[sj]] if k == 0 else None,
                               inc=(k == C - 1))
                    ta = ctx.pe.tok()
                    for k in range(C):
                        ctx.mm(pmm[:, sj, 1, :], w1b[:, j, k, 1, :], xn[:, k, lsl],
                               start=(k == 0), stop=(k == C - 1),
                               inc=(k == C - 1))
                    tb = ctx.pe.tok()
                    b_a = sb1[:, j:j + 1] if e == 0 else rb1[:, e - 1, j:j + 1]
                    b_b = (sb1[:, j + 8:j + 9] if e == 0
                           else rb1[:, e - 1, j + 8:j + 9])
                    t_sa = ctx.a(sa[:, sj, :], pmm[:, sj, 0, :], AF.Silu,
                                 bias=b_a, waits=[tb, sa_cons[sj]])
                    t_swg = ctx.stt(swg[:, lr, j, :], pmm[:, sj, 1, :], b_b,
                                    sa[:, sj, :], OP.add, OP.mult,
                                    waits=[tb, t_sa, swg_cons[lr]])
                    sa_cons[sj] = t_swg
                    mm1_cons[sj] = t_swg
                    if lt == NLTF - 1:
                        w1sl[j].free = tb
                        fetch_w1(e + 1, j)
                for ph in range(2):
                    for j in range(8):
                        for bi in range(4):
                            dch = 2 * bi + ph
                            wl = []
                            if ph == 0 and j == 0 and bi == 0:
                                wl = [t_swg, evac_done]
                            if ph == 0 and bi == 0:
                                wl.append(w2_tok[j])
                            ctx.mm(po[:, dch, :],
                                   w2b[:, j, dch * 128:(dch + 1) * 128],
                                   swg[:, lr, j, :], start=(j == 0),
                                   stop=(j == 7), waits=wl or None,
                                   inc=(ph == 1 and j == 7 and bi == 3))
                mm2_last = ctx.pe.tok()
                swg_cons[lr] = mm2_last
                if lt == NLTF - 1:
                    for j in range(8):
                        w2sl[j].free = mm2_last
                        fetch_w2(e + 1, j)
                xv = x[:, :, lsl]
                if e == 0:
                    evac_done = ctx.stt(xv, po_flat, 0.5, xv, OP.mult, OP.add,
                                        waits=[mm2_last])
                else:
                    cb3 = cwb[:, lsl].unsqueeze(1).broadcast_to([128, 8, LTF])
                    t_p1 = ctx.tt(posb0[:, :, :], po[:, :, :], cb3, OP.mult,
                                  waits=[mm2_last, cwb_ready, posb_rd])
                    evac_done = t_p1
                    posb_rd = ctx.tt(xv, xv, posb0[:, :, :], OP.add)
        if DEBUG_STOP == "experts" or DEBUG_STOP.startswith("exp"):
            ctx.phase()
            return
        ev5 = None
        for mch in range(8):
            for lt4 in range(4):
                ctx.mm(psbc[:, :], rb2[0:8, mch * 128:(mch + 1) * 128],
                       cw_em[0:8, lt4 * 512:(lt4 + 1) * 512], start=True,
                       stop=True, waits=[evac_done, ev5], inc=True)
                pt = ctx.pe.tok()
                ev5 = ctx.tt(x[:, mch, lt4 * 512:(lt4 + 1) * 512],
                             x[:, mch, lt4 * 512:(lt4 + 1) * 512],
                             psbc[:, :], OP.add, waits=[pt])
        for cc_ in range(C):
            ctx.ts(x[:, cc_, :], x[:, cc_, :], sb2h[:, cc_:cc_ + 1], None,
                   OP.add, OP.bypass, waits=[t_s2h])
        ctx.phase()
    cw_cm.__exit__(None, None, None)
    xn_cm.__exit__(None, None, None)


def emit_l0attn(ctx):
    nc = ctx.nc
    xn_cm = ctx.nct("xn0a", [128, C, T], BF16)
    ctx.sb["xn"] = xn_cm.__enter__()
    emit_ln(ctx)
    x, xn = ctx.sb["x"], ctx.sb["xn"]
    onf, onb = ctx.sb["ones_f"], ctx.sb["ones_b"]
    q_scr, k_scr, v_scr, y_scr = (ctx.w["q_scr"], ctx.w["k_scr"],
                                  ctx.w["v_scr"], ctx.w["y_scr"])

    with (
        ctx.nct("a_w", [128, 8, 1024], BF16) as wbuf,
        ctx.nct("a_st", [128, 4, 2048], BF16) as stg,
        ctx.npt("a_pp", [128, 2, 512], F32) as pp,
    ):
        w_free = None
        for ti, (wnm, bnm, scr, scale) in enumerate((
                ("a0Wq", "a0bq", q_scr, 0.125),
                ("a0Wk", "a0bk", k_scr, None),
                ("a0Wv", "a0bv", v_scr, None))):
            wt = ctx.dma(ctx.pool, wbuf[:, :, :],
                         ctx.w[wnm].rearrange("(k p) m -> p k m", p=128),
                         waits=[w_free])
            bias = ctx.sb[bnm]
            dma_done = [None] * 4
            ev = [None, None]
            for m in range(8):
                sm = m % 4
                for lt in range(4):
                    sl_ = slice(lt * 512, (lt + 1) * 512)
                    sb_ = pp[:, lt % 2, :]
                    for k in range(C):
                        ctx.mm(sb_, wbuf[:, k, m * 128:(m + 1) * 128],
                               xn[:, k, sl_], start=(k == 0), stop=(k == C - 1),
                               waits=([wt, ev[lt % 2], dma_done[sm]]
                                      if k == 0 else None),
                               inc=(k == C - 1))
                    pt = ctx.pe.tok()
                    if scale is None:
                        ev[lt % 2] = ctx.ts(stg[:, sm, sl_], sb_,
                                            bias[:, m:m + 1], None,
                                            OP.add, OP.bypass, waits=[pt])
                    else:
                        ev[lt % 2] = ctx.ts(stg[:, sm, sl_], sb_,
                                            bias[:, m:m + 1], scale,
                                            OP.add, OP.mult, waits=[pt])
                dma_done[sm] = ctx.dma(ctx.sp if m % 2 == 0 else ctx.act,
                                       scr[m * 128:(m + 1) * 128, :],
                                       stg[:, sm, :], waits=[ev[0], ev[1]])
            w_free = ctx.pe.tok()
        ctx.phase()

    sel16c, sel16r = ctx.sb["sel16c"], ctx.sb["sel16r"]
    with (
        ctx.nct("a_qd", [64, 16, 512], BF16) as qdm,
        ctx.nct("a_kd", [64, 16, 512], BF16) as kdm,
        ctx.nct("a_ve", [16, 2, 64, 128], BF16) as vem,
        ctx.nct("a_E", [16, 16, 512], BF16) as Eb,
        ctx.nct("a_P", [16, 2, 32, 16], BF16) as Pb,
        ctx.nct("a_rl", [16, 512], F32) as rrln,
        ctx.nct("a_rr", [16, 512], BF16) as rr16,
        ctx.nct("a_oh", [64, 2, 16, 128], BF16) as ohd,
        ctx.npt("a_ps", [128, 2, 512], F32) as ps_s,
        ctx.npt("a_po", [128, 2, 512], F32) as ps_o2,
        ctx.npt("a_pr", [128, 2, 512], F32) as ps_r2,
    ):
        q_r = q_scr.rearrange("(h d) l -> d h l", h=16)
        k_r = k_scr.rearrange("(h d) l -> d h l", h=16)
        v_r = v_scr.rearrange("(e d) l -> e d l", e=16)
        qk_done = None
        v_done = [None, None]
        exp_t = [None, None]
        ptt = [None, None]
        tt_t = [None, None]
        oc_t = [None, None]
        oh_dma = [None, None]
        e_free = None
        trl_prev = None
        bc_last = None
        for lt in range(4):
            lsl = slice(lt * 512, (lt + 1) * 512)
            tq = ctx.dma(ctx.sp, qdm[:, :, :], q_r[:, :, lsl], waits=[qk_done])
            tk = ctx.dma(ctx.sp, kdm[:, :, :], k_r[:, :, lsl])

            def emit_summ(b):
                ctx.mm(ps_r2[0:16, 0, :], sel16c[:, b * 16:(b + 1) * 16],
                       Eb[:, b, :], start=(b == 0), stop=(b == 15),
                       waits=([exp_t[b % 2]]
                              + ([e_free, trl_prev] if b == 0 else [])),
                       inc=(b == 15))

            for b in range(16):
                sb_ = b % 2
                for t in range(32):
                    l = b * 32 + t
                    ctx.mm(ps_s[0:16, sb_, t * 16:(t + 1) * 16],
                           kdm[:, :, l], qdm[:, :, l], start=True, stop=True,
                           waits=([tq, tk, exp_t[sb_]] if t == 0 else None),
                           inc=(t == 31))
                t_s1 = ctx.pe.tok()
                exp_t[sb_] = ctx.a(Eb[:, b, :], ps_s[0:16, sb_, :], AF.Exp,
                                   waits=[t_s1, e_free])
                if b >= 1:
                    emit_summ(b - 1)
            emit_summ(15)
            qk_done = ctx.pe.tok()
            tl = ctx.a(rrln[:, :], ps_r2[0:16, 0, :], AF.Ln, waits=[qk_done])
            trl_prev = tl
            tr = ctx.a(rr16[:, :], rrln[:, :], AF.Exp, scale=-1.0,
                       waits=[tl, bc_last])

            def emit_bc(b):
                bank = b % 2
                ctx.mm(ps_r2[0:16, 1, bank * 256:(bank + 1) * 256]
                       if False else ps_r2[0:16, bank, :],
                       sel16r[:, b * 16:(b + 1) * 16], rr16[:, :],
                       start=True, stop=True,
                       waits=[tr, tt_t[bank]]
                       + ([trl_prev] if bank == 0 else []), inc=True)
                return ctx.pe.tok()

            def emit_tt(b, t_bc):
                bank = b % 2
                pv = ps_r2[0:16, bank, :].rearrange("p (l h) -> p l h", h=16)
                tt_t[bank] = ctx.tt(
                    Pb[:, b % 2, :, :],
                    Eb[:, b, :].rearrange("p (l h) -> p l h", h=16),
                    pv, OP.mult, waits=[t_bc, ptt[b % 2]])

            t_bc = emit_bc(0)
            emit_tt(0, t_bc)
            for b in range(16):
                sb_ = b % 2
                vt = b // 4
                vs = vt % 2
                if b % 4 == 0:
                    gl = slice(lt * 512 + vt * 128, lt * 512 + (vt + 1) * 128)
                    v_tok = ctx.dma(ctx.sp, vem[:, vs, :, :], v_r[:, :, gl],
                                    waits=[v_done[vs]])
                    v_done[vs] = v_tok
                if b + 1 < 16:
                    t_bc = emit_bc(b + 1)
                    emit_tt(b + 1, t_bc)
                for t in range(32):
                    l = b * 32 + t
                    tl_ = l % 128
                    ctx.mm(ps_o2[0:64, sb_, t * 16:(t + 1) * 16],
                           vem[:, vs, :, tl_], Pb[:, sb_, t, :],
                           start=True, stop=True,
                           waits=([tt_t[sb_], v_tok, oc_t[sb_]]
                                  if t == 0 else None),
                           inc=(t == 31))
                ptt[sb_] = ctx.pe.tok()
                if b % 4 == 3:
                    v_done[vs] = ptt[sb_]
                os_ = vt % 2
                oc_t[sb_] = ctx.cp(
                    ohd[:, os_, :, (b % 4) * 32:(b % 4 + 1) * 32],
                    ps_o2[0:64, sb_, :].rearrange("p (l h) -> p h l", h=16),
                    waits=[ptt[sb_], oh_dma[os_]])
                if b % 4 == 3:
                    oh_dma[os_] = ctx.dma(
                        ctx.sp, y_scr[:, :, lt * 512 + vt * 128:
                                      lt * 512 + (vt + 1) * 128],
                        ohd[:, os_, :, :], waits=[oc_t[sb_]])
            bc_last = t_bc
            e_free = tt_t[1]
        ctx.phase()

    with (
        ctx.nct("a_wo", [128, 8, 1024], BF16) as wo,
        ctx.npt("a_pp2", [128, 2, 512], F32) as pp2,
    ):
        wt = ctx.dma(ctx.pool, wo[:, :, :],
                     ctx.w["a0Wo"].rearrange("(k p) m -> p k m", p=128))
        for h in range(16):
            ctx.dma(ctx.sp, xn[(h % 2) * 64:(h % 2) * 64 + 64, h // 2, :],
                    y_scr[:, h, :])
        ytok = (ctx.sp.sems[ctx.sp.cur], ctx.sp.count)
        bo = ctx.sb["a0bo"]
        ev = [None, None]
        for m in range(8):
            for lt in range(4):
                sl_ = slice(lt * 512, (lt + 1) * 512)
                sb_ = pp2[:, lt % 2, :]
                for k in range(C):
                    ctx.mm(sb_, wo[:, k, m * 128:(m + 1) * 128], xn[:, k, sl_],
                           start=(k == 0), stop=(k == C - 1),
                           waits=[wt, ytok, ev[lt % 2]] if k == 0 else None,
                           inc=(k == C - 1))
                pt = ctx.pe.tok()
                ev[lt % 2] = ctx.stt(x[:, m, sl_], sb_, bo[:, m:m + 1],
                                     x[:, m, sl_], OP.add, OP.add, waits=[pt])
        ctx.phase()
    xn_cm.__exit__(None, None, None)


def emit_l1ffn(ctx):
    nc = ctx.nc
    x = ctx.sb["x"]
    onf, onb = ctx.sb["ones_f"], ctx.sb["ones_b"]
    idf, idb = ctx.sb["ident_f"], ctx.sb["ident_b"]
    sel8 = ctx.sb["sel8"]
    centT = ctx.sb["centT"]
    gamma, beta = ctx.sb["gamma"], ctx.sb["beta"]
    b1c, b2c = ctx.sb["b1"], ctx.sb["b2"]
    x_scr, xn_scr = ctx.w["x_scr"], ctx.w["xn_scr"]
    W1T_ap, W2T_ap = ctx.w["W1T"], ctx.w["W2T"]
    lam_cm = ctx.nct("lam_em", [8, T], BF16)
    lam_em = lam_cm.__enter__()

    G_cm = ctx.nct("G_l1", [8, T], F32)
    G = G_cm.__enter__()
    with (
        ctx.nct("l1_xq", [128, C, T], BF16) as xq,
        ctx.nct("l1_rows", [1, 3, T], F32) as rows,
        ctx.nct("l1_xst", [128, 2, C, 512], BF16) as xst,
        ctx.nct("l1_tmp", [128, 2, 512], F32) as tmp,
        ctx.npt("l1_pr", [128, 2, 512], F32) as pr,
        ctx.npt("l1_pb", [128, 2, 512], F32) as pb,
        ctx.npt("l1_pg", [128, 512], F32) as pg,
    ):
        sq_t = [ctx.a(xq[:, cc_, :], x[:, cc_, :], AF.Square) for cc_ in range(C)]
        ev_s = None
        ev_q = None
        for lt in range(4):
            lsl = slice(lt * 512, (lt + 1) * 512)
            for cc_ in range(C):
                ctx.mm(pr[0:1, 0, :], onf[:, 0:1], x[:, cc_, lsl],
                       start=(cc_ == 0), stop=(cc_ == C - 1),
                       waits=[ev_s] if cc_ == 0 else None, inc=(cc_ == C - 1))
            pt_s = ctx.pe.tok()
            for cc_ in range(C):
                ctx.mm(pr[0:1, 1, :], onb[:, 0:1], xq[:, cc_, lsl],
                       start=(cc_ == 0), stop=(cc_ == C - 1),
                       waits=([ev_q] if ev_q else []) + [sq_t[cc_]],
                       inc=(cc_ == C - 1))
            pt_q = ctx.pe.tok()
            ev_s = ctx.cp(rows[0:1, 0, lsl], pr[0:1, 0, :], waits=[pt_s])
            ev_q = ctx.cp(rows[0:1, 1, lsl], pr[0:1, 1, :], waits=[pt_q])
        r0, r1, r2 = rows[0:1, 0, :], rows[0:1, 1, :], rows[0:1, 2, :]
        ctx.ts(r0, r0, 1.0 / D, None, OP.mult, OP.bypass, waits=[ev_s, ev_q])
        ctx.nc.vector.drain()
        ctx.tt(r2, r0, r0, OP.mult)
        ctx.nc.vector.drain()
        t_var = ctx.stt(r1, r1, 1.0 / D, r2, OP.mult, OP.subtract)
        t_ln = ctx.a(r2, r1, AF.Ln, bias=ctx.sb["epsb"][0:1, 0:1], waits=[t_var])
        ctx.nc.scalar.drain()
        t_rs = ctx.a(r1, r2, AF.Exp, scale=-0.5)
        t_mu = ctx.dve.tok()
        tmp_free = [None, None]
        st_dma = [None, None]
        bc_done = None
        g_ev = None
        for lt in range(4):
            lsl = slice(lt * 512, (lt + 1) * 512)
            sl2 = lt % 2
            ctx.mm(pb[:, 0, :], onf[0:1, :], r0[:, lsl], start=True, stop=True,
                   waits=[t_mu, t_rs, bc_done])
            ctx.mm(pb[:, 1, :], onf[0:1, :], r1[:, lsl], start=True, stop=True,
                   inc=True)
            pt = ctx.pe.tok()
            last = None
            for cc_ in range(C):
                s = cc_ % 2
                ctx.tt(tmp[:, s, :], x[:, cc_, lsl], pb[:, 0, :], OP.subtract,
                       waits=[pt, tmp_free[s], st_dma[sl2]])
                tmp_free[s] = ctx.tt(xst[:, sl2, cc_, :], tmp[:, s, :],
                                     pb[:, 1, :], OP.mult)
                last = tmp_free[s]
            bc_done = last
            st_dma[sl2] = ctx.dma(ctx.sp, xn_scr[:, :, lsl], xst[:, sl2, :, :],
                                  waits=[last])
            for cc_ in range(C):
                ctx.mm(pg[0:8, :], centT[:, cc_, :], xst[:, sl2, cc_, :],
                       start=(cc_ == 0), stop=(cc_ == C - 1),
                       waits=[last, g_ev] if cc_ == 0 else None,
                       inc=(cc_ == C - 1))
            pt2 = ctx.pe.tok()
            g_ev = ctx.cp(G[0:8, lsl], pg[0:8, :], waits=[pt2])
            if lt == 3:
                pass
        sp_toks = []
        for lt in range(4):
            lsl = slice(lt * 512, (lt + 1) * 512)
            sp_toks.append(ctx.dma(ctx.sp, x_scr[:, :, lsl], x[:, :, lsl],
                                   waits=[bc_done]))
        ctx.phase()

    with (
        ctx.nct("g2_tm", [128, NT, 8], F32) as Gtm,
        ctx.nct("g2_sm", [128, NT, 1], F32) as smax,
        ctx.nct("g2_al", [128, NT, 1], F32) as alph,
        ctx.nct("g2_m", [128, NT, 8], F32) as m1,
        ctx.nct("g2_lm", [128, NT, 8], BF16) as lamtm,
        ctx.npt("g2_pt", [128, 2, 512], F32) as psgt,
        ctx.npt("g2_c0", [128, 128], BF16) as psct0,
        ctx.npt("g2_c1", [128, 128], BF16) as psct1,
    ):
        ev2 = [None, None]
        for i in range(NT):
            s = i % 2
            ctx.xp(psgt[:, s, 0:8], G[0:8, i * 128:(i + 1) * 128], idf[0:8, 0:8],
                   waits=[ev2[s]], inc=True)
            pt = ctx.pe.tok()
            ev2[s] = ctx.cp(Gtm[:, i, :], psgt[:, s, 0:8], waits=[pt])
        gt = ctx.dve.tok()
        for i in range(NT):
            ctx.red(smax[:, i, :], Gtm[:, i, :], OP.max,
                    waits=[gt] if i == 0 else None)
        dt_ = ctx.dve.tok()
        tA = None
        for i in range(NT):
            tA = ctx.a(alph[:, i, :], smax[:, i, :], AF.Sigmoid,
                       waits=[dt_] if i == 0 else None)
        for i in range(NT):
            ctx.ts(m1[:, i, :], Gtm[:, i, :], smax[:, i, :], None, OP.is_ge,
                   OP.bypass, waits=[tA] if i == 0 else None)
        tlam = None
        for i in range(NT):
            tlam = ctx.ts(lamtm[:, i, :], m1[:, i, :], alph[:, i, :], 0.5,
                          OP.mult, OP.mult)
        ev3 = [None, None]
        psct = [psct0, psct1]
        for i in range(NT):
            s = i % 2
            ctx.xp(psct[s][0:8, :], lamtm[:, i, :], idb[:, :],
                   waits=[tlam, ev3[s]], inc=True)
            pt = ctx.pe.tok()
            ev3[s] = ctx.cp(lam_em[0:8, i * 128:(i + 1) * 128], psct[s][0:8, :],
                            waits=[pt])
        ctx.phase()
    G_cm.__exit__(None, None, None)

    y_cm = ctx.nct("y_acc", [128, C, T], BF16)
    y = y_cm.__enter__()
    w1b = x[:, :, :].bitcast(BF16).rearrange("p c (u l) -> p c (u l)")         if False else x[:, :, :].bitcast(BF16)
    with (
        ctx.nct("f1_w2", [128, 32, 1024], BF16) as w2b,
        ctx.nct("f1_r", [128, 32, LTF], BF16) as rb,
        ctx.nct("f1_xn", [128, C, LTF], BF16) as xnst,
        ctx.nct("f1_xe", [128, C, LTF], BF16) as xe,
        ctx.nct("f1_ps", [128, 4, LTF], BF16) as posb,
        ctx.nct("f1_lb", [128, T], BF16) as lamb,
        ctx.npt("f1_p1", [128, 3, 512], F32) as pm1,
        ctx.npt("f1_po", [128, 8, 256], F32) as po,
        ctx.npt("f1_pb", [128, 512], F32) as psbc,
    ):
        po_flat = po.rearrange("p a b -> p (a b)")
        w1sl = [ctx.slot(f"v1s{j}") for j in range(8)]
        w2sl = [ctx.slot(f"v2s{j}") for j in range(8)]
        w1_tok = [None] * 8
        w2_tok = [None] * 8

        def fetch_w1(e, j):
            if e >= 8:
                return
            w1_tok[j] = ctx.dma(
                ctx.pool, w1b[:, :, j * 512:(j + 1) * 512],
                W1T_ap[e].rearrange("(k p) m -> p k m", p=128)[:, :, j * 512:(j + 1) * 512],
                slot=w1sl[j])

        def fetch_w2(e, j):
            if e >= 8:
                return
            w2_tok[j] = ctx.dma(
                ctx.pool, w2b[:, j * 4:(j + 1) * 4, :],
                W2T_ap[e].rearrange("(m p) d -> p m d", p=128)[:, j * 4:(j + 1) * 4, :],
                slot=w2sl[j])

        for j in range(8):
            fetch_w1(0, j)
        for j in range(8):
            fetch_w2(0, j)

        xn_r = xn_scr
        mm1_cons = [None, None, None]
        xe_t = None
        xnst_t = None
        mm1_done = None
        mm2_done = None
        evac_done = None
        posb_rd = None
        r_done = None
        for e in range(8):
            ev4 = None
            for lt4 in range(4):
                ctx.mm(psbc[:, :], sel8[:, e * 128:(e + 1) * 128],
                       lam_em[0:8, lt4 * 512:(lt4 + 1) * 512],
                       start=True, stop=True, waits=[ev4], inc=True)
                pt = ctx.pe.tok()
                ev4 = ctx.cp(lamb[:, lt4 * 512:(lt4 + 1) * 512], psbc[:, :],
                             waits=[pt])
            lam_ready = ev4
            for lt in range(NLTF):
                lsl = slice(lt * LTF, (lt + 1) * LTF)
                xnst_t = ctx.dma(ctx.sp, xnst[:, :, :], xn_r[:, :, lsl],
                                 waits=[xe_t])
                for cc_ in range(C):
                    xe_t = ctx.ts(xe[:, cc_, :], xnst[:, cc_, :],
                                  gamma[:, e, cc_:cc_ + 1],
                                  beta[:, e, cc_:cc_ + 1], OP.mult, OP.add,
                                  waits=[xnst_t, mm1_done] if cc_ == 0 else None)
                t_r_d = None
                t_r_a = None
                for m in range(32):
                    sm = m % 3
                    jw = m // 4
                    for k in range(C):
                        ctx.mm(pm1[:, sm, 0:256], w1b[:, k, m * 128:(m + 1) * 128],
                               xe[:, k, :], start=(k == 0), stop=(k == C - 1),
                               waits=([w1_tok[jw], xe_t, mm1_cons[sm]]
                                      if k == 0 else None),
                               inc=(k == C - 1))
                    t_m = ctx.pe.tok()
                    bcol = b1c[:, e, m:m + 1]
                    if m % 2 == 0:
                        t_r_d = ctx.ts(rb[:, m, :], pm1[:, sm, 0:256], bcol, 0.0,
                                       OP.add, OP.max, waits=[t_m])
                        mm1_cons[sm] = t_r_d
                    else:
                        t_r_a = ctx.a(rb[:, m, :], pm1[:, sm, 0:256], AF.Relu,
                                      bias=bcol, waits=[t_m])
                        mm1_cons[sm] = t_r_a
                    if lt == NLTF - 1 and m % 4 == 3:
                        w1sl[m // 4].free = t_m
                        fetch_w1(e + 1, m // 4)
                mm1_done = ctx.pe.tok()
                for ph in range(2):
                    for m in range(32):
                        for bi in range(4):
                            dch = 2 * bi + ph
                            wl = []
                            if ph == 0 and m == 0 and bi == 0:
                                wl = [t_r_d, t_r_a, evac_done]
                            if ph == 0 and bi == 0 and m % 4 == 0:
                                wl.append(w2_tok[m // 4])
                            ctx.mm(po[:, dch, :],
                                   w2b[:, m, dch * 128:(dch + 1) * 128],
                                   rb[:, m, :], start=(m == 0), stop=(m == 31),
                                   waits=wl or None,
                                   inc=(ph == 1 and m == 31 and bi == 3))
                mm2_done = ctx.pe.tok()
                lb4 = lamb[:, lsl].unsqueeze(1).broadcast_to([128, 4, LTF])
                yv = y[:, :, lsl]
                if e == 0:
                    lb8 = lamb[:, lsl].unsqueeze(1).broadcast_to([128, 8, LTF])
                    evac_done = ctx.tt(yv, po[:, :, :], lb8, OP.mult,
                                       waits=[mm2_done, lam_ready])
                else:
                    for hh in range(2):
                        t_p1 = ctx.tt(posb[:, :, :], po[:, hh * 4:hh * 4 + 4, :],
                                      lb4, OP.mult,
                                      waits=[mm2_done, lam_ready, posb_rd])
                        posb_rd = ctx.tt(y[:, hh * 4:hh * 4 + 4, lsl],
                                         y[:, hh * 4:hh * 4 + 4, lsl],
                                         posb[:, :, :], OP.add)
                    evac_done = t_p1
                if lt == NLTF - 1:
                    for j in range(8):
                        w2sl[j].free = mm2_done
                        fetch_w2(e + 1, j)
        ctx.phase()

    with (
        ctx.nct("c_xs", [128, 2, C, 512], F32) as xs,
        ctx.nct("c_xn", [128, 2, C, 512], BF16) as xns,
        ctx.nct("c_t", [128, 2, 512], F32) as ct,
        ctx.npt("c_pb", [128, 2, 512], F32) as cpb,
    ):
        done = [None, None]
        for lt in range(4):
            lsl = slice(lt * 512, (lt + 1) * 512)
            s = lt % 2
            tx = ctx.dma(ctx.sp, xs[:, s, :, :], x_scr[:, :, lsl],
                         waits=[done[s]])
            tn = ctx.dma(ctx.sp, xns[:, s, :, :], xn_scr[:, :, lsl])
            last = None
            for cc_ in range(C):
                sc = cc_ % 2
                ctx.mm(cpb[:, sc, :], b2c[0:8, cc_ * 128:(cc_ + 1) * 128],
                       lam_em[0:8, lsl], start=True, stop=True,
                       waits=[done[s], last] if cc_ < 2 else [last], inc=True)
                ptb = ctx.pe.tok()
                ctx.stt(ct[:, sc, :], xns[:, s, cc_, :], 0.5, xs[:, s, cc_, :],
                        OP.mult, OP.add, waits=[tx, tn])
                ctx.tt(ct[:, sc, :], ct[:, sc, :], y[:, cc_, lsl], OP.add)
                last = ctx.tt(x[:, cc_, lsl], ct[:, sc, :], cpb[:, sc, :],
                              OP.add, waits=[ptb])
            done[s] = last
        ctx.phase()
    y_cm.__exit__(None, None, None)
    lam_cm.__exit__(None, None, None)


def emit_l1attn(ctx):
    nc = ctx.nc
    xn_cm = ctx.nct("xn1a", [128, C, T], BF16)
    ctx.sb["xn"] = xn_cm.__enter__()
    emit_ln(ctx)
    x, xn = ctx.sb["x"], ctx.sb["xn"]
    onf = ctx.sb["ones_f"]
    maskb = ctx.sb["maskb"]

    wqd_cm = ctx.nct("t_wqd", [128, 3, 64], BF16)
    wqd = wqd_cm.__enter__()
    for half in (0, 64):
        for ti, src in enumerate(("a1WqT", "a1WkT", "a1WvT")):
            ctx.dma(ctx.pool, wqd[half:half + 64, ti, :], ctx.w[src])
    wqd_t = (ctx.pool.sems[ctx.pool.cur], ctx.pool.count)
    with (
        ctx.nct("t_q", [64, 2, T], BF16) as q_sb,
        ctx.nct("t_k", [64, 2, T], BF16) as k_sb,
        ctx.nct("t_vT", [128, 2, 16, 65], BF16) as vT,
        ctx.nct("t_E", [128, 2, 1024], BF16) as E2,
        ctx.nct("t_yb", [128, 2, 512], F32) as yb,
        ctx.nct("t_rr", [1, 2, 512], F32) as rrow,
        ctx.nct("t_rl", [1, 2, 512], F32) as rln,
        ctx.npt("t_ps", [128, 4, 512], F32) as ps_s,
        ctx.npt("t_pa", [128, 2, 512], F32) as pav,
        ctx.npt("t_pb", [128, 512], F32) as pbc,
    ):
        t_ones = ctx.cp(
            vT[:, :, :, 64],
            ctx.sb["ones_b"][:, 0:32].rearrange("p (a b) -> p a b", a=2))
        ps_ev = [None, None, None, None]
        exp_w = [None, None]
        av_rd = [None, None]
        yb_cp = [None, None]
        yw_t = [None, None]
        bc_t = [None, None]
        yw_last = [None]
        pend = [None]

        def emit_finish(hp_, hc_, qt_, pa_, tr_):
            qsl_ = slice(qt_ * 512, (qt_ + 1) * 512)
            ctx.mm(pbc[0:64, :], onf[0:1, 0:64], rrow[0:1, pa_, :],
                   start=True, stop=True, waits=[tr_, yw_last[0]], inc=True)
            bc_t[pa_] = ctx.pe.tok()
            yw_t[pa_] = ctx.tt(xn[hp_:hp_ + 64, hc_, qsl_], yb[0:64, pa_, :],
                               pbc[0:64, :], OP.mult, waits=[bc_t[pa_]])
            yw_last[0] = yw_t[pa_]

        for h in range(16):
            hp = (h % 2) * 64
            hc = h // 2
            hs = h % 2
            rr_i = 0
            for ti, scale in ((0, 0.125), (1, None)):
                dst = q_sb if ti == 0 else k_sb
                for lt in range(4):
                    lsl = slice(lt * 512, (lt + 1) * 512)
                    b = rr_i % 4
                    rr_i += 1
                    ctx.mm(ps_s[0:64, b, :], wqd[hp:hp + 64, ti, :],
                           xn[hp:hp + 64, hc, lsl], start=True, stop=True,
                           waits=[ps_ev[b], wqd_t], inc=True)
                    pt = ctx.pe.tok()
                    if scale is None:
                        ps_ev[b] = ctx.cp(dst[:, hs, lsl], ps_s[0:64, b, :],
                                          waits=[pt])
                    else:
                        ps_ev[b] = ctx.ts(dst[:, hs, lsl], ps_s[0:64, b, :],
                                          scale, None, OP.mult, OP.bypass,
                                          waits=[pt])
            for kc in range(16):
                b = rr_i % 4
                rr_i += 1
                ctx.mm(ps_s[0:128, b, 0:64],
                       xn[hp:hp + 64, hc, kc * 128:(kc + 1) * 128],
                       wqd[hp:hp + 64, 2, :], start=True, stop=True,
                       waits=[ps_ev[b]], inc=True)
                pt = ctx.pe.tok()
                ps_ev[b] = ctx.cp(vT[:, hs, kc, 0:64], ps_s[0:128, b, 0:64],
                                  waits=[pt])
            qk_ready = list(ps_ev) + [t_ones]
            for qt in range(4):
                qsl = slice(qt * 512, (qt + 1) * 512)
                nkc = 4 * qt + 4
                npair = nkc // 2
                pa = qt % 2

                def emit_Spair(p):
                    s = p % 2
                    for half in range(2):
                        kc = 2 * p + half
                        b = 2 * s + half
                        w = [ps_ev[b]] + (qk_ready if kc == 0 else [])
                        ctx.mm(ps_s[:, b, :],
                               k_sb[:, hs, kc * 128:(kc + 1) * 128],
                               q_sb[:, hs, qsl], start=True, stop=True,
                               waits=w, inc=True)
                    t_s = ctx.pe.tok()
                    et = ctx.a(E2[:, s, :],
                               ps_s[:, 2 * s:2 * s + 2, :].rearrange(
                                   "p a b -> p (a b)"),
                               AF.Exp, waits=[t_s, av_rd[s]])
                    ps_ev[2 * s] = et
                    ps_ev[2 * s + 1] = et
                    for half in range(2):
                        kc = 2 * p + half
                        if kc >= 4 * qt:
                            off = (kc - 4 * qt) * 128
                            et = ctx.tt(
                                E2[:, s, half * 512:(half + 1) * 512],
                                E2[:, s, half * 512:(half + 1) * 512],
                                maskb[:, 384 - off:896 - off], OP.mult,
                                waits=[et])
                    exp_w[s] = et

                def emit_AVpair(p):
                    s = p % 2
                    for half in range(2):
                        kc = 2 * p + half
                        ctx.mm(pav[0:65, pa, :], vT[:, hs, kc, 0:65],
                               E2[:, s, half * 512:(half + 1) * 512],
                               start=(kc == 0), stop=(kc == nkc - 1),
                               waits=([exp_w[s], yb_cp[pa]] if kc == 0
                                      else [exp_w[s]]),
                               inc=True)
                        av_rd[s] = ctx.pe.tok()

                emit_Spair(0)
                for p in range(1, npair):
                    emit_Spair(p)
                    emit_AVpair(p - 1)
                    if p == 1 and pend[0] is not None:
                        emit_finish(*pend[0])
                        pend[0] = None
                emit_AVpair(npair - 1)
                last_AV = av_rd[(npair - 1) % 2]
                yb_cp[pa] = ctx.cp(yb[0:65, pa, :], pav[0:65, pa, :],
                                   waits=[last_AV, yw_t[pa]])
                tl = ctx.a(rln[0:1, pa, :], yb[64:65, pa, :], AF.Ln,
                           waits=[yb_cp[pa]])
                tr = ctx.a(rrow[0:1, pa, :], rln[0:1, pa, :], AF.Exp,
                           scale=-1.0, waits=[tl, bc_t[pa]])
                pend[0] = (hp, hc, qt, pa, tr)
        emit_finish(*pend[0])
        pend[0] = None
        ctx.phase()
    wqd_cm.__exit__(None, None, None)

    with (
        ctx.nct("t_wo", [128, 8, 1024], BF16) as wo,
        ctx.npt("t_pp", [128, 2, 512], F32) as pp2,
    ):
        wt = ctx.dma(ctx.pool, wo[:, :, :],
                     ctx.w["a1WoT"].rearrange("(k p) m -> p k m", p=128))
        ev = [None, None]
        for m in range(8):
            for lt in range(4):
                sl_ = slice(lt * 512, (lt + 1) * 512)
                sb_ = pp2[:, lt % 2, :]
                for k in range(C):
                    ctx.mm(sb_, wo[:, k, m * 128:(m + 1) * 128], xn[:, k, sl_],
                           start=(k == 0), stop=(k == C - 1),
                           waits=[wt, ev[lt % 2]] if k == 0 else None,
                           inc=(k == C - 1))
                pt = ctx.pe.tok()
                ev[lt % 2] = ctx.tt(x[:, m, sl_], x[:, m, sl_], sb_, OP.add,
                                    waits=[pt])
        ctx.phase()
    xn_cm.__exit__(None, None, None)



_EMITTERS = {}


def _build_sub(which):
    nc = bass.Bass("TRN2", target_bir_lowering=False)
    with ExitStack() as es:
        ctx = Ctx(nc, es)
        x_in = nc.dram_tensor("x", [T, D], F32, kind="ExternalInput").ap()
        x_out = nc.dram_tensor("xo", [T, D], F32, kind="ExternalOutput").ap()
        for nm in _SUB_WNAMES[which]:
            shape, npdt = _WSPECS[nm]
            ctx.w[nm] = nc.dram_tensor(nm, list(shape), _NP2BIR[npdt],
                                       kind="ExternalInput").ap()
        if which == "l0attn":
            for nm, shp in (("q_scr", [1024, T]), ("k_scr", [1024, T]),
                            ("v_scr", [1024, T]), ("y_scr", [64, 16, T])):
                ctx.w[nm] = nc.dram_tensor(nm, shp, BF16, kind="Internal").ap()
        if which == "l1ffn":
            ctx.w["x_scr"] = nc.dram_tensor("x_scr", [128, C, T], F32,
                                            kind="Internal").ap()
            ctx.w["xn_scr"] = nc.dram_tensor("xn_scr", [128, C, T], BF16,
                                             kind="Internal").ap()
        ctx.sb["x"] = es.enter_context(nc.sbuf_tensor("x_fm", [128, C, T], F32))
        _load_consts(ctx, _SUB_WNAMES[which])
        with nc.psum_tensor("psx", [128, 2, 512], F32) as psx:
            emit_load_x(ctx, x_in, psx)
            ctx.phase()
        if DEBUG_STOP != "loadx":
            _EMITTERS[which](ctx)
        with nc.psum_tensor("psx2", [128, 2, 512], F32) as psx2:
            st = emit_store_x(ctx, x_out, psx2)
            for t in st:
                ctx.sp.wait(t)
        ctx.finalize()
    return nc


_EMITTERS["l0ffn"] = emit_l0ffn
_EMITTERS["l0attn"] = emit_l0attn
_EMITTERS["l1ffn"] = emit_l1ffn
_EMITTERS["l1attn"] = emit_l1attn

def _build_full():
    nc = bass.Bass("TRN2", target_bir_lowering=False)
    names = []
    for sub in ("l0ffn", "l0attn", "l1ffn", "l1attn"):
        for nm in _SUB_WNAMES[sub]:
            if nm not in names:
                names.append(nm)
    with ExitStack() as es:
        ctx = Ctx(nc, es)
        x_in = nc.dram_tensor("x", [T, D], F32, kind="ExternalInput").ap()
        x_out = nc.dram_tensor("xo", [T, D], F32, kind="ExternalOutput").ap()
        for nm in names:
            shape, npdt = _WSPECS[nm]
            ctx.w[nm] = nc.dram_tensor(nm, list(shape), _NP2BIR[npdt],
                                       kind="ExternalInput").ap()
        for nm, shp, dt in (("q_scr", [1024, T], BF16), ("k_scr", [1024, T], BF16),
                            ("v_scr", [1024, T], BF16), ("y_scr", [64, 16, T], BF16),
                            ("x_scr", [128, C, T], F32),
                            ("xn_scr", [128, C, T], BF16)):
            ctx.w[nm] = nc.dram_tensor(nm, shp, dt, kind="Internal").ap()
        ctx.sb["x"] = es.enter_context(nc.sbuf_tensor("x_fm", [128, C, T], F32))
        _load_consts(ctx, names)
        with nc.psum_tensor("psx", [128, 2, 512], F32) as psx:
            emit_load_x(ctx, x_in, psx)
            ctx.phase()
        for sub in _CHAIN:
            _EMITTERS[sub](ctx)
        with nc.psum_tensor("psx2", [128, 2, 512], F32) as psx2:
            st = emit_store_x(ctx, x_out, psx2)
            for t in st:
                ctx.sp.wait(t)
        ctx.finalize()
    return nc


_PROG_CACHE = {}


def _get_prog(which):
    if which not in _PROG_CACHE:
        _PROG_CACHE[which] = _build_sub(which)
    return _PROG_CACHE[which]


def _run_chain(x_full, w, subs):
    x = np.ascontiguousarray(x_full, dtype=np.float32)
    for which in subs:
        nc = _get_prog(which)
        in_maps = []
        for core in range(NCORES):
            m = {"x": np.ascontiguousarray(x[core])}
            for nm in _SUB_WNAMES[which]:
                m[nm] = w[nm]
            in_maps.append(m)
        res = run_bass_kernel_spmd(nc, in_maps, core_ids=list(range(NCORES)))
        x = np.stack([res.results[i]["xo"] for i in range(NCORES)], axis=0)
    return x


_CHAIN = ["l0ffn", "l0attn", "l0ffn", "l1ffn", "l1attn", "l1ffn"]


def _run_fused(x_full, w):
    if "fused" not in _PROG_CACHE:
        _PROG_CACHE["fused"] = _build_full()
    nc = _PROG_CACHE["fused"]
    names = []
    for sub in ("l0ffn", "l0attn", "l1ffn", "l1attn"):
        for nm in _SUB_WNAMES[sub]:
            if nm not in names:
                names.append(nm)
    x = np.ascontiguousarray(x_full, dtype=np.float32)
    in_maps = []
    for core in range(NCORES):
        m = {"x": np.ascontiguousarray(x[core])}
        for nm in names:
            m[nm] = w[nm]
        in_maps.append(m)
    res = run_bass_kernel_spmd(nc, in_maps, core_ids=list(range(NCORES)))
    return np.stack([res.results[i]["xo"] for i in range(NCORES)], axis=0)


def kernel(**inputs):
    w = _prep_weights(inputs)
    x = np.asarray(inputs["L0_attn_x"], dtype=np.float32)
    if os.environ.get("KUNFUSED", ""):
        return _run_chain(x, w, _CHAIN)
    return _run_fused(x, w)


# version 16
# speedup vs baseline: 1.0548x; 1.0456x over previous
import numpy as np
import ml_dtypes
from contextlib import ExitStack

import concourse.bass as bass
import concourse.mybir as mybir
from concourse.bass_utils import run_bass_kernel_spmd
from concourse.alu_op_type import AluOpType as OP

F32 = mybir.dt.float32
BF16 = mybir.dt.bfloat16
AF = mybir.ActivationFunctionType
AX = mybir.AxisListType

T = 2048
D = 1024
C = 8
NT = 16
LTF = 256
NLTF = T // LTF

BF = ml_dtypes.bfloat16
NCORES = 8



def _prep_weights(inp):
    w = {}
    cc = np.ascontiguousarray
    w["gate_w"] = cc(inp["L0_ffn_gate_w"].reshape(8, 128, 8).transpose(1, 0, 2))
    w["gate_b"] = cc(inp["L0_ffn_gate_b"].reshape(8, 1))
    w["sw1"] = cc(inp["L0_ffn_sw1"][0])
    w["sb1"] = cc(inp["L0_ffn_sb1"][0].reshape(16, 128).T)
    w["sw2"] = cc(inp["L0_ffn_sw2"][0])
    w["sb2"] = cc(inp["L0_ffn_sb2"][0].reshape(8, 128).T)
    w["rw1"] = cc(inp["L0_ffn_rw1"])
    w["rb1"] = cc(inp["L0_ffn_rb1"].reshape(8, 16, 128).transpose(2, 0, 1))
    w["rw2"] = cc(inp["L0_ffn_rw2"])
    w["rb2"] = cc(inp["L0_ffn_rb2"])
    for nm in ("Wq", "Wk", "Wv", "Wo"):
        w["a0" + nm] = cc(inp["L0_attn_" + nm])
    for nm in ("bq", "bk", "bv", "bo"):
        w["a0" + nm] = cc(inp["L0_attn_" + nm].reshape(8, 128).T)
    for nm in ("Wq", "Wk", "Wv"):
        w["a1" + nm + "T"] = cc(inp["L1_attn_" + nm].T)
    w["a1WoT"] = cc(inp["L1_attn_Wo"].T)
    w["centT"] = cc(inp["L1_ffn_centroids"].T.reshape(8, 128, 8).transpose(1, 0, 2))
    w["gamma"] = cc(inp["L1_ffn_gamma"].reshape(8, 8, 128).transpose(2, 0, 1))
    w["beta"] = cc(inp["L1_ffn_beta"].reshape(8, 8, 128).transpose(2, 0, 1))
    w["W1T"] = cc(inp["L1_ffn_W1"].transpose(0, 2, 1))
    w["b1"] = cc(inp["L1_ffn_b1"].reshape(8, 32, 128).transpose(2, 0, 1))
    w["W2T"] = cc(inp["L1_ffn_W2"].transpose(0, 2, 1))
    w["b2"] = cc(inp["L1_ffn_b2"])
    w["ident_f"] = np.eye(128, dtype=np.float32)
    w["ident_b"] = np.eye(128, dtype=BF)
    w["ones_f"] = np.ones((128, 128), dtype=np.float32)
    w["ones_b"] = np.ones((128, 128), dtype=BF)
    sel8 = np.zeros((8, 8 * 128), dtype=BF)
    for e in range(8):
        sel8[e, e * 128:(e + 1) * 128] = 1
    w["sel8"] = sel8
    w["causal"] = np.where(np.tril(np.ones((128, 128), dtype=bool)),
                           0.0, -1e30).astype(np.float32)
    w["maskb"] = (np.arange(896)[None, :]
                  >= (np.arange(128)[:, None] + 384)).astype(BF)
    sel16c = np.zeros((16, 256), dtype=BF)
    sel16r = np.zeros((16, 256), dtype=BF)
    for b in range(16):
        sel16c[:, b * 16 + b] = 1
        sel16r[b, b * 16:(b + 1) * 16] = 1
    w["sel16c"] = sel16c
    w["sel16r"] = sel16r
    w["epsb"] = np.full((1, 1), 1e-5, dtype=np.float32)
    return w


_WSPECS = {
    "gate_w": ([128, 8, 8], np.float32), "gate_b": ([8, 1], np.float32),
    "sw1": ([1024, 2048], np.float32), "sb1": ([128, 16], np.float32),
    "sw2": ([1024, 1024], np.float32), "sb2": ([128, 8], np.float32),
    "rw1": ([8, 1024, 2048], np.float32), "rb1": ([128, 8, 16], np.float32),
    "rw2": ([8, 1024, 1024], np.float32), "rb2": ([8, 1024], np.float32),
    "a0Wq": ([1024, 1024], np.float32), "a0Wk": ([1024, 1024], np.float32),
    "a0Wv": ([1024, 1024], np.float32), "a0Wo": ([1024, 1024], np.float32),
    "a0bq": ([128, 8], np.float32), "a0bk": ([128, 8], np.float32),
    "a0bv": ([128, 8], np.float32), "a0bo": ([128, 8], np.float32),
    "a1WqT": ([64, 64], np.float32), "a1WkT": ([64, 64], np.float32),
    "a1WvT": ([64, 64], np.float32), "a1WoT": ([1024, 1024], np.float32),
    "centT": ([128, 8, 8], np.float32), "gamma": ([128, 8, 8], np.float32),
    "beta": ([128, 8, 8], np.float32),
    "W1T": ([8, 1024, 4096], np.float32), "b1": ([128, 8, 32], np.float32),
    "W2T": ([8, 4096, 1024], np.float32), "b2": ([8, 1024], np.float32),
    "ident_f": ([128, 128], np.float32), "ident_b": ([128, 128], BF),
    "ones_f": ([128, 128], np.float32), "ones_b": ([128, 128], BF),
    "sel8": ([8, 8 * 128], BF), "causal": ([128, 128], np.float32),
    "maskb": ([128, 896], BF),
    "sel16c": ([16, 256], BF), "sel16r": ([16, 256], BF),
    "epsb": ([1, 1], np.float32),
}

_SB_BF16 = {"gate_w", "rb2", "b2", "centT", "a1WqT", "a1WkT", "a1WvT"}

_SUB_WNAMES = {
    "l0ffn": ["gate_w", "gate_b", "sw1", "sb1", "sw2", "sb2", "rw1", "rb1",
              "rw2", "rb2", "ident_f", "ident_b", "ones_f", "ones_b", "sel8",
              "epsb"],
    "l0attn": ["a0Wq", "a0Wk", "a0Wv", "a0Wo", "a0bq", "a0bk", "a0bv", "a0bo",
               "ident_f", "ident_b", "ones_f", "ones_b", "sel16c", "sel16r",
               "epsb"],
    "l1ffn": ["centT", "gamma", "beta", "W1T", "b1", "W2T", "b2",
              "ident_f", "ident_b", "ones_f", "ones_b", "sel8", "epsb"],
    "l1attn": ["a1WqT", "a1WkT", "a1WvT", "a1WoT",
               "ident_f", "ident_b", "ones_f", "ones_b", "maskb", "epsb"],
}

_NP2BIR = {np.float32: F32, BF: BF16}

import os
DEBUG_STOP = os.environ.get("KDEBUG", "")



class _Eng:
    def __init__(self, ctx, obj, name, nsem=4):
        self.ctx = ctx
        self.obj = obj
        self.name = name
        self.sems = [ctx.es.enter_context(ctx.nc.semaphore(f"s_{name}{i}"))
                     for i in range(nsem)]
        self.cur = 0
        self.count = 0
        self.self_waited = 0
        self.last = None
        self.last_inced = True
        self.prev_tok = None
        self.waited = {}

    def rotate(self):
        if self.count > 20000:
            self.prev_tok = self.tok()
            self.cur = (self.cur + 1) % len(self.sems)
            self.count = 0
            self.self_waited = 0
            self.last = None
            self.last_inced = True

    def tok(self):
        if self.last is not None and not self.last_inced:
            self.last.then_inc(self.sems[self.cur], 1)
            self.count += 1
            self.last_inced = True
            if self.count - self.self_waited >= 128:
                self.obj.wait_ge(self.sems[self.cur], self.count - 64)
                self.self_waited = self.count - 64
                k = id(self.sems[self.cur])
                if self.waited.get(k, 0) < self.count - 64:
                    self.waited[k] = self.count - 64
        if self.count == 0:
            return self.prev_tok
        return (self.sems[self.cur], self.count)

    def wait(self, tok):
        if tok is None:
            return
        sem, val = tok
        if val <= 0:
            return
        k = id(sem)
        if self.waited.get(k, 0) >= val:
            return
        self.obj.wait_ge(sem, val)
        self.waited[k] = val

    def issue(self, inst, inc):
        self.last = inst
        self.last_inced = False
        if inc:
            return self.tok()
        return None


class Slot:
    def __init__(self, ctx, name):
        self.sem = ctx.es.enter_context(ctx.nc.semaphore(name))
        self.count = 0
        self.self_waited = 0
        self.free = None
        ctx.all_slots.append(self)


class Ctx:
    def __init__(self, nc, es):
        self.nc = nc
        self.es = es
        self.pe = _Eng(self, nc.tensor, "pe", 6)
        self.dve = _Eng(self, nc.vector, "dve", 4)
        self.act = _Eng(self, nc.scalar, "act", 4)
        self.pool = _Eng(self, nc.gpsimd, "pool", 4)
        self.sp = _Eng(self, nc.sync, "sp", 4)
        self.engs = [self.pe, self.dve, self.act, self.pool, self.sp]
        self.sb = {}
        self.w = {}
        self.all_slots = []
        self.slots = {}
        self._uid = 0

    def finalize(self):
        for e in self.engs:
            t = e.tok()
            self.sp.wait(t)
        for sl in self.all_slots:
            if sl.count > 0:
                self.sp.wait((sl.sem, sl.count))
        for e in self.engs:
            for i, sem in enumerate(e.sems):
                pass

    def nct(self, name, shape, dt):
        self._uid += 1
        return self.nc.sbuf_tensor(f"{name}_{self._uid}", shape, dt)

    def npt(self, name, shape, dt):
        self._uid += 1
        return self.nc.psum_tensor(f"{name}_{self._uid}", shape, dt)

    def slot(self, name):
        if name not in self.slots:
            self.slots[name] = Slot(self, name)
        return self.slots[name]

    def phase(self):
        toks = [e.tok() for e in self.engs]
        for e in self.engs:
            for t in toks:
                if t is not None and (t[0] is not e.sems[e.cur] or True):
                    e.wait(t)
        for e in self.engs:
            e.rotate()

    def _pre(self, eng, waits):
        if waits:
            for t in waits:
                eng.wait(t)

    def dma(self, eng, out, in_, waits=None, slot=None):
        self._pre(eng, waits)
        if slot is not None:
            if slot.free is not None:
                eng.wait(slot.free)
        inst = eng.obj.dma_start(out, in_)
        if slot is not None:
            inst.then_inc(slot.sem, 16)
            slot.count += 16
            eng.last, eng.last_inced = inst, True
            if slot.count - slot.self_waited >= 96:
                eng.obj.wait_ge(slot.sem, slot.count - 32)
                slot.self_waited = slot.count - 32
            return (slot.sem, slot.count)
        inst.then_inc(eng.sems[eng.cur], 16)
        eng.count += 16
        eng.last, eng.last_inced = inst, True
        if eng.count - eng.self_waited >= 128:
            eng.obj.wait_ge(eng.sems[eng.cur], eng.count - 64)
            eng.self_waited = eng.count - 64
            k = id(eng.sems[eng.cur])
            if eng.waited.get(k, 0) < eng.count - 64:
                eng.waited[k] = eng.count - 64
        return (eng.sems[eng.cur], eng.count)

    def mm(self, out, lhsT, rhs, start, stop, waits=None, inc=False):
        self._pre(self.pe, waits)
        return self.pe.issue(
            self.nc.tensor.matmul(out, lhsT, rhs, start=start, stop=stop), inc)

    def xp(self, out, in_, ident, waits=None, inc=False):
        self._pre(self.pe, waits)
        return self.pe.issue(self.nc.tensor.transpose(out, in_, ident), inc)

    def tt(self, out, a, b, op, waits=None, inc=True, eng=None):
        e = eng or self.dve
        self._pre(e, waits)
        return e.issue(e.obj.tensor_tensor(out, a, b, op), inc)

    def ts(self, out, a, s1, s2, op0, op1, waits=None, inc=True, eng=None,
           accum_out=None):
        e = eng or self.dve
        self._pre(e, waits)
        return e.issue(e.obj.tensor_scalar(out, a, s1, s2, op0, op1,
                                           accum_out=accum_out), inc)

    def stt(self, out, in0, scalar, in1, op0, op1, waits=None, inc=True):
        self._pre(self.dve, waits)
        return self.dve.issue(
            self.nc.vector.scalar_tensor_tensor(out, in0, scalar, in1, op0, op1),
            inc)

    def cp(self, out, in_, waits=None, inc=True, eng=None):
        e = eng or self.dve
        self._pre(e, waits)
        if e is self.act:
            return e.issue(self.nc.scalar.copy(out, in_), inc)
        return e.issue(e.obj.tensor_copy(out, in_), inc)

    def red(self, out, in_, op, negate=False, waits=None, inc=True):
        self._pre(self.dve, waits)
        return self.dve.issue(
            self.nc.vector.tensor_reduce(out, in_, AX.X, op, negate=negate), inc)

    def recip(self, out, in_, waits=None, inc=True):
        self._pre(self.dve, waits)
        return self.dve.issue(self.nc.vector.reciprocal(out, in_), inc)

    def a(self, out, in_, func, bias=None, scale=None, accum_out=None,
          waits=None, inc=True):
        self._pre(self.act, waits)
        kw = {}
        if bias is not None:
            kw["bias"] = bias
        if scale is not None:
            kw["scale"] = scale
        if accum_out is not None:
            kw["accum_out"] = accum_out
        return self.act.issue(self.nc.scalar.activation(out, in_, func, **kw), inc)



_STREAMED = {"sw1", "sw2", "rw1", "rw2", "W1T", "W2T",
             "a0Wq", "a0Wk", "a0Wv", "a0Wo", "a1WoT"}


def _load_consts(ctx, names):
    toks = []
    for nm in names:
        if nm in _STREAMED:
            continue
        shape, npdt = _WSPECS[nm]
        sb_dt = BF16 if nm in _SB_BF16 else _NP2BIR[npdt]
        t = ctx.es.enter_context(ctx.nc.sbuf_tensor("c_" + nm, list(shape), sb_dt))
        ctx.sb[nm] = t
        q = ctx.pool if sb_dt != _NP2BIR[npdt] else ctx.sp
        sl = tuple(slice(None) for _ in shape)
        toks.append(ctx.dma(q, t[sl], ctx.w[nm]))
    return toks


def emit_load_x(ctx, x_ap, psx):
    x, idf = ctx.sb["x"], ctx.sb["ident_f"]
    stage_cm = ctx.nct("stage_l", [128, 4, 1024], F32)
    stage = stage_cm.__enter__()
    stage_free = [None] * 4
    ev = [None, None]
    for i in range(NT):
        s = i % 4
        ld = ctx.dma(ctx.sp, stage[:, s, :], x_ap[i * 128:(i + 1) * 128, :],
                     waits=[stage_free[s]])
        for c in range(C):
            sl = c % 2
            ctx.xp(psx[:, sl, 0:128], stage[:, s, c * 128:(c + 1) * 128],
                   idf[:, :], waits=[ld, ev[sl]], inc=True)
            pt = ctx.pe.tok()
            ev[sl] = ctx.cp(x[:, c, i * 128:(i + 1) * 128], psx[:, sl, 0:128],
                            waits=[pt])
        stage_free[s] = ctx.pe.tok()
    stage_cm.__exit__(None, None, None)


def emit_store_x(ctx, xo_ap, psx):
    x, idf = ctx.sb["x"], ctx.sb["ident_f"]
    stage_cm = ctx.nct("stage_s", [128, 4, 1024], F32)
    stage = stage_cm.__enter__()
    dma_done = [None] * 4
    prev_ev = [None, None]
    toks = []
    for i in range(NT):
        s = i % 4
        ev = None
        for c in range(C):
            sl = c % 2
            ctx.xp(psx[:, sl, 0:128], x[:, c, i * 128:(i + 1) * 128], idf[:, :],
                   waits=[prev_ev[sl], dma_done[s]], inc=True)
            pt = ctx.pe.tok()
            ev = ctx.cp(stage[:, s, c * 128:(c + 1) * 128], psx[:, sl, 0:128],
                        waits=[pt])
            prev_ev[sl] = ev
        dma_done[s] = ctx.dma(ctx.sp, xo_ap[i * 128:(i + 1) * 128, :],
                              stage[:, s, :], waits=[ev])
        toks.append(dma_done[s])
    stage_cm.__exit__(None, None, None)
    return toks


def emit_ln(ctx):
    nc = ctx.nc
    x, xn = ctx.sb["x"], ctx.sb["xn"]
    onf, onb = ctx.sb["ones_f"], ctx.sb["ones_b"]
    with (
        ctx.nct("ln_xq", [128, C, T], BF16) as xq,
        ctx.nct("ln_rows", [1, 3, T], F32) as rows,
        ctx.nct("ln_tmp", [128, 2, 512], F32) as tmp,
        ctx.npt("ln_pr", [128, 2, 512], F32) as pr,
        ctx.npt("ln_pb", [128, 2, 512], F32) as pb,
    ):
        sq_t = [ctx.a(xq[:, cc_, :], x[:, cc_, :], AF.Square) for cc_ in range(C)]
        ev_s = None
        ev_q = None
        for lt in range(4):
            lsl = slice(lt * 512, (lt + 1) * 512)
            for cc_ in range(C):
                ctx.mm(pr[0:1, 0, :], onf[:, 0:1], x[:, cc_, lsl],
                       start=(cc_ == 0), stop=(cc_ == C - 1),
                       waits=[ev_s] if cc_ == 0 else None, inc=(cc_ == C - 1))
            pt_s = ctx.pe.tok()
            for cc_ in range(C):
                ctx.mm(pr[0:1, 1, :], onb[:, 0:1], xq[:, cc_, lsl],
                       start=(cc_ == 0), stop=(cc_ == C - 1),
                       waits=([ev_q] if ev_q else []) + [sq_t[cc_]],
                       inc=(cc_ == C - 1))
            pt_q = ctx.pe.tok()
            ev_s = ctx.cp(rows[0:1, 0, lsl], pr[0:1, 0, :], waits=[pt_s])
            ev_q = ctx.cp(rows[0:1, 1, lsl], pr[0:1, 1, :], waits=[pt_q])
        r0, r1, r2 = rows[0:1, 0, :], rows[0:1, 1, :], rows[0:1, 2, :]
        ctx.ts(r0, r0, 1.0 / D, None, OP.mult, OP.bypass, waits=[ev_s, ev_q])
        ctx.nc.vector.drain()
        ctx.tt(r2, r0, r0, OP.mult)
        ctx.nc.vector.drain()
        t_var = ctx.stt(r1, r1, 1.0 / D, r2, OP.mult, OP.subtract)
        t_ln = ctx.a(r2, r1, AF.Ln, bias=ctx.sb["epsb"][0:1, 0:1],
                     waits=[t_var])
        ctx.nc.scalar.drain()
        t_rs = ctx.a(r1, r2, AF.Exp, scale=-0.5)
        t_mu = ctx.dve.tok()
        tmp_free = [None, None]
        bc_done = None
        last = None
        for lt in range(4):
            lsl = slice(lt * 512, (lt + 1) * 512)
            ctx.mm(pb[:, 0, :], onf[0:1, :], r0[:, lsl], start=True, stop=True,
                   waits=[t_mu, t_rs, bc_done])
            ctx.mm(pb[:, 1, :], onf[0:1, :], r1[:, lsl], start=True, stop=True,
                   inc=True)
            pt = ctx.pe.tok()
            for cc_ in range(C):
                s = cc_ % 2
                ctx.tt(tmp[:, s, :], x[:, cc_, lsl], pb[:, 0, :], OP.subtract,
                       waits=[pt, tmp_free[s]])
                tmp_free[s] = ctx.tt(xn[:, cc_, lsl], tmp[:, s, :], pb[:, 1, :],
                                     OP.mult)
                last = tmp_free[s]
            bc_done = last
        ctx.phase()


def emit_l0ffn(ctx):
    nc = ctx.nc
    xn_cm = ctx.nct("xn0", [128, C, T], BF16)
    ctx.sb["xn"] = xn_cm.__enter__()
    cw_cm = ctx.nct("cw_em", [8, T], BF16)
    ctx.sb["cw_em"] = cw_cm.__enter__()
    emit_ln(ctx)
    x, xn = ctx.sb["x"], ctx.sb["xn"]
    idf, idb = ctx.sb["ident_f"], ctx.sb["ident_b"]
    sel8 = ctx.sb["sel8"]
    gate_w, gate_b = ctx.sb["gate_w"], ctx.sb["gate_b"]
    sb1, sb2 = ctx.sb["sb1"], ctx.sb["sb2"]
    rb1, rb2 = ctx.sb["rb1"], ctx.sb["rb2"]
    cw_em = ctx.sb["cw_em"]
    rw1_ap, rw2_ap = ctx.w["rw1"], ctx.w["rw2"]
    sw1_ap, sw2_ap = ctx.w["sw1"], ctx.w["sw2"]

    with (
        ctx.nct("g_G", [8, T], F32) as G,
        ctx.nct("g_tm", [128, NT, 8], F32) as Gtm,
        ctx.nct("g_E", [128, NT, 8], F32) as Eo,
        ctx.nct("g_pr", [128, NT, 8], F32) as probs,
        ctx.nct("g_m1", [128, NT, 8], F32) as m1,
        ctx.nct("g_pm", [128, NT, 8], F32) as pm,
        ctx.nct("g_mk", [128, NT, 8], F32) as msk,
        ctx.nct("g_cw", [128, NT, 8], BF16) as cwtm,
        ctx.nct("g_r1", [128, NT, 1], F32) as nm_,
        ctx.nct("g_r2", [128, NT, 1], F32) as ss,
        ctx.nct("g_r3", [128, NT, 1], F32) as rr,
        ctx.nct("g_r4", [128, NT, 1], F32) as p1,
        ctx.nct("g_r5", [128, NT, 1], F32) as p2,
        ctx.npt("g_ps", [128, 512], F32) as psg,
        ctx.npt("g_pt", [128, 2, 512], F32) as psgt,
        ctx.npt("g_ct0", [128, 128], BF16) as psct0,
        ctx.npt("g_ct1", [128, 128], BF16) as psct1,
    ):
        ev = None
        for lt in range(4):
            lsl = slice(lt * 512, (lt + 1) * 512)
            for k in range(C):
                ctx.mm(psg[0:8, :], gate_w[:, k, :], xn[:, k, lsl],
                       start=(k == 0), stop=(k == C - 1),
                       waits=[ev] if k == 0 else None, inc=(k == C - 1))
            pt = ctx.pe.tok()
            ev = ctx.ts(G[0:8, lsl], psg[0:8, :], gate_b[:, :], None,
                        OP.add, OP.bypass, waits=[pt])
        ev2 = [None, None]
        for i in range(NT):
            s = i % 2
            ctx.xp(psgt[:, s, 0:8], G[0:8, i * 128:(i + 1) * 128], idf[0:8, 0:8],
                   waits=[ev, ev2[s]], inc=True)
            pt = ctx.pe.tok()
            ev2[s] = ctx.cp(Gtm[:, i, :], psgt[:, s, 0:8], waits=[pt])
        gt = ctx.dve.tok()
        for i in range(NT):
            ctx.red(nm_[:, i, :], Gtm[:, i, :], OP.max, negate=True,
                    waits=[gt] if i == 0 else None)
        dt_ = ctx.dve.tok()
        tE = None
        for i in range(NT):
            tE = ctx.a(Eo[:, i, :], Gtm[:, i, :], AF.Exp, bias=nm_[:, i, :],
                       waits=[dt_] if i == 0 else None)
        for i in range(NT):
            ctx.red(ss[:, i, :], Eo[:, i, :], OP.add,
                    waits=[tE] if i == 0 else None)
        for i in range(NT):
            ctx.recip(rr[:, i, :], ss[:, i, :])
        for i in range(NT):
            ctx.ts(probs[:, i, :], Eo[:, i, :], rr[:, i, :], None, OP.mult,
                   OP.bypass)
        for i in range(NT):
            ctx.red(p1[:, i, :], probs[:, i, :], OP.max)
        for i in range(NT):
            ctx.ts(m1[:, i, :], probs[:, i, :], p1[:, i, :], None, OP.is_ge,
                   OP.bypass)
        for i in range(NT):
            ctx.stt(pm[:, i, :], m1[:, i, :], -1e30, probs[:, i, :], OP.mult,
                    OP.add)
        for i in range(NT):
            ctx.red(p2[:, i, :], pm[:, i, :], OP.max)
        for i in range(NT):
            ctx.ts(msk[:, i, :], probs[:, i, :], p2[:, i, :], None, OP.is_ge,
                   OP.bypass)
        tcw = None
        for i in range(NT):
            tcw = ctx.stt(cwtm[:, i, :], msk[:, i, :], 0.5, probs[:, i, :],
                          OP.mult, OP.mult)
        ev3 = [None, None]
        psct = [psct0, psct1]
        for i in range(NT):
            s = i % 2
            ctx.xp(psct[s][0:8, :], cwtm[:, i, :], idb[:, :],
                   waits=[tcw, ev3[s]], inc=True)
            pt = ctx.pe.tok()
            ev3[s] = ctx.cp(cw_em[0:8, i * 128:(i + 1) * 128], psct[s][0:8, :],
                            waits=[pt])
        ctx.phase()
    if DEBUG_STOP == "gate":
        return

    with (
        ctx.nct("f_w1", [128, 8, 8, 2, 128], BF16) as w1b,
        ctx.nct("f_w2", [128, 8, 1024], BF16) as w2b,
        ctx.nct("f_sa", [128, 3, LTF], BF16) as sa,
        ctx.nct("f_sw", [128, 2, 8, LTF], BF16) as swg,
        ctx.nct("f_cwb", [128, T], BF16) as cwb,
        ctx.nct("f_po", [128, 8, LTF], F32) as posb0,
        ctx.nct("f_s2h", [128, 8], F32) as sb2h,
        ctx.npt("f_p1", [128, 3, 2, 256], F32) as pmm,
        ctx.npt("f_po", [128, 8, 256], F32) as po,
        ctx.npt("f_pb", [128, 512], F32) as psbc,
    ):
        po_flat = po.rearrange("p a b -> p (a b)")
        w1sl = [ctx.slot(f"w1s{j}") for j in range(8)]
        w2sl = [ctx.slot(f"w2s{j}") for j in range(8)]
        w1_tok = [None] * 8
        w2_tok = [None] * 8

        def w1_src(e, j, b):
            src = sw1_ap if e == 0 else rw1_ap[e - 1]
            r = src.rearrange("(k p) (b m) -> p k b m", p=128, b=16)
            return r[:, :, j + 8 * b, :]

        def w2_src(e, j):
            src = sw2_ap if e == 0 else rw2_ap[e - 1]
            return src[j * 128:(j + 1) * 128, :]

        def fetch_w1(e, j):
            if e >= 9:
                return
            ctx.dma(ctx.pool, w1b[:, j, :, 0, :], w1_src(e, j, 0), slot=w1sl[j])
            w1_tok[j] = ctx.dma(ctx.pool, w1b[:, j, :, 1, :], w1_src(e, j, 1),
                                slot=w1sl[j])

        def fetch_w2(e, j):
            if e >= 9:
                return
            w2_tok[j] = ctx.dma(ctx.pool, w2b[:, j, :], w2_src(e, j),
                                slot=w2sl[j])

        for j in range(8):
            fetch_w1(0, j)
        for j in range(8):
            fetch_w2(0, j)

        t_s2h = ctx.ts(sb2h[:, :], sb2[:, :], 0.5, None, OP.mult, OP.bypass)

        mm1_cons = [None, None, None]
        sa_cons = [None, None, None]
        swg_cons = [None, None]
        evac_done = None
        posb_rd = None
        w1_tok = w1_tok
        w2_tok = w2_tok
        cwb_ready = None
        n_exp = 9
        if DEBUG_STOP.startswith("exp"):
            n_exp = int(DEBUG_STOP[3:]) + 1
        for e in range(n_exp):
            if e >= 1:
                ev4 = None
                for lt4 in range(4):
                    ctx.mm(psbc[:, :], sel8[:, (e - 1) * 128:e * 128],
                           cw_em[0:8, lt4 * 512:(lt4 + 1) * 512],
                           start=True, stop=True, waits=[ev4], inc=True)
                    pt = ctx.pe.tok()
                    ev4 = ctx.cp(cwb[:, lt4 * 512:(lt4 + 1) * 512], psbc[:, :],
                                 waits=[pt])
                cwb_ready = ev4
            for lt in range(NLTF):
                lsl = slice(lt * LTF, (lt + 1) * LTF)
                lr = lt % 2
                t_swg = None
                for j in range(8):
                    sj = j % 3
                    for k in range(C):
                        ctx.mm(pmm[:, sj, 0, :], w1b[:, j, k, 0, :], xn[:, k, lsl],
                               start=(k == 0), stop=(k == C - 1),
                               waits=[w1_tok[j], mm1_cons[sj]] if k == 0 else None,
                               inc=(k == C - 1))
                    ta = ctx.pe.tok()
                    for k in range(C):
                        ctx.mm(pmm[:, sj, 1, :], w1b[:, j, k, 1, :], xn[:, k, lsl],
                               start=(k == 0), stop=(k == C - 1),
                               inc=(k == C - 1))
                    tb = ctx.pe.tok()
                    b_a = sb1[:, j:j + 1] if e == 0 else rb1[:, e - 1, j:j + 1]
                    b_b = (sb1[:, j + 8:j + 9] if e == 0
                           else rb1[:, e - 1, j + 8:j + 9])
                    t_sa = ctx.a(sa[:, sj, :], pmm[:, sj, 0, :], AF.Silu,
                                 bias=b_a, waits=[tb, sa_cons[sj]])
                    t_swg = ctx.stt(swg[:, lr, j, :], pmm[:, sj, 1, :], b_b,
                                    sa[:, sj, :], OP.add, OP.mult,
                                    waits=[tb, t_sa, swg_cons[lr]])
                    sa_cons[sj] = t_swg
                    mm1_cons[sj] = t_swg
                    if lt == NLTF - 1:
                        w1sl[j].free = tb
                        fetch_w1(e + 1, j)
                for ph in range(2):
                    for j in range(8):
                        for bi in range(4):
                            dch = 2 * bi + ph
                            wl = []
                            if ph == 0 and j == 0 and bi == 0:
                                wl = [t_swg, evac_done]
                            if ph == 0 and bi == 0:
                                wl.append(w2_tok[j])
                            ctx.mm(po[:, dch, :],
                                   w2b[:, j, dch * 128:(dch + 1) * 128],
                                   swg[:, lr, j, :], start=(j == 0),
                                   stop=(j == 7), waits=wl or None,
                                   inc=(ph == 1 and j == 7 and bi == 3))
                mm2_last = ctx.pe.tok()
                swg_cons[lr] = mm2_last
                if lt == NLTF - 1:
                    for j in range(8):
                        w2sl[j].free = mm2_last
                        fetch_w2(e + 1, j)
                xv = x[:, :, lsl]
                if e == 0:
                    evac_done = ctx.stt(xv, po_flat, 0.5, xv, OP.mult, OP.add,
                                        waits=[mm2_last])
                else:
                    cb3 = cwb[:, lsl].unsqueeze(1).broadcast_to([128, 8, LTF])
                    t_p1 = ctx.tt(posb0[:, :, :], po[:, :, :], cb3, OP.mult,
                                  waits=[mm2_last, cwb_ready, posb_rd])
                    evac_done = t_p1
                    posb_rd = ctx.tt(xv, xv, posb0[:, :, :], OP.add)
        if DEBUG_STOP == "experts" or DEBUG_STOP.startswith("exp"):
            ctx.phase()
            return
        ev5 = None
        for mch in range(8):
            for lt4 in range(4):
                ctx.mm(psbc[:, :], rb2[0:8, mch * 128:(mch + 1) * 128],
                       cw_em[0:8, lt4 * 512:(lt4 + 1) * 512], start=True,
                       stop=True, waits=[evac_done, ev5], inc=True)
                pt = ctx.pe.tok()
                ev5 = ctx.tt(x[:, mch, lt4 * 512:(lt4 + 1) * 512],
                             x[:, mch, lt4 * 512:(lt4 + 1) * 512],
                             psbc[:, :], OP.add, waits=[pt])
        for cc_ in range(C):
            ctx.ts(x[:, cc_, :], x[:, cc_, :], sb2h[:, cc_:cc_ + 1], None,
                   OP.add, OP.bypass, waits=[t_s2h])
        ctx.phase()
    cw_cm.__exit__(None, None, None)
    xn_cm.__exit__(None, None, None)


def emit_l0attn(ctx):
    nc = ctx.nc
    xn_cm = ctx.nct("xn0a", [128, C, T], BF16)
    ctx.sb["xn"] = xn_cm.__enter__()
    emit_ln(ctx)
    x, xn = ctx.sb["x"], ctx.sb["xn"]
    onf, onb = ctx.sb["ones_f"], ctx.sb["ones_b"]
    q_scr, k_scr, v_scr, y_scr = (ctx.w["q_scr"], ctx.w["k_scr"],
                                  ctx.w["v_scr"], ctx.w["y_scr"])

    with (
        ctx.nct("a_w", [128, 8, 1024], BF16) as wbuf,
        ctx.nct("a_st", [128, 4, 2048], BF16) as stg,
        ctx.npt("a_pp", [128, 2, 512], F32) as pp,
    ):
        w_free = None
        for ti, (wnm, bnm, scr, scale) in enumerate((
                ("a0Wq", "a0bq", q_scr, 0.125),
                ("a0Wk", "a0bk", k_scr, None),
                ("a0Wv", "a0bv", v_scr, None))):
            wt = ctx.dma(ctx.pool, wbuf[:, :, :],
                         ctx.w[wnm].rearrange("(k p) m -> p k m", p=128),
                         waits=[w_free])
            bias = ctx.sb[bnm]
            dma_done = [None] * 4
            ev = [None, None]
            for m in range(8):
                sm = m % 4
                for lt in range(4):
                    sl_ = slice(lt * 512, (lt + 1) * 512)
                    sb_ = pp[:, lt % 2, :]
                    for k in range(C):
                        ctx.mm(sb_, wbuf[:, k, m * 128:(m + 1) * 128],
                               xn[:, k, sl_], start=(k == 0), stop=(k == C - 1),
                               waits=([wt, ev[lt % 2], dma_done[sm]]
                                      if k == 0 else None),
                               inc=(k == C - 1))
                    pt = ctx.pe.tok()
                    if scale is None:
                        ev[lt % 2] = ctx.ts(stg[:, sm, sl_], sb_,
                                            bias[:, m:m + 1], None,
                                            OP.add, OP.bypass, waits=[pt])
                    else:
                        ev[lt % 2] = ctx.ts(stg[:, sm, sl_], sb_,
                                            bias[:, m:m + 1], scale,
                                            OP.add, OP.mult, waits=[pt])
                dma_done[sm] = ctx.dma(ctx.sp, scr[m * 128:(m + 1) * 128, :],
                                       stg[:, sm, :], waits=[ev[0], ev[1]])
            w_free = ctx.pe.tok()
        ctx.phase()

    sel16c, sel16r = ctx.sb["sel16c"], ctx.sb["sel16r"]
    with (
        ctx.nct("a_qd", [64, 16, 512], BF16) as qdm,
        ctx.nct("a_kd", [64, 16, 512], BF16) as kdm,
        ctx.nct("a_ve", [16, 2, 64, 128], BF16) as vem,
        ctx.nct("a_E", [16, 16, 512], BF16) as Eb,
        ctx.nct("a_P", [16, 2, 32, 16], BF16) as Pb,
        ctx.nct("a_rl", [16, 512], F32) as rrln,
        ctx.nct("a_rr", [16, 512], BF16) as rr16,
        ctx.nct("a_oh", [64, 2, 16, 128], BF16) as ohd,
        ctx.npt("a_ps", [128, 2, 512], F32) as ps_s,
        ctx.npt("a_po", [128, 2, 512], F32) as ps_o2,
        ctx.npt("a_pr", [128, 2, 512], F32) as ps_r2,
    ):
        q_r = q_scr.rearrange("(h d) l -> d h l", h=16)
        k_r = k_scr.rearrange("(h d) l -> d h l", h=16)
        v_r = v_scr.rearrange("(e d) l -> e d l", e=16)
        qk_done = None
        v_done = [None, None]
        exp_t = [None, None]
        ptt = [None, None]
        tt_t = [None, None]
        oc_t = [None, None]
        oh_dma = [None, None]
        e_free = None
        trl_prev = None
        bc_last = None
        for lt in range(4):
            lsl = slice(lt * 512, (lt + 1) * 512)
            tq = ctx.dma(ctx.sp, qdm[:, :, :], q_r[:, :, lsl], waits=[qk_done])
            tk = ctx.dma(ctx.sp, kdm[:, :, :], k_r[:, :, lsl])

            def emit_summ(b):
                ctx.mm(ps_r2[0:16, 0, :], sel16c[:, b * 16:(b + 1) * 16],
                       Eb[:, b, :], start=(b == 0), stop=(b == 15),
                       waits=([exp_t[b % 2]]
                              + ([e_free, trl_prev] if b == 0 else [])),
                       inc=(b == 15))

            for b in range(16):
                sb_ = b % 2
                for t in range(32):
                    l = b * 32 + t
                    ctx.mm(ps_s[0:16, sb_, t * 16:(t + 1) * 16],
                           kdm[:, :, l], qdm[:, :, l], start=True, stop=True,
                           waits=([tq, tk, exp_t[sb_]] if t == 0 else None),
                           inc=(t == 31))
                t_s1 = ctx.pe.tok()
                exp_t[sb_] = ctx.a(Eb[:, b, :], ps_s[0:16, sb_, :], AF.Exp,
                                   waits=[t_s1, e_free])
                if b >= 1:
                    emit_summ(b - 1)
            emit_summ(15)
            qk_done = ctx.pe.tok()
            tl = ctx.a(rrln[:, :], ps_r2[0:16, 0, :], AF.Ln, waits=[qk_done])
            trl_prev = tl
            tr = ctx.a(rr16[:, :], rrln[:, :], AF.Exp, scale=-1.0,
                       waits=[tl, bc_last])

            def emit_bc(b):
                bank = b % 2
                ctx.mm(ps_r2[0:16, 1, bank * 256:(bank + 1) * 256]
                       if False else ps_r2[0:16, bank, :],
                       sel16r[:, b * 16:(b + 1) * 16], rr16[:, :],
                       start=True, stop=True,
                       waits=[tr, tt_t[bank]]
                       + ([trl_prev] if bank == 0 else []), inc=True)
                return ctx.pe.tok()

            def emit_tt(b, t_bc):
                bank = b % 2
                pv = ps_r2[0:16, bank, :].rearrange("p (l h) -> p l h", h=16)
                tt_t[bank] = ctx.tt(
                    Pb[:, b % 2, :, :],
                    Eb[:, b, :].rearrange("p (l h) -> p l h", h=16),
                    pv, OP.mult, waits=[t_bc, ptt[b % 2]])

            t_bc = emit_bc(0)
            emit_tt(0, t_bc)
            for b in range(16):
                sb_ = b % 2
                vt = b // 4
                vs = vt % 2
                if b % 4 == 0:
                    gl = slice(lt * 512 + vt * 128, lt * 512 + (vt + 1) * 128)
                    v_tok = ctx.dma(ctx.sp, vem[:, vs, :, :], v_r[:, :, gl],
                                    waits=[v_done[vs]])
                    v_done[vs] = v_tok
                if b + 1 < 16:
                    t_bc = emit_bc(b + 1)
                    emit_tt(b + 1, t_bc)
                for t in range(32):
                    l = b * 32 + t
                    tl_ = l % 128
                    ctx.mm(ps_o2[0:64, sb_, t * 16:(t + 1) * 16],
                           vem[:, vs, :, tl_], Pb[:, sb_, t, :],
                           start=True, stop=True,
                           waits=([tt_t[sb_], v_tok, oc_t[sb_]]
                                  if t == 0 else None),
                           inc=(t == 31))
                ptt[sb_] = ctx.pe.tok()
                if b % 4 == 3:
                    v_done[vs] = ptt[sb_]
                os_ = vt % 2
                oc_t[sb_] = ctx.cp(
                    ohd[:, os_, :, (b % 4) * 32:(b % 4 + 1) * 32],
                    ps_o2[0:64, sb_, :].rearrange("p (l h) -> p h l", h=16),
                    waits=[ptt[sb_], oh_dma[os_]])
                if b % 4 == 3:
                    oh_dma[os_] = ctx.dma(
                        ctx.sp, y_scr[:, :, lt * 512 + vt * 128:
                                      lt * 512 + (vt + 1) * 128],
                        ohd[:, os_, :, :], waits=[oc_t[sb_]])
            bc_last = t_bc
            e_free = tt_t[1]
        ctx.phase()

    with (
        ctx.nct("a_wo", [128, 8, 1024], BF16) as wo,
        ctx.npt("a_pp2", [128, 2, 512], F32) as pp2,
    ):
        wt = ctx.dma(ctx.pool, wo[:, :, :],
                     ctx.w["a0Wo"].rearrange("(k p) m -> p k m", p=128))
        for h in range(16):
            ctx.dma(ctx.sp, xn[(h % 2) * 64:(h % 2) * 64 + 64, h // 2, :],
                    y_scr[:, h, :])
        ytok = (ctx.sp.sems[ctx.sp.cur], ctx.sp.count)
        bo = ctx.sb["a0bo"]
        ev = [None, None]
        for m in range(8):
            for lt in range(4):
                sl_ = slice(lt * 512, (lt + 1) * 512)
                sb_ = pp2[:, lt % 2, :]
                for k in range(C):
                    ctx.mm(sb_, wo[:, k, m * 128:(m + 1) * 128], xn[:, k, sl_],
                           start=(k == 0), stop=(k == C - 1),
                           waits=[wt, ytok, ev[lt % 2]] if k == 0 else None,
                           inc=(k == C - 1))
                pt = ctx.pe.tok()
                ev[lt % 2] = ctx.stt(x[:, m, sl_], sb_, bo[:, m:m + 1],
                                     x[:, m, sl_], OP.add, OP.add, waits=[pt])
        ctx.phase()
    xn_cm.__exit__(None, None, None)


def emit_l1ffn(ctx):
    nc = ctx.nc
    x = ctx.sb["x"]
    onf, onb = ctx.sb["ones_f"], ctx.sb["ones_b"]
    idf, idb = ctx.sb["ident_f"], ctx.sb["ident_b"]
    sel8 = ctx.sb["sel8"]
    centT = ctx.sb["centT"]
    gamma, beta = ctx.sb["gamma"], ctx.sb["beta"]
    b1c, b2c = ctx.sb["b1"], ctx.sb["b2"]
    x_scr, xn_scr = ctx.w["x_scr"], ctx.w["xn_scr"]
    W1T_ap, W2T_ap = ctx.w["W1T"], ctx.w["W2T"]
    lam_cm = ctx.nct("lam_em", [8, T], BF16)
    lam_em = lam_cm.__enter__()

    G_cm = ctx.nct("G_l1", [8, T], F32)
    G = G_cm.__enter__()
    with (
        ctx.nct("l1_xq", [128, C, T], BF16) as xq,
        ctx.nct("l1_rows", [1, 3, T], F32) as rows,
        ctx.nct("l1_xst", [128, 2, C, 512], BF16) as xst,
        ctx.nct("l1_tmp", [128, 2, 512], F32) as tmp,
        ctx.npt("l1_pr", [128, 2, 512], F32) as pr,
        ctx.npt("l1_pb", [128, 2, 512], F32) as pb,
        ctx.npt("l1_pg", [128, 512], F32) as pg,
    ):
        sq_t = [ctx.a(xq[:, cc_, :], x[:, cc_, :], AF.Square) for cc_ in range(C)]
        ev_s = None
        ev_q = None
        for lt in range(4):
            lsl = slice(lt * 512, (lt + 1) * 512)
            for cc_ in range(C):
                ctx.mm(pr[0:1, 0, :], onf[:, 0:1], x[:, cc_, lsl],
                       start=(cc_ == 0), stop=(cc_ == C - 1),
                       waits=[ev_s] if cc_ == 0 else None, inc=(cc_ == C - 1))
            pt_s = ctx.pe.tok()
            for cc_ in range(C):
                ctx.mm(pr[0:1, 1, :], onb[:, 0:1], xq[:, cc_, lsl],
                       start=(cc_ == 0), stop=(cc_ == C - 1),
                       waits=([ev_q] if ev_q else []) + [sq_t[cc_]],
                       inc=(cc_ == C - 1))
            pt_q = ctx.pe.tok()
            ev_s = ctx.cp(rows[0:1, 0, lsl], pr[0:1, 0, :], waits=[pt_s])
            ev_q = ctx.cp(rows[0:1, 1, lsl], pr[0:1, 1, :], waits=[pt_q])
        r0, r1, r2 = rows[0:1, 0, :], rows[0:1, 1, :], rows[0:1, 2, :]
        ctx.ts(r0, r0, 1.0 / D, None, OP.mult, OP.bypass, waits=[ev_s, ev_q])
        ctx.nc.vector.drain()
        ctx.tt(r2, r0, r0, OP.mult)
        ctx.nc.vector.drain()
        t_var = ctx.stt(r1, r1, 1.0 / D, r2, OP.mult, OP.subtract)
        t_ln = ctx.a(r2, r1, AF.Ln, bias=ctx.sb["epsb"][0:1, 0:1], waits=[t_var])
        ctx.nc.scalar.drain()
        t_rs = ctx.a(r1, r2, AF.Exp, scale=-0.5)
        t_mu = ctx.dve.tok()
        tmp_free = [None, None]
        st_dma = [None, None]
        bc_done = None
        g_ev = None
        for lt in range(4):
            lsl = slice(lt * 512, (lt + 1) * 512)
            sl2 = lt % 2
            ctx.mm(pb[:, 0, :], onf[0:1, :], r0[:, lsl], start=True, stop=True,
                   waits=[t_mu, t_rs, bc_done])
            ctx.mm(pb[:, 1, :], onf[0:1, :], r1[:, lsl], start=True, stop=True,
                   inc=True)
            pt = ctx.pe.tok()
            last = None
            for cc_ in range(C):
                s = cc_ % 2
                ctx.tt(tmp[:, s, :], x[:, cc_, lsl], pb[:, 0, :], OP.subtract,
                       waits=[pt, tmp_free[s], st_dma[sl2]])
                tmp_free[s] = ctx.tt(xst[:, sl2, cc_, :], tmp[:, s, :],
                                     pb[:, 1, :], OP.mult)
                last = tmp_free[s]
            bc_done = last
            st_dma[sl2] = ctx.dma(ctx.sp, xn_scr[:, :, lsl], xst[:, sl2, :, :],
                                  waits=[last])
            for cc_ in range(C):
                ctx.mm(pg[0:8, :], centT[:, cc_, :], xst[:, sl2, cc_, :],
                       start=(cc_ == 0), stop=(cc_ == C - 1),
                       waits=[last, g_ev] if cc_ == 0 else None,
                       inc=(cc_ == C - 1))
            pt2 = ctx.pe.tok()
            g_ev = ctx.cp(G[0:8, lsl], pg[0:8, :], waits=[pt2])
            if lt == 3:
                pass
        sp_toks = []
        for lt in range(4):
            lsl = slice(lt * 512, (lt + 1) * 512)
            sp_toks.append(ctx.dma(ctx.sp, x_scr[:, :, lsl], x[:, :, lsl],
                                   waits=[bc_done]))
        ctx.phase()

    with (
        ctx.nct("g2_tm", [128, NT, 8], F32) as Gtm,
        ctx.nct("g2_sm", [128, NT, 1], F32) as smax,
        ctx.nct("g2_al", [128, NT, 1], F32) as alph,
        ctx.nct("g2_m", [128, NT, 8], F32) as m1,
        ctx.nct("g2_lm", [128, NT, 8], BF16) as lamtm,
        ctx.npt("g2_pt", [128, 2, 512], F32) as psgt,
        ctx.npt("g2_c0", [128, 128], BF16) as psct0,
        ctx.npt("g2_c1", [128, 128], BF16) as psct1,
    ):
        ev2 = [None, None]
        for i in range(NT):
            s = i % 2
            ctx.xp(psgt[:, s, 0:8], G[0:8, i * 128:(i + 1) * 128], idf[0:8, 0:8],
                   waits=[ev2[s]], inc=True)
            pt = ctx.pe.tok()
            ev2[s] = ctx.cp(Gtm[:, i, :], psgt[:, s, 0:8], waits=[pt])
        gt = ctx.dve.tok()
        for i in range(NT):
            ctx.red(smax[:, i, :], Gtm[:, i, :], OP.max,
                    waits=[gt] if i == 0 else None)
        dt_ = ctx.dve.tok()
        tA = None
        for i in range(NT):
            tA = ctx.a(alph[:, i, :], smax[:, i, :], AF.Sigmoid,
                       waits=[dt_] if i == 0 else None)
        for i in range(NT):
            ctx.ts(m1[:, i, :], Gtm[:, i, :], smax[:, i, :], None, OP.is_ge,
                   OP.bypass, waits=[tA] if i == 0 else None)
        tlam = None
        for i in range(NT):
            tlam = ctx.ts(lamtm[:, i, :], m1[:, i, :], alph[:, i, :], 0.5,
                          OP.mult, OP.mult)
        ev3 = [None, None]
        psct = [psct0, psct1]
        for i in range(NT):
            s = i % 2
            ctx.xp(psct[s][0:8, :], lamtm[:, i, :], idb[:, :],
                   waits=[tlam, ev3[s]], inc=True)
            pt = ctx.pe.tok()
            ev3[s] = ctx.cp(lam_em[0:8, i * 128:(i + 1) * 128], psct[s][0:8, :],
                            waits=[pt])
        ctx.phase()
    G_cm.__exit__(None, None, None)

    y_cm = ctx.nct("y_acc", [128, C, T], BF16)
    y = y_cm.__enter__()
    w1b = x[:, :, :].bitcast(BF16).rearrange("p c (u l) -> p c (u l)")         if False else x[:, :, :].bitcast(BF16)
    with (
        ctx.nct("f1_w2", [128, 32, 1024], BF16) as w2b,
        ctx.nct("f1_r", [128, 32, LTF], BF16) as rb,
        ctx.nct("f1_xn", [128, C, LTF], BF16) as xnst,
        ctx.nct("f1_xe", [128, C, LTF], BF16) as xe,
        ctx.nct("f1_ps", [128, 4, LTF], BF16) as posb,
        ctx.nct("f1_lb", [128, T], BF16) as lamb,
        ctx.npt("f1_p1", [128, 3, 512], F32) as pm1,
        ctx.npt("f1_po", [128, 8, 256], F32) as po,
        ctx.npt("f1_pb", [128, 512], F32) as psbc,
    ):
        po_flat = po.rearrange("p a b -> p (a b)")
        w1sl = [ctx.slot(f"v1s{j}") for j in range(8)]
        w2sl = [ctx.slot(f"v2s{j}") for j in range(8)]
        w1_tok = [None] * 8
        w2_tok = [None] * 8

        def fetch_w1(e, j):
            if e >= 8:
                return
            w1_tok[j] = ctx.dma(
                ctx.pool, w1b[:, :, j * 512:(j + 1) * 512],
                W1T_ap[e].rearrange("(k p) m -> p k m", p=128)[:, :, j * 512:(j + 1) * 512],
                slot=w1sl[j])

        def fetch_w2(e, j):
            if e >= 8:
                return
            w2_tok[j] = ctx.dma(
                ctx.pool, w2b[:, j * 4:(j + 1) * 4, :],
                W2T_ap[e].rearrange("(m p) d -> p m d", p=128)[:, j * 4:(j + 1) * 4, :],
                slot=w2sl[j])

        for j in range(8):
            fetch_w1(0, j)
        for j in range(8):
            fetch_w2(0, j)

        xn_r = xn_scr
        mm1_cons = [None, None, None]
        xe_t = None
        xnst_t = None
        mm1_done = None
        mm2_done = None
        evac_done = None
        posb_rd = None
        r_done = None
        for e in range(8):
            ev4 = None
            for lt4 in range(4):
                ctx.mm(psbc[:, :], sel8[:, e * 128:(e + 1) * 128],
                       lam_em[0:8, lt4 * 512:(lt4 + 1) * 512],
                       start=True, stop=True, waits=[ev4], inc=True)
                pt = ctx.pe.tok()
                ev4 = ctx.cp(lamb[:, lt4 * 512:(lt4 + 1) * 512], psbc[:, :],
                             waits=[pt])
            lam_ready = ev4
            for lt in range(NLTF):
                lsl = slice(lt * LTF, (lt + 1) * LTF)
                xnst_t = ctx.dma(ctx.sp, xnst[:, :, :], xn_r[:, :, lsl],
                                 waits=[xe_t])
                for cc_ in range(C):
                    xe_t = ctx.ts(xe[:, cc_, :], xnst[:, cc_, :],
                                  gamma[:, e, cc_:cc_ + 1],
                                  beta[:, e, cc_:cc_ + 1], OP.mult, OP.add,
                                  waits=[xnst_t, mm1_done] if cc_ == 0 else None)
                t_r_d = None
                t_r_a = None
                for m in range(32):
                    sm = m % 3
                    jw = m // 4
                    for k in range(C):
                        ctx.mm(pm1[:, sm, 0:256], w1b[:, k, m * 128:(m + 1) * 128],
                               xe[:, k, :], start=(k == 0), stop=(k == C - 1),
                               waits=([w1_tok[jw], xe_t, mm1_cons[sm]]
                                      if k == 0 else None),
                               inc=(k == C - 1))
                    t_m = ctx.pe.tok()
                    bcol = b1c[:, e, m:m + 1]
                    if m % 2 == 0:
                        t_r_d = ctx.ts(rb[:, m, :], pm1[:, sm, 0:256], bcol, 0.0,
                                       OP.add, OP.max, waits=[t_m])
                        mm1_cons[sm] = t_r_d
                    else:
                        t_r_a = ctx.a(rb[:, m, :], pm1[:, sm, 0:256], AF.Relu,
                                      bias=bcol, waits=[t_m])
                        mm1_cons[sm] = t_r_a
                    if lt == NLTF - 1 and m % 4 == 3:
                        w1sl[m // 4].free = t_m
                        fetch_w1(e + 1, m // 4)
                mm1_done = ctx.pe.tok()
                for ph in range(2):
                    for m in range(32):
                        for bi in range(4):
                            dch = 2 * bi + ph
                            wl = []
                            if ph == 0 and m == 0 and bi == 0:
                                wl = [t_r_d, t_r_a, evac_done]
                            if ph == 0 and bi == 0 and m % 4 == 0:
                                wl.append(w2_tok[m // 4])
                            ctx.mm(po[:, dch, :],
                                   w2b[:, m, dch * 128:(dch + 1) * 128],
                                   rb[:, m, :], start=(m == 0), stop=(m == 31),
                                   waits=wl or None,
                                   inc=(ph == 1 and m == 31 and bi == 3))
                mm2_done = ctx.pe.tok()
                lb4 = lamb[:, lsl].unsqueeze(1).broadcast_to([128, 4, LTF])
                yv = y[:, :, lsl]
                if e == 0:
                    lb8 = lamb[:, lsl].unsqueeze(1).broadcast_to([128, 8, LTF])
                    evac_done = ctx.tt(yv, po[:, :, :], lb8, OP.mult,
                                       waits=[mm2_done, lam_ready])
                else:
                    for hh in range(2):
                        t_p1 = ctx.tt(posb[:, :, :], po[:, hh * 4:hh * 4 + 4, :],
                                      lb4, OP.mult,
                                      waits=[mm2_done, lam_ready, posb_rd])
                        posb_rd = ctx.tt(y[:, hh * 4:hh * 4 + 4, lsl],
                                         y[:, hh * 4:hh * 4 + 4, lsl],
                                         posb[:, :, :], OP.add)
                    evac_done = t_p1
                if lt == NLTF - 1:
                    for j in range(8):
                        w2sl[j].free = mm2_done
                        fetch_w2(e + 1, j)
        ctx.phase()

    with (
        ctx.nct("c_xs", [128, 2, C, 512], F32) as xs,
        ctx.nct("c_xn", [128, 2, C, 512], BF16) as xns,
        ctx.nct("c_t", [128, 2, 512], F32) as ct,
        ctx.npt("c_pb", [128, 2, 512], F32) as cpb,
    ):
        done = [None, None]
        for lt in range(4):
            lsl = slice(lt * 512, (lt + 1) * 512)
            s = lt % 2
            tx = ctx.dma(ctx.sp, xs[:, s, :, :], x_scr[:, :, lsl],
                         waits=[done[s]])
            tn = ctx.dma(ctx.sp, xns[:, s, :, :], xn_scr[:, :, lsl])
            last = None
            for cc_ in range(C):
                sc = cc_ % 2
                ctx.mm(cpb[:, sc, :], b2c[0:8, cc_ * 128:(cc_ + 1) * 128],
                       lam_em[0:8, lsl], start=True, stop=True,
                       waits=[done[s], last] if cc_ < 2 else [last], inc=True)
                ptb = ctx.pe.tok()
                ctx.stt(ct[:, sc, :], xns[:, s, cc_, :], 0.5, xs[:, s, cc_, :],
                        OP.mult, OP.add, waits=[tx, tn])
                ctx.tt(ct[:, sc, :], ct[:, sc, :], y[:, cc_, lsl], OP.add)
                last = ctx.tt(x[:, cc_, lsl], ct[:, sc, :], cpb[:, sc, :],
                              OP.add, waits=[ptb])
            done[s] = last
        ctx.phase()
    y_cm.__exit__(None, None, None)
    lam_cm.__exit__(None, None, None)


def emit_l1attn(ctx):
    nc = ctx.nc
    xn_cm = ctx.nct("xn1a", [128, C, T], BF16)
    ctx.sb["xn"] = xn_cm.__enter__()
    emit_ln(ctx)
    x, xn = ctx.sb["x"], ctx.sb["xn"]
    onf = ctx.sb["ones_f"]
    maskb = ctx.sb["maskb"]

    wqd_cm = ctx.nct("t_wqd", [128, 3, 64], BF16)
    wqd = wqd_cm.__enter__()
    for half in (0, 64):
        for ti, src in enumerate(("a1WqT", "a1WkT", "a1WvT")):
            ctx.dma(ctx.pool, wqd[half:half + 64, ti, :], ctx.w[src])
    wqd_t = (ctx.pool.sems[ctx.pool.cur], ctx.pool.count)
    with (
        ctx.nct("t_q", [64, 2, T], BF16) as q_sb,
        ctx.nct("t_k", [64, 2, T], BF16) as k_sb,
        ctx.nct("t_vT", [128, 2, 16, 65], BF16) as vT,
        ctx.nct("t_E", [128, 2, 1024], BF16) as E2,
        ctx.nct("t_yb", [128, 2, 512], F32) as yb,
        ctx.nct("t_rr", [1, 2, 512], F32) as rrow,
        ctx.nct("t_rl", [1, 2, 512], F32) as rln,
        ctx.npt("t_ps", [128, 4, 512], F32) as ps_s,
        ctx.npt("t_pa", [128, 2, 512], F32) as pav,
        ctx.npt("t_pb", [128, 512], F32) as pbc,
    ):
        t_ones = ctx.cp(
            vT[:, :, :, 64],
            ctx.sb["ones_b"][:, 0:32].rearrange("p (a b) -> p a b", a=2))
        ps_ev = [None, None, None, None]
        exp_w = [None, None]
        av_rd = [None, None]
        yb_cp = [None, None]
        yw_t = [None, None]
        bc_t = [None, None]
        yw_last = [None]
        pend = [None]

        def emit_finish(hp_, hc_, qt_, pa_, tr_):
            qsl_ = slice(qt_ * 512, (qt_ + 1) * 512)
            ctx.mm(pbc[0:64, :], onf[0:1, 0:64], rrow[0:1, pa_, :],
                   start=True, stop=True, waits=[tr_, yw_last[0]], inc=True)
            bc_t[pa_] = ctx.pe.tok()
            yw_t[pa_] = ctx.tt(xn[hp_:hp_ + 64, hc_, qsl_], yb[0:64, pa_, :],
                               pbc[0:64, :], OP.mult, waits=[bc_t[pa_]])
            yw_last[0] = yw_t[pa_]

        for h in range(16):
            hp = (h % 2) * 64
            hc = h // 2
            hs = h % 2
            rr_i = 0
            for ti, scale in ((0, 0.125), (1, None)):
                dst = q_sb if ti == 0 else k_sb
                for lt in range(4):
                    lsl = slice(lt * 512, (lt + 1) * 512)
                    b = rr_i % 4
                    rr_i += 1
                    ctx.mm(ps_s[0:64, b, :], wqd[hp:hp + 64, ti, :],
                           xn[hp:hp + 64, hc, lsl], start=True, stop=True,
                           waits=[ps_ev[b], wqd_t], inc=True)
                    pt = ctx.pe.tok()
                    if scale is None:
                        ps_ev[b] = ctx.cp(dst[:, hs, lsl], ps_s[0:64, b, :],
                                          waits=[pt], eng=ctx.act)
                    else:
                        ps_ev[b] = ctx.ts(dst[:, hs, lsl], ps_s[0:64, b, :],
                                          scale, None, OP.mult, OP.bypass,
                                          waits=[pt])
            for kc in range(16):
                b = rr_i % 4
                rr_i += 1
                ctx.mm(ps_s[0:128, b, 0:64],
                       xn[hp:hp + 64, hc, kc * 128:(kc + 1) * 128],
                       wqd[hp:hp + 64, 2, :], start=True, stop=True,
                       waits=[ps_ev[b]], inc=True)
                pt = ctx.pe.tok()
                ps_ev[b] = ctx.cp(vT[:, hs, kc, 0:64], ps_s[0:128, b, 0:64],
                                  waits=[pt],
                                  eng=(ctx.act if kc % 2 else ctx.dve))
            qk_ready = list(ps_ev) + [t_ones]
            for qt in range(4):
                qsl = slice(qt * 512, (qt + 1) * 512)
                nkc = 4 * qt + 4
                npair = nkc // 2
                pa = qt % 2

                def emit_Spair(p):
                    s = p % 2
                    for half in range(2):
                        kc = 2 * p + half
                        b = 2 * s + half
                        w = [ps_ev[b]] + (qk_ready if kc == 0 else [])
                        ctx.mm(ps_s[:, b, :],
                               k_sb[:, hs, kc * 128:(kc + 1) * 128],
                               q_sb[:, hs, qsl], start=True, stop=True,
                               waits=w, inc=True)
                    t_s = ctx.pe.tok()
                    et = ctx.a(E2[:, s, :],
                               ps_s[:, 2 * s:2 * s + 2, :].rearrange(
                                   "p a b -> p (a b)"),
                               AF.Exp, waits=[t_s, av_rd[s]])
                    ps_ev[2 * s] = et
                    ps_ev[2 * s + 1] = et
                    for half in range(2):
                        kc = 2 * p + half
                        if kc >= 4 * qt:
                            off = (kc - 4 * qt) * 128
                            et = ctx.tt(
                                E2[:, s, half * 512:(half + 1) * 512],
                                E2[:, s, half * 512:(half + 1) * 512],
                                maskb[:, 384 - off:896 - off], OP.mult,
                                waits=[et])
                    exp_w[s] = et

                def emit_AVpair(p):
                    s = p % 2
                    for half in range(2):
                        kc = 2 * p + half
                        ctx.mm(pav[0:65, pa, :], vT[:, hs, kc, 0:65],
                               E2[:, s, half * 512:(half + 1) * 512],
                               start=(kc == 0), stop=(kc == nkc - 1),
                               waits=([exp_w[s], yb_cp[pa]] if kc == 0
                                      else [exp_w[s]]),
                               inc=True)
                        av_rd[s] = ctx.pe.tok()

                emit_Spair(0)
                for p in range(1, npair):
                    emit_Spair(p)
                    emit_AVpair(p - 1)
                    if p == 1 and pend[0] is not None:
                        emit_finish(*pend[0])
                        pend[0] = None
                emit_AVpair(npair - 1)
                last_AV = av_rd[(npair - 1) % 2]
                yb_cp[pa] = ctx.cp(yb[0:65, pa, :], pav[0:65, pa, :],
                                   waits=[last_AV, yw_t[pa]])
                tl = ctx.a(rln[0:1, pa, :], yb[64:65, pa, :], AF.Ln,
                           waits=[yb_cp[pa]])
                tr = ctx.a(rrow[0:1, pa, :], rln[0:1, pa, :], AF.Exp,
                           scale=-1.0, waits=[tl, bc_t[pa]])
                pend[0] = (hp, hc, qt, pa, tr)
        emit_finish(*pend[0])
        pend[0] = None
        ctx.phase()
    wqd_cm.__exit__(None, None, None)

    with (
        ctx.nct("t_wo", [128, 8, 1024], BF16) as wo,
        ctx.npt("t_pp", [128, 2, 512], F32) as pp2,
    ):
        wt = ctx.dma(ctx.pool, wo[:, :, :],
                     ctx.w["a1WoT"].rearrange("(k p) m -> p k m", p=128))
        ev = [None, None]
        for m in range(8):
            for lt in range(4):
                sl_ = slice(lt * 512, (lt + 1) * 512)
                sb_ = pp2[:, lt % 2, :]
                for k in range(C):
                    ctx.mm(sb_, wo[:, k, m * 128:(m + 1) * 128], xn[:, k, sl_],
                           start=(k == 0), stop=(k == C - 1),
                           waits=[wt, ev[lt % 2]] if k == 0 else None,
                           inc=(k == C - 1))
                pt = ctx.pe.tok()
                ev[lt % 2] = ctx.tt(x[:, m, sl_], x[:, m, sl_], sb_, OP.add,
                                    waits=[pt])
        ctx.phase()
    xn_cm.__exit__(None, None, None)



_EMITTERS = {}


def _build_sub(which):
    nc = bass.Bass("TRN2", target_bir_lowering=False)
    with ExitStack() as es:
        ctx = Ctx(nc, es)
        x_in = nc.dram_tensor("x", [T, D], F32, kind="ExternalInput").ap()
        x_out = nc.dram_tensor("xo", [T, D], F32, kind="ExternalOutput").ap()
        for nm in _SUB_WNAMES[which]:
            shape, npdt = _WSPECS[nm]
            ctx.w[nm] = nc.dram_tensor(nm, list(shape), _NP2BIR[npdt],
                                       kind="ExternalInput").ap()
        if which == "l0attn":
            for nm, shp in (("q_scr", [1024, T]), ("k_scr", [1024, T]),
                            ("v_scr", [1024, T]), ("y_scr", [64, 16, T])):
                ctx.w[nm] = nc.dram_tensor(nm, shp, BF16, kind="Internal").ap()
        if which == "l1ffn":
            ctx.w["x_scr"] = nc.dram_tensor("x_scr", [128, C, T], F32,
                                            kind="Internal").ap()
            ctx.w["xn_scr"] = nc.dram_tensor("xn_scr", [128, C, T], BF16,
                                             kind="Internal").ap()
        ctx.sb["x"] = es.enter_context(nc.sbuf_tensor("x_fm", [128, C, T], F32))
        _load_consts(ctx, _SUB_WNAMES[which])
        with nc.psum_tensor("psx", [128, 2, 512], F32) as psx:
            emit_load_x(ctx, x_in, psx)
            ctx.phase()
        if DEBUG_STOP != "loadx":
            _EMITTERS[which](ctx)
        with nc.psum_tensor("psx2", [128, 2, 512], F32) as psx2:
            st = emit_store_x(ctx, x_out, psx2)
            for t in st:
                ctx.sp.wait(t)
        ctx.finalize()
    return nc


_EMITTERS["l0ffn"] = emit_l0ffn
_EMITTERS["l0attn"] = emit_l0attn
_EMITTERS["l1ffn"] = emit_l1ffn
_EMITTERS["l1attn"] = emit_l1attn

def _build_full():
    nc = bass.Bass("TRN2", target_bir_lowering=False)
    names = []
    for sub in ("l0ffn", "l0attn", "l1ffn", "l1attn"):
        for nm in _SUB_WNAMES[sub]:
            if nm not in names:
                names.append(nm)
    with ExitStack() as es:
        ctx = Ctx(nc, es)
        x_in = nc.dram_tensor("x", [T, D], F32, kind="ExternalInput").ap()
        x_out = nc.dram_tensor("xo", [T, D], F32, kind="ExternalOutput").ap()
        for nm in names:
            shape, npdt = _WSPECS[nm]
            ctx.w[nm] = nc.dram_tensor(nm, list(shape), _NP2BIR[npdt],
                                       kind="ExternalInput").ap()
        for nm, shp, dt in (("q_scr", [1024, T], BF16), ("k_scr", [1024, T], BF16),
                            ("v_scr", [1024, T], BF16), ("y_scr", [64, 16, T], BF16),
                            ("x_scr", [128, C, T], F32),
                            ("xn_scr", [128, C, T], BF16)):
            ctx.w[nm] = nc.dram_tensor(nm, shp, dt, kind="Internal").ap()
        ctx.sb["x"] = es.enter_context(nc.sbuf_tensor("x_fm", [128, C, T], F32))
        _load_consts(ctx, names)
        with nc.psum_tensor("psx", [128, 2, 512], F32) as psx:
            emit_load_x(ctx, x_in, psx)
            ctx.phase()
        for sub in _CHAIN:
            _EMITTERS[sub](ctx)
        with nc.psum_tensor("psx2", [128, 2, 512], F32) as psx2:
            st = emit_store_x(ctx, x_out, psx2)
            for t in st:
                ctx.sp.wait(t)
        ctx.finalize()
    return nc


_PROG_CACHE = {}


def _get_prog(which):
    if which not in _PROG_CACHE:
        _PROG_CACHE[which] = _build_sub(which)
    return _PROG_CACHE[which]


def _run_chain(x_full, w, subs):
    x = np.ascontiguousarray(x_full, dtype=np.float32)
    for which in subs:
        nc = _get_prog(which)
        in_maps = []
        for core in range(NCORES):
            m = {"x": np.ascontiguousarray(x[core])}
            for nm in _SUB_WNAMES[which]:
                m[nm] = w[nm]
            in_maps.append(m)
        res = run_bass_kernel_spmd(nc, in_maps, core_ids=list(range(NCORES)))
        x = np.stack([res.results[i]["xo"] for i in range(NCORES)], axis=0)
    return x


_CHAIN = ["l0ffn", "l0attn", "l0ffn", "l1ffn", "l1attn", "l1ffn"]


def _run_fused(x_full, w):
    if "fused" not in _PROG_CACHE:
        _PROG_CACHE["fused"] = _build_full()
    nc = _PROG_CACHE["fused"]
    names = []
    for sub in ("l0ffn", "l0attn", "l1ffn", "l1attn"):
        for nm in _SUB_WNAMES[sub]:
            if nm not in names:
                names.append(nm)
    x = np.ascontiguousarray(x_full, dtype=np.float32)
    in_maps = []
    for core in range(NCORES):
        m = {"x": np.ascontiguousarray(x[core])}
        for nm in names:
            m[nm] = w[nm]
        in_maps.append(m)
    res = run_bass_kernel_spmd(nc, in_maps, core_ids=list(range(NCORES)))
    return np.stack([res.results[i]["xo"] for i in range(NCORES)], axis=0)


def kernel(**inputs):
    w = _prep_weights(inputs)
    x = np.asarray(inputs["L0_attn_x"], dtype=np.float32)
    if os.environ.get("KUNFUSED", ""):
        return _run_chain(x, w, _CHAIN)
    return _run_fused(x, w)
